# version 09
# baseline (speedup 1.0000x reference)
import os
os.environ.setdefault("TRN_ALLOW_ICI_ALL_TO_ALL", "1")
import contextlib
import numpy as np
import ml_dtypes
import concourse.bass as bass
import concourse.mybir as mybir
from concourse.bass_utils import run_bass_kernel_spmd
from concourse.aot_env import allow_ici_all_to_all

F32 = mybir.dt.float32
BF16 = mybir.dt.bfloat16
AF = mybir.ActivationFunctionType
ALU = mybir.AluOpType
AX = mybir.AxisListType

NCORES = 8
T = 4096
D = 1024
CT = D // 128
TS = T // NCORES
EPS = 1e-5

BF = ml_dtypes.bfloat16



_UNIQ = [0]


def _uname(s):
    _UNIQ[0] += 1
    return f"{s}_{_UNIQ[0]}"


class EM:

    ENG = ("tensor", "vector", "scalar", "gpsimd", "sync")

    def __init__(self, nc):
        self.nc = nc
        self.sem = {}
        self.cnt = {}
        self.ctx = []
        for e in self.ENG:
            h = nc.alloc_semaphore(f"prog_{e}")
            self.sem[e] = h
            self.cnt[e] = 0
        self.dsem = {}
        self.dcnt = {}
        for q in ("sync", "gpsimd", "scalar", "vector"):
            self.dsem[q] = nc.alloc_semaphore(f"dma_{q}")
            self.dcnt[q] = 0

    def eng(self, name):
        return getattr(self.nc, name)

    def mark(self, engine, inst):
        inst.then_inc(self.sem[engine], 1)
        self.cnt[engine] += 1
        return self.cnt[engine]

    def wait(self, consumer, producer, cnt=None):
        c = self.cnt[producer] if cnt is None else cnt
        if c > 0:
            self.eng(consumer).wait_ge(self.sem[producer], c)

    def dma(self, q, dst, src, **kw):
        inst = self.eng(q).dma_start(dst, src, **kw)
        inst.then_inc(self.dsem[q], 16)
        self.dcnt[q] += 1
        return self.dcnt[q]

    def dwait(self, consumer, q, cnt=None):
        c = self.dcnt[q] if cnt is None else cnt
        if c > 0:
            self.eng(consumer).wait_ge(self.dsem[q], 16 * c)

    def barrier(self):
        self.nc.all_engine_barrier()

    def syncall(self, engines=("tensor", "vector", "scalar", "gpsimd", "sync")):
        for c in engines:
            for p in self.ENG:
                if p != c:
                    self.wait(c, p)
            for q in self.dsem:
                self.dwait(c, q)


def emit_const_pool(nc, em, cpool_sb, values, alloc=None):
    if cpool_sb is None:
        cpool_sb = nc.alloc_sbuf_tensor("g_cpool", [128, max(1, len(values))], F32)
    arr = np.zeros((128, max(1, len(values))), np.float32)
    for i, v in enumerate(values):
        arr[:, i] = v
    d = nc.inline_tensor(arr, _uname("const_pool"))
    em.dma("sync", cpool_sb[:, :arr.shape[1]], d.ap())
    for i, v in enumerate(values):
        nc.const_aps.register(F32, float(v), cpool_sb[:, i:i + 1])


class Bank:
    def __init__(self, ap):
        self.ap = ap
        self.tok = None

    def acquire(self, em, user="tensor"):
        if self.tok is not None:
            em.wait(user, self.tok[0], self.tok[1])
            self.tok = None
        return self.ap

    def release(self, engine, cnt):
        self.tok = (engine, cnt)



def emit_ln_stats_strip(em, x_tile_fn, ncols, ones_bf, ones_f32, sq_scr,
                        pa, pb, rstd, mrstd, tmp, x_dtype_bf=False,
                        extra_producers=()):
    nc = em.nc
    em.wait("tensor", "vector")
    em.wait("tensor", "scalar")
    em.wait("scalar", "tensor")
    em.wait("scalar", "vector")
    for p in extra_producers:
        em.wait("tensor", p)
        em.wait("scalar", p)
    s_base = em.cnt["scalar"]
    ones = ones_bf if x_dtype_bf else ones_f32
    for ct in range(CT):
        nc.tensor.matmul(pa[:, :ncols], ones, x_tile_fn(ct),
                         start=(ct == 0), stop=(ct == CT - 1))
    for ct in range(CT):
        em.mark("scalar", nc.scalar.activation(
            sq_scr[:, ct * ncols:(ct + 1) * ncols], x_tile_fn(ct), AF.Square))
    for ct in range(CT):
        em.wait("tensor", "scalar", s_base + ct + 1)
        mm = nc.tensor.matmul(pb[:, :ncols], ones_bf,
                              sq_scr[:, ct * ncols:(ct + 1) * ncols],
                              start=(ct == 0), stop=(ct == CT - 1))
    em.mark("tensor", mm)
    mm_c = em.cnt["tensor"]
    em.wait("vector", "tensor", mm_c)
    nc.vector.tensor_scalar(mrstd[:, :ncols], pa[:, :ncols], 1.0 / D, None, ALU.mult)
    em.mark("vector", nc.vector.tensor_scalar(rstd[:, :ncols], pb[:, :ncols], 1.0 / D, None, ALU.mult))
    em.wait("vector", "vector")
    nc.vector.scalar_tensor_tensor(
        tmp[:, :ncols], mrstd[:, :ncols], 1.0, mrstd[:, :ncols], ALU.mult, ALU.mult)
    v1 = em.mark("vector", nc.vector.tensor_sub(
        rstd[:, :ncols], tmp[:, :ncols], rstd[:, :ncols]))
    em.wait("scalar", "vector", v1)
    em.mark("scalar", nc.scalar.activation(rstd[:, :ncols], rstd[:, :ncols],
                                           AF.Ln, bias=EPS, scale=-1.0))
    em.wait("scalar", "scalar")
    s1 = em.mark("scalar", nc.scalar.activation(
        rstd[:, :ncols], rstd[:, :ncols], AF.Exp, bias=0.0, scale=-0.5))
    em.wait("vector", "scalar", s1)
    v2 = em.mark("vector", nc.vector.tensor_mul(mrstd[:, :ncols], mrstd[:, :ncols],
                                                rstd[:, :ncols]))
    return v2


def emit_ln_apply_tile(em, out_ap, x_ap, rstd_ap, mrstd_ap, tmp_ap):
    nc = em.nc
    nc.vector.tensor_mul(tmp_ap, x_ap, rstd_ap)
    return em.mark("vector", nc.vector.tensor_sub(out_ap, tmp_ap, mrstd_ap))



def emit_gemm_acc(em, psum_ap, lhsT_list, rhs_list, start=True, stop=True):
    nc = em.nc
    n = len(lhsT_list)
    mm = None
    for k in range(n):
        mm = nc.tensor.matmul(psum_ap, lhsT_list[k], rhs_list[k],
                              start=(start and k == 0), stop=(stop and k == n - 1))
    return em.mark("tensor", mm)


def load_weight_cast(em, sb, dram_ap):
    return em.dma("gpsimd", sb, dram_ap)



def build_b1():
    nc = bass.Bass("TRN2", target_bir_lowering=False, debug=False)
    x_in = nc.dram_tensor("x", [D, T], F32, kind="ExternalInput").ap()
    xn_out = nc.dram_tensor("xn", [D, T], BF16, kind="ExternalOutput").ap()
    em = EM(nc)
    emit_const_pool(nc, em, None, [EPS], alloc=nc)
    emit_b1(nc, em, x_in, xn_out)
    return nc


def emit_b1(nc, em, x_in, xn_out, xn_fn=None):
    NS = 8
    ncols = T // NS
    with (
        nc.sbuf_tensor("b1_X", [128, CT * T], F32) as X,
        nc.sbuf_tensor("b1_xnb", [128, CT * ncols], BF16) as xnb,
        nc.sbuf_tensor("b1_sqb", [128, CT * ncols], BF16) as sqb,
        nc.sbuf_tensor("b1_rstd", [128, ncols], F32) as rstd,
        nc.sbuf_tensor("b1_mrstd", [128, ncols], F32) as mrstd,
        nc.sbuf_tensor("b1_tmp", [128, ncols], F32) as tmp,
        nc.sbuf_tensor("b1_ones_bf", [128, 128], BF16) as ones_bf,
        nc.sbuf_tensor("b1_ones_f", [128, 128], F32) as ones_f,
        nc.psum_tensor("b1_pa", [128, ncols], F32) as pa,
        nc.psum_tensor("b1_pb", [128, ncols], F32) as pb,
    ):
        ones_d = nc.inline_tensor(np.ones((128, 128), BF), _uname("ones_bf_c"))
        ones_fd = nc.inline_tensor(np.ones((128, 128), np.float32), _uname("ones_f_c"))
        em.dma("sync", ones_bf[:], ones_d.ap())
        em.dma("sync", ones_f[:], ones_fd.ap())
        for ct in range(CT):
            em.dma("sync", X[:, ct * T:(ct + 1) * T], x_in[ct * 128:(ct + 1) * 128, :])
        em.dwait("tensor", "sync")
        em.dwait("scalar", "sync")
        em.dwait("vector", "sync")
        for s in range(NS):
            col = slice(s * ncols, (s + 1) * ncols)
            vr = emit_ln_stats_strip(
                em, lambda ct: X[:, ct * T + s * ncols: ct * T + (s + 1) * ncols],
                ncols, ones_bf[:], ones_f[:], sqb, pa, pb, rstd, mrstd, tmp)
            em.dwait("vector", "scalar")
            for ct in range(CT):
                xt = X[:, ct * T + s * ncols: ct * T + (s + 1) * ncols]
                emit_ln_apply_tile(em, xnb[:, ct * ncols:(ct + 1) * ncols], xt,
                                   rstd[:, :ncols], mrstd[:, :ncols], tmp[:])
            vv = em.cnt["vector"]
            em.wait("scalar", "vector", vv)
            for ct in range(CT):
                em.dma("scalar",
                       xn_fn(ct, s) if xn_fn else xn_out[ct * 128:(ct + 1) * 128, col],
                       xnb[:, ct * ncols:(ct + 1) * ncols])
        em.dwait("scalar", "scalar")
        em.barrier()
    return


def build_b2():
    nc = bass.Bass("TRN2", target_bir_lowering=False, debug=False)
    xg = nc.dram_tensor("xg", [NCORES, D, TS], BF16, kind="ExternalInput").ap()
    w_names = ["wfT", "ufT", "wiT", "uiT", "wcT", "ucT", "uoT"]
    wd = {n: nc.dram_tensor(n, [D, D], F32, kind="ExternalInput").ap() for n in w_names}
    bcols = nc.dram_tensor("bcols", [128, 4 * CT], F32, kind="ExternalInput").ap()
    ut_out = nc.dram_tensor("UT", [D, TS], F32, kind="ExternalOutput").ap()
    em = EM(nc)
    emit_b2(nc, em, xg, wd, bcols, ut_out)
    return nc


def emit_b2(nc, em, xg, wd, bcols, ut_out):
    w_names = ["wfT", "ufT", "wiT", "uiT", "wcT", "ucT", "uoT"]
    with (
        nc.sbuf_tensor("b2_W", [128, 7 * CT * D // 128 * 128], BF16) as Wbuf,
        nc.sbuf_tensor("b2_xgs", [128, CT * TS], BF16) as xgs,
        nc.sbuf_tensor("b2_c", [128, CT * TS], F32) as c_sb,
        nc.sbuf_tensor("b2_cbf", [128, CT * TS], BF16) as cbf,
        nc.sbuf_tensor("b2_bc", [128, 4 * CT], F32) as bc_sb,
        nc.sbuf_tensor("b2_f", [128, TS], F32) as f_sb,
        nc.sbuf_tensor("b2_i", [128, TS], F32) as i_sb,
        nc.sbuf_tensor("b2_g", [128, TS], F32) as g_sb,
        nc.sbuf_tensor("b2_u", [128, TS], F32) as u_sb,
        nc.psum_tensor("b2_pf", [128, TS], F32) as pf,
        nc.psum_tensor("b2_pi", [128, TS], F32) as pi,
        nc.psum_tensor("b2_pg", [128, TS], F32) as pg,
    ):
        def wtile(w, ki, mi):
            idx = (w * CT + ki) * CT + mi
            return Wbuf[:, idx * 128:(idx + 1) * 128]
        for w, nme in enumerate(w_names):
            for ki in range(CT):
                em.dma("gpsimd", Wbuf[:, (w * CT + ki) * CT * 128:(w * CT + ki + 1) * CT * 128],
                       wd[nme][ki * 128:(ki + 1) * 128, :])
        em.dma("sync", bc_sb[:], bcols)
        em.dwait("vector", "sync")
        ini = None
        for ct in range(CT):
            ini = nc.vector.tensor_scalar(c_sb[:, ct * TS:(ct + 1) * TS],
                                          bc_sb[:, 3 * CT + ct:3 * CT + ct + 1].broadcast_to([128, TS]),
                                          1.0, None, ALU.mult)
        em.mark("vector", ini)
        em.dwait("tensor", "gpsimd")
        em.dwait("scalar", "sync")
        for step in range(NCORES):
            em.wait("vector", "tensor")
            cp = None
            for ct in range(CT):
                cp = nc.vector.tensor_copy(cbf[:, ct * TS:(ct + 1) * TS], c_sb[:, ct * TS:(ct + 1) * TS])
            vcast = em.mark("vector", cp)
            em.wait("sync", "tensor")
            for ct in range(CT):
                em.dma("sync", xgs[:, ct * TS:(ct + 1) * TS],
                       xg[step, ct * 128:(ct + 1) * 128, :])
            em.dwait("tensor", "sync")
            em.wait("tensor", "vector", vcast)
            for ct in range(CT):
                for (wi_, ui_, bank) in ((0, 1, pf), (2, 3, pi), (4, 5, pg)):
                    for k in range(CT):
                        nc.tensor.matmul(bank[:], wtile(wi_, k, ct),
                                         xgs[:, k * TS:(k + 1) * TS], start=(k == 0), stop=False)
                    for k in range(CT):
                        mm = nc.tensor.matmul(bank[:], wtile(ui_, k, ct),
                                              cbf[:, k * TS:(k + 1) * TS], start=False, stop=(k == CT - 1))
                mmc = em.mark("tensor", mm)
                em.wait("scalar", "tensor", mmc)
                nc.scalar.activation(f_sb[:], pf[:], AF.Sigmoid, bias=bc_sb[:, ct:ct + 1])
                nc.scalar.activation(i_sb[:], pi[:], AF.Sigmoid, bias=bc_sb[:, CT + ct:CT + ct + 1])
                sc = em.mark("scalar", nc.scalar.activation(
                    g_sb[:], pg[:], AF.Tanh, bias=bc_sb[:, 2 * CT + ct:2 * CT + ct + 1]))
                em.wait("vector", "scalar", sc)
                cs = c_sb[:, ct * TS:(ct + 1) * TS]
                nc.vector.tensor_mul(u_sb[:], f_sb[:], cs)
                nc.vector.tensor_mul(g_sb[:], i_sb[:], g_sb[:])
                vc = em.mark("vector", nc.vector.tensor_add(cs, u_sb[:], g_sb[:]))
                em.wait("tensor", "scalar", sc)
        em.wait("vector", "tensor")
        cp = None
        for ct in range(CT):
            cp = nc.vector.tensor_copy(cbf[:, ct * TS:(ct + 1) * TS], c_sb[:, ct * TS:(ct + 1) * TS])
        vcast = em.mark("vector", cp)
        em.wait("tensor", "vector", vcast)
        for ct in range(CT):
            for k in range(CT):
                mm = nc.tensor.matmul(pf[:], wtile(6, k, ct), cbf[:, k * TS:(k + 1) * TS],
                                      start=(k == 0), stop=(k == CT - 1))
            mmc = em.mark("tensor", mm)
            em.wait("vector", "tensor", mmc)
            vc = em.mark("vector", nc.vector.tensor_copy(u_sb[:], pf[:]))
            em.wait("scalar", "vector", vc)
            em.dma("scalar", ut_out[ct * 128:(ct + 1) * 128, :], u_sb[:])
            em.dwait("tensor", "scalar")
            em.dwait("vector", "scalar")
        em.dwait("scalar", "scalar")
        em.barrier()
    return


def build_b3():
    nc = bass.Bass("TRN2", target_bir_lowering=False, debug=False)
    x_in = nc.dram_tensor("x", [D, T], F32, kind="ExternalInput").ap()
    xn_in = nc.dram_tensor("xn", [D, T], BF16, kind="ExternalInput").ap()
    ut_in = nc.dram_tensor("UT", [D, T], F32, kind="ExternalInput").ap()
    woT = nc.dram_tensor("woT", [D, D], F32, kind="ExternalInput").ap()
    bo_col = nc.dram_tensor("bo_col", [128, CT], F32, kind="ExternalInput").ap()
    x_out = nc.dram_tensor("xo", [D, T], F32, kind="ExternalOutput").ap()
    em = EM(nc)
    emit_const_pool(nc, em, None, [EPS], alloc=nc)
    emit_b3(nc, em, x_in, xn_in, ut_in, woT, bo_col, x_out)
    return nc


def emit_b3(nc, em, x_in, xn_in, ut_in, woT, bo_col, x_out,
            xn_fn=None, ut_fn=None):
    NS = 8
    ncols = T // NS
    with (
        nc.sbuf_tensor("b3_W", [128, CT * CT * 128], BF16) as Wbuf,
        nc.sbuf_tensor("b3_xnb", [128, 2 * CT * ncols], BF16) as xnb,
        nc.sbuf_tensor("b3_ut", [128, ncols], F32) as ut_sb,
        nc.sbuf_tensor("b3_h", [128, CT * ncols], BF16) as h_sb,
        nc.sbuf_tensor("b3_bo", [128, CT], F32) as bo_sb,
        nc.sbuf_tensor("b3_u", [128, ncols], F32) as u_sb,
        nc.sbuf_tensor("b3_sqb", [128, CT * ncols], BF16) as sqb,
        nc.sbuf_tensor("b3_rstd", [128, ncols], F32) as rstd,
        nc.sbuf_tensor("b3_mrstd", [128, ncols], F32) as mrstd,
        nc.sbuf_tensor("b3_tmp", [128, ncols], F32) as tmp,
        nc.sbuf_tensor("b3_xt", [128, ncols], F32) as xt_sb,
        nc.sbuf_tensor("b3_ones_bf", [128, 128], BF16) as ones_bf,
        nc.psum_tensor("b3_ph", [128, ncols], F32) as ph,
        nc.psum_tensor("b3_pa", [128, ncols], F32) as pa,
        nc.psum_tensor("b3_pb", [128, ncols], F32) as pb,
    ):
        ones_d = nc.inline_tensor(np.ones((128, 128), BF), _uname("ones_bf_c"))
        em.dma("sync", ones_bf[:], ones_d.ap())
        em.dma("sync", bo_sb[:], bo_col)
        for ki in range(CT):
            em.dma("gpsimd", Wbuf[:, ki * CT * 128:(ki + 1) * CT * 128],
                   woT[ki * 128:(ki + 1) * 128, :])
        em.dwait("tensor", "gpsimd")
        em.dwait("scalar", "sync")
        em.dwait("vector", "sync")

        def wtile(ki, mi):
            return Wbuf[:, (ki * CT + mi) * 128:(ki * CT + mi + 1) * 128]

        for s in range(NS):
            col = slice(s * ncols, (s + 1) * ncols)
            xb = xnb[:, (s % 2) * CT * ncols:((s % 2) + 1) * CT * ncols]
            em.wait("sync", "tensor")
            for ct in range(CT):
                em.dma("sync", xb[:, ct * ncols:(ct + 1) * ncols],
                       xn_fn(ct, s) if xn_fn else xn_in[ct * 128:(ct + 1) * 128, col])
            em.dwait("tensor", "sync")
            for ct in range(CT):
                em.dma("sync", ut_sb[:],
                       ut_fn(ct, s) if ut_fn else ut_in[ct * 128:(ct + 1) * 128, col])
                for k in range(CT):
                    mm = nc.tensor.matmul(ph[:], wtile(k, ct), xb[:, k * ncols:(k + 1) * ncols],
                                          start=(k == 0), stop=(k == CT - 1))
                mmc = em.mark("tensor", mm)
                em.wait("vector", "tensor", mmc)
                em.dwait("vector", "sync")
                nc.vector.scalar_tensor_tensor(u_sb[:], ut_sb[:], 1.0, ph[:], ALU.mult, ALU.add)
                vc = em.mark("vector", nc.vector.tensor_scalar(
                    u_sb[:], u_sb[:], bo_sb[:, ct:ct + 1], None, ALU.add))
                em.wait("scalar", "vector", vc)
                sc = em.mark("scalar", nc.scalar.activation(
                    h_sb[:, ct * ncols:(ct + 1) * ncols], u_sb[:], AF.Tanh))
                em.wait("tensor", "scalar", sc)
                em.wait("sync", "vector", vc)
            vr = emit_ln_stats_strip(
                em, lambda ct: h_sb[:, ct * ncols:(ct + 1) * ncols],
                ncols, ones_bf[:], None, sqb, pa, pb, rstd, mrstd, tmp,
                x_dtype_bf=True, extra_producers=())
            for ct in range(CT):
                em.dma("sync", xt_sb[:], x_in[ct * 128:(ct + 1) * 128, col])
                nc.vector.tensor_mul(tmp[:], h_sb[:, ct * ncols:(ct + 1) * ncols], rstd[:, :ncols])
                em.dwait("vector", "sync")
                nc.vector.tensor_sub(tmp[:], tmp[:], mrstd[:, :ncols])
                vc = em.mark("vector", nc.vector.tensor_add(xt_sb[:], xt_sb[:], tmp[:]))
                em.wait("scalar", "vector", vc)
                em.dma("scalar", x_out[ct * 128:(ct + 1) * 128, col], xt_sb[:])
                em.dwait("vector", "scalar")
                em.dwait("sync", "scalar")
        em.dwait("scalar", "scalar")
        em.barrier()
    return



def _l0_mask():
    m = np.zeros((128, 1024), np.float32)
    t = np.arange(128)
    for jb in range(8):
        for ji in range(8):
            j = jb * 8 + ji
            bad = j < 63 - t
            for h in range(16):
                m[bad, jb * 128 + ji * 16 + h] = -1e30
    return m.astype(BF)


def build_a(dbg=False):
    nc = bass.Bass("TRN2", target_bir_lowering=False, debug=False)
    x_in = nc.dram_tensor("x", [D, T], F32, kind="ExternalInput").ap()
    names = ("W_disp", "W_val", "W_cproj", "relT", "W_strain", "W_pos", "W_bond",
             "W_dmg", "W_dmg_out", "bdmg_rep", "bdo_rep")
    shapes = ([D, 256], [D, D], [D, D], [16, 64], [16, 16], [16, 16], [16, 1],
              [16, 16], [16, 1], [128, 1], [128, 1])
    IO = {n: nc.dram_tensor(n, s, F32, kind="ExternalInput").ap()
          for n, s in zip(names, shapes)}
    x_out = nc.dram_tensor("xo", [D, T], F32, kind="ExternalOutput").ap()
    em = EM(nc)
    emit_const_pool(nc, em, None, [EPS], alloc=nc)
    emit_a(nc, em, x_in, x_out, IO, dbg=dbg)
    return nc


def emit_a(nc, em, x_in, x_out, IO, dbg=False):
    dd = {}
    W_disp = IO["W_disp"]; W_val = IO["W_val"]; W_cproj = IO["W_cproj"]
    relT = IO["relT"]; W_strain = IO["W_strain"]; W_pos = IO["W_pos"]
    W_bond = IO["W_bond"]; W_dmg = IO["W_dmg"]; W_dmg_out = IO["W_dmg_out"]
    bdmg_rep = IO["bdmg_rep"]; bdo_rep = IO["bdo_rep"]

    if dbg:
        for nme, shp, dt_ in (("d_disp", [128, 2 * T], BF16), ("d_A", [128, 2 * 4160], BF16),
                              ("d_wtm", [128, 4096], BF16), ("d_whd", [128, 4096], BF16),
                              ("d_Pp", [128, 1024], BF16), ("d_atT", [128, CT * 512], BF16),
                              ("d_prel", [128, 64], F32)):
            dd[nme] = nc.dram_tensor(nme, shp, dt_, kind="ExternalOutput").ap()
    vdr = nc.dram_tensor("a_vdr", [D, T], BF16, kind="ExternalOutput" if dbg else "Internal").ap()
    npan = 8 if dbg else 1
    pan_full = nc.dram_tensor("a_pan", [npan * 16 * 4 * 128 * 256], BF16,
                              kind="ExternalOutput" if dbg else "Internal").ap()
    if dbg:
        whd_all = nc.dram_tensor("whd_all", [8, 128, 4096], BF16,
                                 kind="ExternalOutput").ap()
        atT_all = nc.dram_tensor("atT_all", [8, 128, 4096], BF16,
                                 kind="ExternalOutput").ap()
        bos_all = nc.dram_tensor("bos_all", [8, 128, 4096], BF16,
                                 kind="ExternalOutput").ap()

    CH = 512
    es = contextlib.ExitStack()
    with es:
        ones_bf = es.enter_context(nc.sbuf_tensor("a_ones_bf", [128, 128], BF16))
        ones_f = es.enter_context(nc.sbuf_tensor("a_ones_f", [128, 128], F32))
        ident = es.enter_context(nc.sbuf_tensor("a_ident", [128, 128], BF16))
        xn1 = es.enter_context(nc.sbuf_tensor("a_xn1", [128, CT * T], BF16))
        bdiag = es.enter_context(nc.sbuf_tensor("a_bdiag", [128, 2 * 128 + 2 * 16], BF16))
        prel = es.enter_context(nc.sbuf_tensor("a_prel", [128, 64], F32))
        bcols = es.enter_context(nc.sbuf_tensor("a_bcols", [128, 2], F32))
        m0 = es.enter_context(nc.sbuf_tensor("a_m0", [128, 1024], BF16))
        zz = es.enter_context(nc.sbuf_tensor("a_zz", [128, 2048], BF16))
        em.dma("sync", ones_bf[:], nc.inline_tensor(np.ones((128, 128), BF), _uname("ones_c")).ap())
        em.dma("sync", ones_f[:], nc.inline_tensor(np.ones((128, 128), np.float32), _uname("onesf_c")).ap())
        em.dma("sync", ident[:], nc.inline_tensor(np.eye(128, dtype=BF), _uname("ident_c")).ap())
        em.dma("sync", m0[:], nc.inline_tensor(_l0_mask(), _uname("l0mask")).ap())
        em.dma("sync", bcols[:, 0:1], bdmg_rep)
        em.dma("sync", bcols[:, 1:2], bdo_rep)
        em.dwait("vector", "sync")
        em.mark("vector", nc.vector.memset(bdiag[:], 0.0))
        em.mark("vector", nc.vector.memset(zz[:], 0.0))
        em.dwait("gpsimd", "sync")
        em.wait("gpsimd", "vector")
        for hh in range(8):
            em.dma("gpsimd", bdiag[hh * 16:(hh + 1) * 16, hh * 16:(hh + 1) * 16], W_strain)
            em.dma("gpsimd", bdiag[hh * 16:(hh + 1) * 16, 128 + hh * 16:128 + (hh + 1) * 16], W_dmg)
            em.dma("gpsimd", bdiag[hh * 16:(hh + 1) * 16, 256 + hh:256 + hh + 1], W_bond)
            em.dma("gpsimd", bdiag[hh * 16:(hh + 1) * 16, 272 + hh:272 + hh + 1], W_dmg_out)
        em.wait("scalar", "vector")
        for i in range(16 * npan):
            em.dma("scalar",
                   pan_full.rearrange("(a b) -> a b", b=2048)[i * 64:(i + 1) * 64, :],
                   zz[0:64, :])

        p1 = contextlib.ExitStack()
        with p1:
            xb = p1.enter_context(nc.sbuf_tensor("a_p1_xb", [128, CT * 512], F32))
            sqb = p1.enter_context(nc.sbuf_tensor("a_p1_sqb", [128, CT * 512], BF16))
            rstd = p1.enter_context(nc.sbuf_tensor("a_p1_rstd", [128, 512], F32))
            mrstd = p1.enter_context(nc.sbuf_tensor("a_p1_mrstd", [128, 512], F32))
            tmp = p1.enter_context(nc.sbuf_tensor("a_p1_tmp", [128, 512], F32))
            pa = p1.enter_context(nc.psum_tensor("a_p1_pa", [128, 512], F32))
            pb = p1.enter_context(nc.psum_tensor("a_p1_pb", [128, 512], F32))
            for s in range(8):
                col = slice(s * 512, (s + 1) * 512)
                em.wait("sync", "tensor")
                em.wait("sync", "vector")
                em.wait("sync", "scalar")
                for ct in range(CT):
                    em.dma("sync", xb[:, ct * 512:(ct + 1) * 512],
                           x_in[ct * 128:(ct + 1) * 128, col])
                em.dwait("tensor", "sync")
                em.dwait("scalar", "sync")
                em.dwait("vector", "sync")
                emit_ln_stats_strip(em, lambda ct: xb[:, ct * 512:(ct + 1) * 512],
                                    512, ones_bf[:], ones_f[:], sqb, pa, pb,
                                    rstd, mrstd, tmp)
                for ct in range(CT):
                    emit_ln_apply_tile(
                        em, xn1[:, ct * T + s * 512: ct * T + (s + 1) * 512],
                        xb[:, ct * 512:(ct + 1) * 512], rstd[:], mrstd[:], tmp[:])
            em.barrier()

        p2 = contextlib.ExitStack()
        with p2:
            wsl = p2.enter_context(nc.sbuf_tensor("a_p2_wsl", [128, 2 * CT * 128], BF16))
            A_buf = p2.enter_context(nc.sbuf_tensor("a_p2_A", [128, 2 * 4160], BF16))
            D_buf = p2.enter_context(nc.sbuf_tensor("a_p2_D", [128, 2 * 4160], BF16))
            pbuf = p2.enter_context(nc.sbuf_tensor("a_p2_pb", [128, 2 * 512], BF16))
            pp0 = p2.enter_context(nc.psum_tensor("a_p2_p0", [128, 512], F32))
            pp1 = p2.enter_context(nc.psum_tensor("a_p2_p1", [128, 512], F32))
            p2d = contextlib.ExitStack()
            dispT = p2d.enter_context(nc.sbuf_tensor("a_p2_disp", [128, 2 * T], BF16))
            em.wait("vector", "tensor")
            nc.vector.memset(A_buf[:, 0:64], 0.0)
            nc.vector.memset(A_buf[:, 4160:4160 + 64], 0.0)
            nc.vector.memset(D_buf[:, 0:64], 0.0)
            em.mark("vector", nc.vector.memset(D_buf[:, 4160:4160 + 64], 0.0))
            pbanks = [Bank(pp0[:]), Bank(pp1[:])]
            wslot = [wsl[:, 0:CT * 128], wsl[:, CT * 128:2 * CT * 128]]
            watok = [0, 0]
            wi = 0
            for s in range(8):
                col = slice(s * 512, (s + 1) * 512)
                for mi in range(2):
                    sl = wi % 2
                    slot = wslot[sl]
                    em.wait("gpsimd", "tensor", watok[sl])
                    dc = em.dma("gpsimd", slot.rearrange("p (k c) -> p k c", c=128),
                           W_disp.rearrange("(k p) m -> p k m", p=128)
                           [:, :, mi * 128:(mi + 1) * 128])
                    em.dwait("tensor", "gpsimd", dc)
                    bank = pbanks[wi % 2]
                    pap = bank.acquire(em)
                    for k in range(CT):
                        mm = nc.tensor.matmul(
                            pap, slot[:, k * 128:(k + 1) * 128],
                            xn1[:, k * T + s * 512: k * T + (s + 1) * 512],
                            start=(k == 0), stop=(k == CT - 1))
                    mmc = em.mark("tensor", mm)
                    watok[sl] = mmc
                    em.wait("vector", "tensor", mmc)
                    vcp = em.mark("vector", nc.vector.tensor_copy(
                        dispT[:, mi * T + s * 512: mi * T + (s + 1) * 512], pap))
                    bank.release("vector", vcp)
                    wi += 1
            for s in range(8):
                col = slice(s * 512, (s + 1) * 512)
                for mi in range(CT):
                    sl = wi % 2
                    slot = wslot[sl]
                    em.wait("gpsimd", "tensor", watok[sl])
                    dc = em.dma("gpsimd", slot.rearrange("p (k c) -> p k c", c=128),
                           W_val.rearrange("(k p) m -> p k m", p=128)
                           [:, :, mi * 128:(mi + 1) * 128])
                    em.dwait("tensor", "gpsimd", dc)
                    bank = pbanks[wi % 2]
                    pap = bank.acquire(em)
                    for k in range(CT):
                        mm = nc.tensor.matmul(
                            pap, slot[:, k * 128:(k + 1) * 128],
                            xn1[:, k * T + s * 512: k * T + (s + 1) * 512],
                            start=(k == 0), stop=(k == CT - 1))
                    mmc = em.mark("tensor", mm)
                    watok[sl] = mmc
                    em.wait("vector", "tensor", mmc)
                    em.dwait("vector", "scalar", max(0, em.dcnt["scalar"] - 1))
                    pbv = pbuf[:, (wi % 2) * 512:(wi % 2 + 1) * 512]
                    vcp = em.mark("vector", nc.vector.tensor_copy(pbv, pap))
                    bank.release("vector", vcp)
                    em.wait("scalar", "vector", vcp)
                    em.dma("scalar", vdr[mi * 128:(mi + 1) * 128, col], pbv)
                    wi += 1
            em.dwait("tensor", "gpsimd")
            for hcT in range(2):
                for s in range(8):
                    for (wofs, dst) in ((0, A_buf), (128, D_buf)):
                        bank = pbanks[wi % 2]
                        pap = bank.acquire(em)
                        mm = nc.tensor.matmul(
                            pap, bdiag[:, wofs:wofs + 128],
                            dispT[:, hcT * T + s * 512: hcT * T + (s + 1) * 512],
                            start=True, stop=True)
                        mmc = em.mark("tensor", mm)
                        em.wait("vector", "tensor", mmc)
                        vcp = em.mark("vector", nc.vector.tensor_copy(
                            dst[:, hcT * 4160 + 64 + s * 512: hcT * 4160 + 64 + (s + 1) * 512],
                            pap))
                        bank.release("vector", vcp)
                        wi += 1
            em.wait("vector", "vector")
            p2d.close()
            wps = wsl[:, 0:16]
            rel_s = wsl[:, 16:16 + 64]
            em.wait("gpsimd", "tensor")
            em.dma("gpsimd", wps[0:16, :], W_pos)
            em.dma("gpsimd", rel_s[0:16, 0:64], relT)
            em.dwait("tensor", "gpsimd")
            mm = nc.tensor.matmul(pp0[0:16, 0:64], wps[0:16, :], rel_s[0:16, 0:64],
                                  start=True, stop=True)
            mmc = em.mark("tensor", mm)
            em.wait("vector", "tensor", mmc)
            vcp = em.mark("vector", nc.vector.tensor_copy(prel[0:16, :], pp0[0:16, 0:64]))
            em.wait("sync", "vector", vcp)
            for hh in range(1, 8):
                em.dma("sync", prel[hh * 16:(hh + 1) * 16, :], prel[0:16, :])
            em.dwait("scalar", "scalar")
            em.dwait("scalar", "sync")
            em.barrier()
            if dbg:
                em.dma("sync", dd["d_disp"], dispT[:])
                em.dma("sync", dd["d_A"], A_buf[:])
                em.dma("sync", dd["d_prel"], prel[:])
                em.dwait("sync", "sync")
                em.barrier()

            p3 = contextlib.ExitStack()
            with p3:
                st0 = p3.enter_context(nc.sbuf_tensor("a_p3_st0", [128, 512], BF16))
                st1 = p3.enter_context(nc.sbuf_tensor("a_p3_st1", [128, 512], BF16))
                st2 = p3.enter_context(nc.sbuf_tensor("a_p3_st2", [128, 512], BF16))
                st3 = p3.enter_context(nc.sbuf_tensor("a_p3_st3", [128, 512], BF16))
                ba0 = p3.enter_context(nc.sbuf_tensor("a_p3_ba0", [128, 512], BF16))
                ba1 = p3.enter_context(nc.sbuf_tensor("a_p3_ba1", [128, 512], BF16))
                ba2 = p3.enter_context(nc.sbuf_tensor("a_p3_ba2", [128, 512], BF16))
                ba3 = p3.enter_context(nc.sbuf_tensor("a_p3_ba3", [128, 512], BF16))
                Lb = p3.enter_context(nc.sbuf_tensor("a_p3_LB", [16, 8 * 512], BF16))
                Ld = p3.enter_context(nc.sbuf_tensor("a_p3_LD", [16, 8 * 512], BF16))
                sg = p3.enter_context(nc.sbuf_tensor("a_p3_sg", [128, 512], F32))
                Lsb = p3.enter_context(nc.sbuf_tensor("a_p3_Lsb", [128, 512], BF16))
                w_tm = p3.enter_context(nc.sbuf_tensor("a_p3_wtm", [128, 4096], BF16))
                P_w = p3.enter_context(nc.sbuf_tensor("a_p3_Pw", [128, 4096], BF16))
                w_hd = p3.enter_context(nc.sbuf_tensor("a_p3_whd", [128, 4096], BF16))
                mxs = p3.enter_context(nc.sbuf_tensor("a_p3_mxs", [128, 8], F32))
                vTh = p3.enter_context(nc.sbuf_tensor("a_p3_vTh", [64, 640], BF16))
                vtk = p3.enter_context(nc.sbuf_tensor("a_p3_vtk", [128, 5 * 64], BF16))
                Pp = p3.enter_context(nc.sbuf_tensor("a_p3_Pp", [128, 4 * 256], BF16))
                PTs = p3.enter_context(nc.sbuf_tensor("a_p3_PTs", [128, 256], BF16))
                atT = p3.enter_context(nc.sbuf_tensor("a_p3_atT", [128, CT * 512], BF16))
                bo_s = p3.enter_context(nc.sbuf_tensor("a_p3_bo", [128, CT * 512], BF16))
                sqb = p3.enter_context(nc.sbuf_tensor("a_p3_sqb", [128, CT * 512], BF16))
                rstd = p3.enter_context(nc.sbuf_tensor("a_p3_rstd", [128, 512], F32))
                mrstd = p3.enter_context(nc.sbuf_tensor("a_p3_mrstd", [128, 512], F32))
                tmp = p3.enter_context(nc.sbuf_tensor("a_p3_tmp", [128, 512], F32))
                xt = p3.enter_context(nc.sbuf_tensor("a_p3_xt", [128, 512], F32))
                wctok = [0, 0]
                p16a = p3.enter_context(nc.psum_tensor("a_p3_ba", [16, 512], F32))
                p16b = p3.enter_context(nc.psum_tensor("a_p3_bb", [16, 512], F32))
                pstk = p3.enter_context(nc.psum_tensor("a_p3_stk", [128, 512], BF16))
                ptr2 = p3.enter_context(nc.psum_tensor("a_p3_tr", [128, 512], BF16))
                ppv = p3.enter_context(nc.psum_tensor("a_p3_pv", [64, 128], F32))
                for ch in range(8):
                    t0 = ch * CH
                    for jb8 in range(8):
                        for jj in range(8):
                            j = jb8 * 8 + jj
                            em.wait("vector", "scalar")
                            nc.vector.tensor_sub(
                                st0[:], A_buf[:, 64 + t0 + j - 63:64 + t0 + j - 63 + 512],
                                A_buf[:, 64 + t0:64 + t0 + 512])
                            nc.vector.tensor_sub(
                                st1[:], A_buf[:, 4160 + 64 + t0 + j - 63:4160 + 64 + t0 + j - 63 + 512],
                                A_buf[:, 4160 + 64 + t0:4160 + 64 + t0 + 512])
                            nc.vector.tensor_sub(
                                st2[:], D_buf[:, 64 + t0 + j - 63:64 + t0 + j - 63 + 512],
                                D_buf[:, 64 + t0:64 + t0 + 512])
                            em.mark("vector", nc.vector.tensor_sub(
                                st3[:], D_buf[:, 4160 + 64 + t0 + j - 63:4160 + 64 + t0 + j - 63 + 512],
                                D_buf[:, 4160 + 64 + t0:4160 + 64 + t0 + 512]))
                            em.wait("scalar", "vector")
                            em.wait("scalar", "tensor")
                            nc.scalar.activation(ba0[:], st0[:], AF.Gelu, bias=prel[:, j:j + 1])
                            nc.scalar.activation(ba1[:], st1[:], AF.Gelu, bias=prel[:, j:j + 1])
                            nc.scalar.activation(ba2[:], st2[:], AF.Gelu, bias=bcols[:, 0:1])
                            sc = em.mark("scalar", nc.scalar.activation(
                                ba3[:], st3[:], AF.Gelu, bias=bcols[:, 0:1]))
                            em.wait("tensor", "scalar", sc)
                            em.wait("tensor", "vector")
                            nc.tensor.matmul(p16a[:], bdiag[:, 256:272], ba0[:],
                                             start=True, stop=False)
                            nc.tensor.matmul(p16a[:], bdiag[:, 256:272], ba1[:],
                                             start=False, stop=True)
                            nc.tensor.matmul(p16b[:], bdiag[:, 272:288], ba2[:],
                                             start=True, stop=False)
                            md = nc.tensor.matmul(p16b[:], bdiag[:, 272:288], ba3[:],
                                                  start=False, stop=True)
                            mmc = em.mark("tensor", md)
                            em.wait("vector", "tensor", mmc)
                            nc.vector.tensor_copy(Lb[:, jj * 512:(jj + 1) * 512], p16a[0:16, :])
                            em.mark("vector", nc.vector.tensor_copy(
                                Ld[:, jj * 512:(jj + 1) * 512], p16b[0:16, :]))
                        for jj in range(8):
                            j = jb8 * 8 + jj
                            em.wait("scalar", "vector")
                            sc = em.mark("scalar", nc.scalar.activation(
                                sg[0:16, :], Ld[:, jj * 512:(jj + 1) * 512], AF.Sigmoid,
                                bias=bcols[0:16, 1:2]))
                            em.wait("vector", "scalar", sc)
                            em.wait("vector", "tensor")
                            vcp = em.mark("vector", nc.vector.scalar_tensor_tensor(
                                Lsb[0:16, :], sg[0:16, :], -10.0, Lb[:, jj * 512:(jj + 1) * 512],
                                ALU.mult, ALU.add))
                            em.wait("tensor", "vector", vcp)
                            for tsub in range(4):
                                tr = nc.tensor.transpose(
                                    pstk[:, tsub * 128 + jj * 16: tsub * 128 + jj * 16 + 16],
                                    Lsb[0:16, tsub * 128:(tsub + 1) * 128], ident[0:16, 0:16])
                            tc = em.mark("tensor", tr)
                        em.wait("vector", "tensor")
                        for tsub in range(4):
                            vcp = nc.vector.tensor_copy(
                                w_tm[:, tsub * 1024 + jb8 * 128: tsub * 1024 + (jb8 + 1) * 128],
                                pstk[:, tsub * 128:(tsub + 1) * 128])
                        em.mark("vector", vcp)
                        em.wait("tensor", "vector")
                    if ch == 0:
                        em.mark("vector", nc.vector.tensor_add(
                            w_tm[:, 0:1024], w_tm[:, 0:1024], m0[:]))
                    if dbg and ch == 0:
                        em.barrier()
                        em.dma("sync", dd["d_wtm"], w_tm[:])
                        em.dwait("sync", "sync")
                        em.barrier()
                    em.wait("vector", "vector")
                    for tsub in range(4):
                        for h in range(16):
                            src = w_tm[:].rearrange("p (ts j h) -> p ts j h",
                                                    ts=4, j=64, h=16)[:, tsub, :, h]
                            em.wait("vector", "scalar")
                            em.mark("vector", nc.vector.reduce_max(
                                mxs[:, 0:1], src, axis=AX.X, negate=True))
                            em.wait("vector", "vector")
                            em.wait("scalar", "vector")
                            pdst = P_w[:].rearrange("p (ts j h) -> p ts j h",
                                                    ts=4, j=64, h=16)[:, tsub, :, h]
                            sc = em.mark("scalar", nc.scalar.activation(
                                pdst, src, AF.Exp, bias=mxs[:, 0:1]))
                            em.wait("vector", "scalar", sc)
                            em.mark("vector", nc.vector.reduce_sum(
                                mxs[:, 1:2], pdst, axis=AX.X))
                            em.wait("vector", "vector")
                            em.mark("vector", nc.vector.reciprocal(mxs[:, 2:3], mxs[:, 1:2]))
                            em.wait("vector", "vector")
                            em.dwait("vector", "gpsimd")
                            em.mark("vector", nc.vector.tensor_scalar(
                                w_hd[:].rearrange("p (a b) -> p a b", b=64)[:, h * 4 + tsub],
                                pdst, mxs[:, 2:3], None, ALU.mult))
                    if dbg and ch == 0:
                        em.barrier()
                        em.dma("sync", dd["d_whd"], w_hd[:])
                        em.dwait("sync", "sync")
                        em.barrier()
                    pan = pan_full[(ch % npan) * 16 * 131072:
                                    ((ch % npan) + 1) * 16 * 131072]
                    vv = em.cnt["vector"]
                    em.wait("scalar", "vector", vv)
                    em.dwait("scalar", "sync")
                    em.dwait("scalar", "scalar")
                    if dbg:
                        em.dma("scalar", whd_all[ch], w_hd[:])
                    for hq in range(16):
                        em.dma("scalar", _l0_band_dst(pan, hq),
                               w_hd[:, hq * 256:(hq + 1) * 256]
                               .rearrange("p (ts j) -> p ts j", ts=4, j=64))
                    for h in range(16):
                        em.wait("sync", "tensor")
                        lo = max(0, t0 - 128)
                        ln = min(t0 + 512, T) - lo
                        seg0 = (lo - (t0 - 128)) // 128
                        em.dma("sync", vTh[:, seg0 * 128:seg0 * 128 + ln],
                               vdr[h * 64:(h + 1) * 64, lo:lo + ln])
                        em.dwait("tensor", "sync")
                        em.wait("tensor", "vector")
                        if ch == 0:
                            em.wait("vector", "tensor")
                            em.mark("vector", nc.vector.memset(vtk[:, 0:64], 0.0))
                            em.wait("tensor", "vector")
                        for sgi in range(seg0, 5):
                            tr = nc.tensor.transpose(
                                ptr2[:, sgi * 64:(sgi + 1) * 64],
                                vTh[:, sgi * 128:(sgi + 1) * 128], ident[0:64, 0:64])
                        tc = em.mark("tensor", tr)
                        em.wait("vector", "tensor", tc)
                        vcp = em.mark("vector", nc.vector.tensor_copy(
                            vtk[:, seg0 * 64:5 * 64], ptr2[:, seg0 * 64:5 * 64]))
                        em.wait("tensor", "vector", vcp)
                        em.dwait("sync", "scalar")
                        em.dma("sync", Pp[:],
                               pan.rearrange("(h ti c) -> h ti c", h=16, ti=128)[h])
                        em.dwait("tensor", "sync")
                        for tsub in range(4):
                            for half in range(2):
                                tr = nc.tensor.transpose(
                                    ptr2[:, half * 128:(half + 1) * 128],
                                    Pp[:, tsub * 256 + half * 128: tsub * 256 + (half + 1) * 128],
                                    ident[:])
                            tc = em.mark("tensor", tr)
                            em.wait("vector", "tensor", tc)
                            vcp = em.mark("vector", nc.vector.tensor_copy(
                                PTs[:], ptr2[:, 0:256]))
                            em.wait("tensor", "vector", vcp)
                            for half in range(2):
                                mm = nc.tensor.matmul(
                                    ppv[:], vtk[:, (tsub + half) * 64:(tsub + half + 1) * 64],
                                    PTs[:, half * 128:(half + 1) * 128],
                                    start=(half == 0), stop=(half == 1))
                            mmc = em.mark("tensor", mm)
                            em.wait("vector", "tensor", mmc)
                            vcp = em.mark("vector", nc.vector.tensor_copy(
                                atT[(h % 2) * 64:(h % 2) * 64 + 64,
                                    (h // 2) * 512 + tsub * 128:(h // 2) * 512 + (tsub + 1) * 128],
                                ppv[:]))
                            em.wait("tensor", "vector", vcp)
                    if dbg and ch == 0:
                        em.barrier()
                        em.dma("sync", dd["d_Pp"], Pp[:])
                        em.dma("sync", dd["d_atT"], atT[:])
                        em.dwait("sync", "sync")
                        em.barrier()
                    em.wait("tensor", "vector")
                    if dbg:
                        em.wait("scalar", "vector")
                        em.dma("scalar", atT_all[ch], atT[:])
                    wslot2 = [wsl[:, 0:CT * 128], wsl[:, CT * 128:2 * CT * 128]]
                    for ct in range(CT):
                        sl = ct % 2
                        slot = wslot2[sl]
                        em.wait("gpsimd", "tensor", wctok[sl])
                        dc = em.dma("gpsimd", slot.rearrange("p (k c) -> p k c", c=128),
                               W_cproj.rearrange("(k p) m -> p k m", p=128)
                               [:, :, ct * 128:(ct + 1) * 128])
                        em.dwait("tensor", "gpsimd", dc)
                        bank = pbanks[ct % 2]
                        pap = bank.acquire(em)
                        for k in range(CT):
                            mm = nc.tensor.matmul(pap, slot[:, k * 128:(k + 1) * 128],
                                                  atT[:, k * 512:(k + 1) * 512],
                                                  start=(k == 0), stop=(k == CT - 1))
                        mmc = em.mark("tensor", mm)
                        wctok[sl] = mmc
                        em.wait("vector", "tensor", mmc)
                        em.wait("vector", "scalar")
                        vcp = em.mark("vector", nc.vector.tensor_copy(
                            bo_s[:, ct * 512:(ct + 1) * 512], pap))
                        bank.release("vector", vcp)
                    if dbg:
                        em.wait("scalar", "vector")
                        em.dma("scalar", bos_all[ch], bo_s[:])
                    emit_ln_stats_strip(em, lambda ct: bo_s[:, ct * 512:(ct + 1) * 512],
                                        512, ones_bf[:], None, sqb,
                                        pp0[:], pp1[:], rstd, mrstd, tmp,
                                        x_dtype_bf=True)
                    pbanks[0].release("vector", em.cnt["vector"])
                    pbanks[1].release("vector", em.cnt["vector"])
                    for ct in range(CT):
                        em.wait("sync", "vector")
                        em.dwait("sync", "scalar")
                        em.dma("sync", xt[:], x_in[ct * 128:(ct + 1) * 128, t0:t0 + 512])
                        em.dwait("vector", "sync")
                        nc.vector.tensor_mul(tmp[:], bo_s[:, ct * 512:(ct + 1) * 512], rstd[:])
                        nc.vector.tensor_sub(tmp[:], tmp[:], mrstd[:])
                        vcp = em.mark("vector", nc.vector.tensor_add(xt[:], xt[:], tmp[:]))
                        em.wait("scalar", "vector", vcp)
                        em.dma("scalar", x_out[ct * 128:(ct + 1) * 128, t0:t0 + 512], xt[:])
                        em.dwait("vector", "scalar")
                em.dwait("scalar", "scalar")
                em.barrier()
    return


def _l0_band_dst(pan, h):
    from concourse.ap import AP as _AP
    return _AP(pan.tensor, pan.offset + h * 131072 + 65,
               [[1025, 128], [256, 4], [1, 64]])


def run_l0_attn_unfused(xT_list, I):
    bdmg_rep = np.ascontiguousarray(np.tile(I["L0_attn_b_dmg"], 8)[:, None]).astype(np.float32)
    bdo_rep = np.ascontiguousarray(np.tile(I["L0_attn_b_dmg_out"], 128)[:, None]).astype(np.float32)
    nc = build_a()
    ins = [dict(x=xT, W_disp=I["L0_attn_W_disp"], W_val=I["L0_attn_W_val"],
                W_cproj=I["L0_attn_W_cproj"],
                relT=np.ascontiguousarray(I["L0_attn_rel_pos_emb"].T),
                W_strain=I["L0_attn_W_strain"], W_pos=I["L0_attn_W_pos"],
                W_bond=I["L0_attn_W_bond"], W_dmg=I["L0_attn_W_dmg"],
                W_dmg_out=I["L0_attn_W_dmg_out"], bdmg_rep=bdmg_rep,
                bdo_rep=bdo_rep) for xT in xT_list]
    r = run_bass_kernel_spmd(nc, ins, core_ids=list(range(len(ins))))
    return [res["xo"] for res in r.results]



def build_c(dbg=False):
    nc = bass.Bass("TRN2", target_bir_lowering=False, debug=False)
    x_in = nc.dram_tensor("x", [D, T], F32, kind="ExternalInput").ap()
    IO = {n: nc.dram_tensor(n, s, F32, kind="ExternalInput").ap()
          for n, s in (("Wq", [D, D]), ("Wk", [D, D]), ("Wv", [D, D]),
                       ("Wr", [D, 512]), ("Wo", [D, D]), ("bcols", [128, 36]))}
    x_out = nc.dram_tensor("xo", [D, T], F32, kind="ExternalOutput").ap()
    em = EM(nc)
    emit_const_pool(nc, em, None, [EPS, -30.0, 1e-30], alloc=nc)
    emit_c(nc, em, x_in, x_out, IO, dbg=dbg)
    return nc


def emit_c(nc, em, x_in, x_out, IO, dbg=False):
    Wq = IO["Wq"]; Wk = IO["Wk"]; Wv = IO["Wv"]; Wr = IO["Wr"]; Wo = IO["Wo"]
    bcols = IO["bcols"]

    kind = "ExternalOutput" if dbg else "Internal"
    if dbg:
        pdbg = nc.dram_tensor("pdbg", [128, 4 * T], BF16, kind="ExternalOutput").ap()
        vdbg = nc.dram_tensor("vdbg", [128, 2048], BF16, kind="ExternalOutput").ap()
        ndbg = nc.dram_tensor("ndbg", [128, 8], F32, kind="ExternalOutput").ap()
        mdbg = nc.dram_tensor("mdbg", [128, 32], F32, kind="ExternalOutput").ap()
    qdr = nc.dram_tensor("c_qdr", [D, T], BF16, kind=kind).ap()
    kdr = nc.dram_tensor("c_kdr", [D, T], BF16, kind=kind).ap()
    vdr = nc.dram_tensor("c_vdr", [D, T], BF16, kind=kind).ap()
    rdr = nc.dram_tensor("c_rdr", [512, T], BF16, kind=kind).ap()
    adr = nc.dram_tensor("c_adr", [D, T], BF16, kind=kind).ap()

    SCALE = 1.0 / np.sqrt(96.0)
    SCALE_R = float(np.sqrt(SCALE))
    NS = 8
    ncols = 512

    es = contextlib.ExitStack()
    with es:
        xn = es.enter_context(nc.sbuf_tensor("c_xn", [128, CT * T], BF16))
        bc_sb = es.enter_context(nc.sbuf_tensor("c_bc", [128, 36], F32))
        ones_bf = es.enter_context(nc.sbuf_tensor("c_ones_bf", [128, 128], BF16))
        ident = es.enter_context(nc.sbuf_tensor("c_ident", [128, 128], BF16))
        em.dma("sync", ones_bf[:], nc.inline_tensor(np.ones((128, 128), BF), _uname("ones_c")).ap())
        em.dma("sync", ident[:], nc.inline_tensor(np.eye(128, dtype=BF), _uname("ident_c")).ap())
        em.dma("sync", bc_sb[:], bcols)

        p1 = contextlib.ExitStack()
        with p1:
            xb = p1.enter_context(nc.sbuf_tensor("c_xb", [128, CT * ncols], F32))
            sqb = p1.enter_context(nc.sbuf_tensor("c_sqb", [128, CT * ncols], BF16))
            rstd = p1.enter_context(nc.sbuf_tensor("c_rstd", [128, ncols], F32))
            mrstd = p1.enter_context(nc.sbuf_tensor("c_mrstd", [128, ncols], F32))
            tmp = p1.enter_context(nc.sbuf_tensor("c_tmp", [128, ncols], F32))
            ones_f = p1.enter_context(nc.sbuf_tensor("c_ones_f", [128, 128], F32))
            pa = p1.enter_context(nc.psum_tensor("c_pa", [128, ncols], F32))
            pb = p1.enter_context(nc.psum_tensor("c_pb", [128, ncols], F32))
            em.dma("sync", ones_f[:], nc.inline_tensor(np.ones((128, 128), np.float32), _uname("onesf_c")).ap())
            for s in range(NS):
                col = slice(s * ncols, (s + 1) * ncols)
                em.wait("sync", "tensor")
                em.wait("sync", "vector")
                em.wait("sync", "scalar")
                for ct in range(CT):
                    em.dma("sync", xb[:, ct * ncols:(ct + 1) * ncols],
                           x_in[ct * 128:(ct + 1) * 128, col])
                em.dwait("tensor", "sync")
                em.dwait("scalar", "sync")
                em.dwait("vector", "sync")
                emit_ln_stats_strip(em, lambda ct: xb[:, ct * ncols:(ct + 1) * ncols],
                                    ncols, ones_bf[:], ones_f[:], sqb, pa, pb,
                                    rstd, mrstd, tmp)
                for ct in range(CT):
                    emit_ln_apply_tile(
                        em, xn[:, ct * T + s * ncols: ct * T + (s + 1) * ncols],
                        xb[:, ct * ncols:(ct + 1) * ncols],
                        rstd[:, :ncols], mrstd[:, :ncols], tmp[:])
            em.barrier()

        projs = [
            (Wq, qdr, CT, 0, SCALE),
            (Wk, kdr, CT, 8, None),
            (Wv, vdr, CT, 16, None),
            (Wr, rdr, 4, 32, SCALE_R),
        ]
        p2 = contextlib.ExitStack()
        with p2:
            wsl = p2.enter_context(nc.sbuf_tensor("c_wsl", [128, 4 * CT * 128], BF16))
            pbuf = p2.enter_context(nc.sbuf_tensor("c_pbuf", [128, 2 * ncols], BF16))
            pp0 = p2.enter_context(nc.psum_tensor("c_pp0", [128, ncols], F32))
            pp1 = p2.enter_context(nc.psum_tensor("c_pp1", [128, ncols], F32))
            pbanks = [Bank(pp0[:]), Bank(pp1[:])]
            bi = 0
            wslot = [wsl[:, i * CT * 128:(i + 1) * CT * 128] for i in range(4)]
            wtok = [0] * 4
            wi = 0
            for s in range(NS):
                col = slice(s * ncols, (s + 1) * ncols)
                for (Wd, odr, nmi, boff, scl) in projs:
                    for mi in range(nmi):
                        sl = wi % 4
                        slot = wslot[sl]
                        em.wait("gpsimd", "tensor", wtok[sl])
                        dc = em.dma("gpsimd", slot.rearrange("p (k c) -> p k c", c=128),
                               Wd.rearrange("(k p) m -> p k m", p=128)[:, :, mi * 128:(mi + 1) * 128])
                        em.dwait("tensor", "gpsimd", dc)
                        wi += 1
                        bank = pbanks[bi % 2]
                        pap = bank.acquire(em)
                        for k in range(CT):
                            mm = nc.tensor.matmul(
                                pap, slot[:, k * 128:(k + 1) * 128],
                                xn[:, k * T + s * ncols: k * T + (s + 1) * ncols],
                                start=(k == 0), stop=(k == CT - 1))
                        mmc = em.mark("tensor", mm)
                        wtok[sl] = mmc
                        em.wait("vector", "tensor", mmc)
                        em.dwait("vector", "scalar", max(0, em.dcnt["scalar"] - 1))
                        pb_ = pbuf[:, (bi % 2) * ncols:((bi % 2) + 1) * ncols]
                        if scl is None:
                            vc = em.mark("vector", nc.vector.tensor_scalar(
                                pb_, pap, bc_sb[:, boff + mi:boff + mi + 1], None, ALU.add))
                        else:
                            vc = em.mark("vector", nc.vector.tensor_scalar(
                                pb_, pap, bc_sb[:, boff + mi:boff + mi + 1], scl,
                                ALU.add, ALU.mult))
                        bank.release("vector", vc)
                        em.wait("scalar", "vector", vc)
                        em.dma("scalar", odr[mi * 128:(mi + 1) * 128, col], pb_)
                        bi += 1
            em.dwait("scalar", "scalar")
            em.barrier()

        NG = 4
        GQ = NG * 128
        p3 = contextlib.ExitStack()
        with p3:
            qr_sb = p3.enter_context(nc.sbuf_tensor("c_qr", [128, T], BF16))
            kr_sb = p3.enter_context(nc.sbuf_tensor("c_kr", [128, T], BF16))
            vT_sb = p3.enter_context(nc.sbuf_tensor("c_vT", [64, T], BF16))
            vtok = p3.enter_context(nc.sbuf_tensor("c_vtok", [128, 32 * 64], BF16))
            P_sb = p3.enter_context(nc.sbuf_tensor("c_Pb", [128, NG * T], BF16))
            PT_sb = p3.enter_context(nc.sbuf_tensor("c_PT", [128, 32 * GQ], BF16))
            mx = p3.enter_context(nc.sbuf_tensor("c_mx", [128, 8 * NG], F32))
            nmx = p3.enter_context(nc.sbuf_tensor("c_nmx", [128, NG], F32))
            dn = p3.enter_context(nc.sbuf_tensor("c_dn", [128, 2 * NG], F32))
            ag = p3.enter_context(nc.sbuf_tensor("c_ag", [128, CT * ncols], BF16))
            sA = p3.enter_context(nc.psum_tensor("c_sA", [128, 512], F32))
            sB = p3.enter_context(nc.psum_tensor("c_sB", [128, 512], F32))
            ptrb = p3.enter_context(nc.psum_tensor("c_ptr", [128, 512], BF16))
            pvb = p3.enter_context(nc.psum_tensor("c_pv", [64, 512], F32))
            simb = [Bank(sA[:]), Bank(sB[:])]
            for h in range(16):
                em.wait("sync", "tensor")
                em.wait("sync", "scalar")
                em.dma("sync", qr_sb[0:64, :], qdr[h * 64:(h + 1) * 64, :])
                em.dma("sync", qr_sb[64:96, :], rdr[h * 32:(h + 1) * 32, :])
                em.dma("sync", kr_sb[0:64, :], kdr[h * 64:(h + 1) * 64, :])
                em.dma("sync", kr_sb[64:96, :], rdr[h * 32:(h + 1) * 32, :])
                em.dma("sync", vT_sb[:, :], vdr[h * 64:(h + 1) * 64, :])
                em.dwait("tensor", "sync")
                em.wait("tensor", "vector")
                for j8 in range(4):
                    for jj in range(8):
                        j = j8 * 8 + jj
                        tr = nc.tensor.transpose(
                            ptrb[:, jj * 64:(jj + 1) * 64],
                            vT_sb[:, j * 128:(j + 1) * 128], ident[0:64, 0:64])
                    tc = em.mark("tensor", tr)
                    em.wait("vector", "tensor", tc)
                    vc = em.mark("vector", nc.vector.tensor_copy(
                        vtok[:, j8 * 512:(j8 + 1) * 512], ptrb[:, 0:512]))
                    em.wait("tensor", "vector", vc)
                for g in range(T // GQ):
                    em.wait("scalar", "vector")
                    for qt in range(NG):
                        q0 = g * GQ + qt * 128
                        lhs = qr_sb[0:96, q0:q0 + 128]
                        for si in range(NS):
                            bank = simb[si % 2]
                            pap = bank.acquire(em)
                            mm = nc.tensor.matmul(pap, lhs,
                                                  kr_sb[0:96, si * 512:(si + 1) * 512],
                                                  start=True, stop=True)
                            mmc = em.mark("tensor", mm)
                            em.wait("scalar", "tensor", mmc)
                            sc = em.mark("scalar", nc.scalar.activation(
                                P_sb[:, qt * T + si * 512: qt * T + (si + 1) * 512],
                                pap, AF.Exp, bias=-30.0))
                            bank.release("scalar", sc)
                    scc = em.cnt["scalar"]
                    em.wait("vector", "scalar", scc)
                    for qt in range(NG):
                        em.mark("vector", nc.vector.reduce_sum(
                            dn[:, qt:qt + 1], P_sb[:, qt * T:(qt + 1) * T], axis=AX.X))
                    em.wait("vector", "vector")
                    for qt in range(NG):
                        em.mark("vector", nc.vector.reciprocal(
                            dn[:, NG + qt:NG + qt + 1], dn[:, qt:qt + 1]))
                    vcc = em.cnt["vector"]
                    em.wait("scalar", "vector", vcc)
                    for qt in range(NG):
                        for hf in range(2):
                            cs = slice(qt * T + hf * 2048, qt * T + (hf + 1) * 2048)
                            em.mark("scalar", nc.scalar.activation(
                                P_sb[:, cs], P_sb[:, cs], AF.Copy,
                                scale=dn[:, NG + qt:NG + qt + 1]))
                    if dbg and h == 0 and g == 0:
                        em.barrier()
                        em.dma("scalar", pdbg, P_sb[:])
                        em.dma("scalar", vdbg, vtok[:])
                        em.dma("scalar", ndbg[:, 4:4 + NG], dn[:, NG:2 * NG])
                        em.dwait("scalar", "scalar")
                        em.barrier()
                    em.wait("tensor", "scalar")
                    em.wait("tensor", "vector")
                    for j in range(32):
                        for qt in range(NG):
                            tr = nc.tensor.transpose(
                                ptrb[:, qt * 128:(qt + 1) * 128],
                                P_sb[:, qt * T + j * 128: qt * T + (j + 1) * 128],
                                ident[:])
                        tc = em.mark("tensor", tr)
                        em.wait("vector", "tensor", tc)
                        vc = em.mark("vector", nc.vector.tensor_copy(
                            PT_sb[:, j * GQ:(j + 1) * GQ], ptrb[:, 0:GQ]))
                        em.wait("tensor", "vector", vc)
                    for j in range(32):
                        mm = nc.tensor.matmul(pvb[:], vtok[:, j * 64:(j + 1) * 64],
                                              PT_sb[:, j * GQ:(j + 1) * GQ],
                                              start=(j == 0), stop=(j == 31))
                    mmc = em.mark("tensor", mm)
                    em.wait("vector", "tensor", mmc)
                    em.dwait("vector", "scalar")
                    vc = em.mark("vector", nc.vector.tensor_copy(
                        ag[(h % 2) * 64:(h % 2) * 64 + 64,
                           (h // 2) * ncols:(h // 2 + 1) * ncols], pvb[:]))
                    em.wait("scalar", "vector", vc)
                    em.wait("tensor", "vector", vc)
                    em.dma("scalar",
                           adr.rearrange("(ct p) t -> ct p t", p=128)
                           [h // 2, (h % 2) * 64:(h % 2) * 64 + 64, g * GQ:(g + 1) * GQ],
                           ag[(h % 2) * 64:(h % 2) * 64 + 64,
                              (h // 2) * ncols:(h // 2 + 1) * ncols])
            em.dwait("scalar", "scalar")
            em.barrier()

        p4 = contextlib.ExitStack()
        with p4:
            wsl = p4.enter_context(nc.sbuf_tensor("c_wsl4", [128, 2 * CT * 128], BF16))
            ab = p4.enter_context(nc.sbuf_tensor("c_ab", [128, CT * ncols], BF16))
            bo_strip = p4.enter_context(nc.sbuf_tensor("c_bo_strip", [128, CT * ncols], BF16))
            sqb = p4.enter_context(nc.sbuf_tensor("c_sqb4", [128, CT * ncols], BF16))
            rstd = p4.enter_context(nc.sbuf_tensor("c_rstd4", [128, ncols], F32))
            mrstd = p4.enter_context(nc.sbuf_tensor("c_mrstd4", [128, ncols], F32))
            tmp = p4.enter_context(nc.sbuf_tensor("c_tmp4", [128, ncols], F32))
            xt_sb = p4.enter_context(nc.sbuf_tensor("c_xt", [128, ncols], F32))
            pp0 = p4.enter_context(nc.psum_tensor("c_pp04", [128, ncols], F32))
            pp1 = p4.enter_context(nc.psum_tensor("c_pp14", [128, ncols], F32))
            pa = p4.enter_context(nc.psum_tensor("c_pa4", [128, ncols], F32))
            pb = p4.enter_context(nc.psum_tensor("c_pb4", [128, ncols], F32))
            wslot = [wsl[:, 0:CT * 128], wsl[:, CT * 128:2 * CT * 128]]
            w4tok = [0, 0]
            pbanks = [Bank(pp0[:]), Bank(pp1[:])]
            wi = 0
            for s in range(NS):
                col = slice(s * ncols, (s + 1) * ncols)
                em.wait("sync", "tensor")
                for ct in range(CT):
                    em.dma("sync", ab[:, ct * ncols:(ct + 1) * ncols],
                           adr[ct * 128:(ct + 1) * 128, col])
                em.dwait("tensor", "sync")
                for mi in range(CT):
                    sl = wi % 2
                    slot = wslot[sl]
                    em.wait("gpsimd", "tensor", w4tok[sl])
                    dc = em.dma("gpsimd", slot.rearrange("p (k c) -> p k c", c=128),
                           Wo.rearrange("(k p) m -> p k m", p=128)[:, :, mi * 128:(mi + 1) * 128])
                    em.dwait("tensor", "gpsimd", dc)
                    bank = pbanks[wi % 2]
                    pap = bank.acquire(em)
                    for k in range(CT):
                        mm = nc.tensor.matmul(pap, slot[:, k * 128:(k + 1) * 128],
                                              ab[:, k * ncols:(k + 1) * ncols],
                                              start=(k == 0), stop=(k == CT - 1))
                    mmc = em.mark("tensor", mm)
                    w4tok[sl] = mmc
                    em.wait("vector", "tensor", mmc)
                    em.wait("vector", "scalar")
                    vc = em.mark("vector", nc.vector.tensor_scalar(
                        bo_strip[:, mi * ncols:(mi + 1) * ncols], pap,
                        bc_sb[:, 24 + mi:24 + mi + 1], None, ALU.add))
                    bank.release("vector", vc)
                    wi += 1
                emit_ln_stats_strip(em, lambda ct: bo_strip[:, ct * ncols:(ct + 1) * ncols],
                                    ncols, ones_bf[:], None, sqb, pa, pb,
                                    rstd, mrstd, tmp, x_dtype_bf=True)
                for ct in range(CT):
                    em.wait("sync", "vector")
                    em.dwait("sync", "scalar")
                    em.dma("sync", xt_sb[:], x_in[ct * 128:(ct + 1) * 128, col])
                    em.dwait("vector", "sync")
                    nc.vector.tensor_mul(tmp[:], bo_strip[:, ct * ncols:(ct + 1) * ncols],
                                         rstd[:, :ncols])
                    nc.vector.tensor_sub(tmp[:], tmp[:], mrstd[:, :ncols])
                    vc = em.mark("vector", nc.vector.tensor_add(xt_sb[:], xt_sb[:], tmp[:]))
                    em.wait("scalar", "vector", vc)
                    em.dma("scalar", x_out[ct * 128:(ct + 1) * 128, col], xt_sb[:])
                    em.dwait("vector", "scalar")
            em.dwait("scalar", "scalar")
            em.barrier()
    return


def run_l1_attn_unfused(xT_list, I):
    bcols = np.zeros((128, 36), np.float32)
    bcols[:, 0:8] = _cols(I["L1_attn_bq"])
    bcols[:, 8:16] = _cols(I["L1_attn_bk"])
    bcols[:, 16:24] = _cols(I["L1_attn_bv"])
    bcols[:, 24:32] = _cols(I["L1_attn_bo"])
    bcols[:, 32:36] = np.ascontiguousarray(I["L1_attn_br"].reshape(4, 128).T)
    nc = build_c()
    ins = [dict(x=xT, Wq=I["L1_attn_Wq"], Wk=I["L1_attn_Wk"], Wv=I["L1_attn_Wv"],
                Wr=I["L1_attn_Wr"], Wo=I["L1_attn_Wo"], bcols=bcols)
           for xT in xT_list]
    r = run_bass_kernel_spmd(nc, ins, core_ids=list(range(NCORES)))
    return [res["xo"] for res in r.results]



def _l1ffn_tables():
    DIM_HEAD, WINDOW, n = 64, 256, T
    half = np.arange(0, DIM_HEAD, 2, dtype=np.float64)
    inv_freq = 1.0 / 10000.0 ** (half / DIM_HEAD)
    t = np.arange(n, dtype=np.float64)
    freqs = t[:, None] * inv_freq[None, :]
    freqs = np.concatenate([freqs, freqs], -1)
    base_scale = (half + 0.4 * DIM_HEAD) / (1.4 * DIM_HEAD)
    power = (t - n // 2) / (WINDOW // 2)
    scale = base_scale[None, :] ** power[:, None]
    scale = np.concatenate([scale, scale], -1)
    cos, sin = np.cos(freqs), np.sin(freqs)
    sgn = np.concatenate([-np.ones(32), np.ones(32)])[None, :]
    QK_SCALE = 8.0
    Aq = (QK_SCALE * cos * scale).astype(BF)
    Bq = (QK_SCALE * sin * scale * sgn).astype(BF)
    Ak = (cos / scale).astype(BF)
    Bk = (sin / scale * sgn).astype(BF)
    return Aq, Bq, Ak, Bk


def _l1ffn_masks():
    i = np.arange(128)[:, None]
    jl = np.arange(384)[None, :]
    std = np.where((jl >= i) & (jl <= i + 256), 0.0, -1e30).astype(np.float32)
    f0 = np.where((jl >= i) & (jl <= i + 256) & (jl >= 256), 0.0, -1e30).astype(np.float32)
    f1 = np.where((jl >= i) & (jl <= i + 256) & (jl >= 128), 0.0, -1e30).astype(np.float32)
    return std, f0, f1


def build_d(dbg=False):
    nc = bass.Bass("TRN2", target_bir_lowering=False, debug=False)
    x_in = nc.dram_tensor("x", [D, T], F32, kind="ExternalInput").ap()
    IO = {n: nc.dram_tensor(n, s, F32, kind="ExternalInput").ap()
          for n, s in (("Wqkv", [6, D, 3072]), ("Wo_d", [6, D, D]),
                       ("W1", [6, D, 5460]), ("W2", [6, 2730, D]),
                       ("qs_rows", [6, 2, D]), ("gb_cols", [6, 2, 128, CT]),
                       ("b1_cols", [6, 2, 128, 22]), ("b2_cols", [6, 128, CT]))}
    x_out = nc.dram_tensor("xo", [D, T], F32, kind="ExternalOutput").ap()
    em = EM(nc)
    emit_const_pool(nc, em, None, [EPS, 1e-24, -20.0, 1e-30], alloc=nc)
    emit_d(nc, em, x_in, x_out, IO, dbg=dbg)
    return nc


def emit_d(nc, em, x_in, x_out, IO, dbg=False):
    Wqkv = IO["Wqkv"]; Wo = IO["Wo_d"]; W1 = IO["W1"]; W2 = IO["W2"]
    qs_rows = IO["qs_rows"]; gb_cols = IO["gb_cols"]; b1_cols = IO["b1_cols"]
    b2_cols = IO["b2_cols"]

    xi = nc.dram_tensor("d_xi", [D, T], F32).ap()
    if dbg:
        xd = [nc.dram_tensor(f"xd{i}", [D, T], F32, kind="ExternalOutput").ap()
              for i in range(3)]
        dqh = [nc.dram_tensor("dqh", [128, 8 * 512], BF16, kind="ExternalOutput").ap(),
               nc.dram_tensor("dkh", [128, 8 * 768], BF16, kind="ExternalOutput").ap(),
               nc.dram_tensor("dvb", [128, 16 * 6 * 64], BF16, kind="ExternalOutput").ap(),
               nc.dram_tensor("drot", [128, 3200], BF16, kind="ExternalOutput").ap(),
               nc.dram_tensor("drec", [128, 16], BF16, kind="ExternalOutput").ap(),
               nc.dram_tensor("dxs", [128, 8 * 512], BF16, kind="ExternalOutput").ap(),
               nc.dram_tensor("dslot", [128, 8 * 512], BF16, kind="ExternalOutput").ap()]
        dff = [nc.dram_tensor("dxn2", [128, 8 * 1024], BF16, kind="ExternalOutput").ap(),
               nc.dram_tensor("drst", [128, 2 * 1024], F32, kind="ExternalOutput").ap(),
               nc.dram_tensor("dxs2", [128, 8 * 1024], BF16, kind="ExternalOutput").ap()]

    Aq, Bq, Ak, Bk = _l1ffn_tables()
    mstd, mf0, mf1 = _l1ffn_masks()

    NCH = 8
    CH = 512
    es = contextlib.ExitStack()
    with es:
        ones_bf = es.enter_context(nc.sbuf_tensor("d_ones_bf", [128, 128], BF16))
        ones_f = es.enter_context(nc.sbuf_tensor("d_ones_f", [128, 128], F32))
        ident = es.enter_context(nc.sbuf_tensor("d_ident", [128, 128], BF16))
        masks = es.enter_context(nc.sbuf_tensor("d_masks", [128, 3 * 384], F32))
        xcol = es.enter_context(nc.sbuf_tensor("d_xcol", [128, CT], F32))
        qsr = es.enter_context(nc.sbuf_tensor("d_qsr", [128, 2 * D], BF16))
        gbc = es.enter_context(nc.sbuf_tensor("d_gbc", [128, 2 * CT], F32))
        b1c = es.enter_context(nc.sbuf_tensor("d_b1c", [128, 2 * 22], F32))
        b2c = es.enter_context(nc.sbuf_tensor("d_b2c", [128, CT], F32))
        em.dma("sync", ones_bf[:], nc.inline_tensor(np.ones((128, 128), BF), _uname("ones_c")).ap())
        em.dma("sync", ones_f[:], nc.inline_tensor(np.ones((128, 128), np.float32), _uname("onesf_c")).ap())
        em.dma("sync", ident[:], nc.inline_tensor(np.eye(128, dtype=BF), _uname("ident_c")).ap())
        em.dma("sync", masks[:, 0:384], nc.inline_tensor(mstd, _uname("mask_std")).ap())
        em.dma("sync", masks[:, 384:768], nc.inline_tensor(mf0, _uname("mask_f0")).ap())
        em.dma("sync", masks[:, 768:1152], nc.inline_tensor(mf1, _uname("mask_f1")).ap())
        aq_d = nc.inline_tensor(Aq, _uname("rot_aq")).ap()
        bq_d = nc.inline_tensor(Bq, _uname("rot_bq")).ap()
        ak_d = nc.inline_tensor(Ak, _uname("rot_ak")).ap()
        bk_d = nc.inline_tensor(Bk, _uname("rot_bk")).ap()

        p0s = contextlib.ExitStack()
        with p0s:
            xb = p0s.enter_context(nc.sbuf_tensor("d_i_xb", [128, CT * 512], F32))
            sqb = p0s.enter_context(nc.sbuf_tensor("d_i_sqb", [128, CT * 512], BF16))
            rstd = p0s.enter_context(nc.sbuf_tensor("d_i_rstd", [128, 512], F32))
            mrstd = p0s.enter_context(nc.sbuf_tensor("d_i_mrstd", [128, 512], F32))
            tmp = p0s.enter_context(nc.sbuf_tensor("d_i_tmp", [128, 512], F32))
            ob = p0s.enter_context(nc.sbuf_tensor("d_i_ob", [128, CT * 512], F32))
            pa = p0s.enter_context(nc.psum_tensor("d_i_pa", [128, 512], F32))
            pb = p0s.enter_context(nc.psum_tensor("d_i_pb", [128, 512], F32))
            for s in range(8):
                col = slice(s * 512, (s + 1) * 512)
                em.wait("sync", "tensor")
                em.wait("sync", "vector")
                em.wait("sync", "scalar")
                em.dwait("sync", "scalar")
                for ct in range(CT):
                    em.dma("sync", xb[:, ct * 512:(ct + 1) * 512],
                           x_in[ct * 128:(ct + 1) * 128, col])
                em.dwait("tensor", "sync")
                em.dwait("scalar", "sync")
                em.dwait("vector", "sync")
                emit_ln_stats_strip(em, lambda ct: xb[:, ct * 512:(ct + 1) * 512],
                                    512, ones_bf[:], ones_f[:], sqb, pa, pb,
                                    rstd, mrstd, tmp)
                for ct in range(CT):
                    emit_ln_apply_tile(em, ob[:, ct * 512:(ct + 1) * 512],
                                       xb[:, ct * 512:(ct + 1) * 512],
                                       rstd[:], mrstd[:], tmp[:])
                vv = em.cnt["vector"]
                em.wait("scalar", "vector", vv)
                for ct in range(CT):
                    em.dma("scalar", xi[ct * 128:(ct + 1) * 128, col],
                           ob[:, ct * 512:(ct + 1) * 512])
            em.dwait("scalar", "scalar")
            em.barrier()

        def dump(i):
            if dbg:
                em.barrier()
                em.dma("sync", xd[i], xi)
                em.dwait("sync", "sync")
                em.barrier()
        dump(0)
        for li in range(6):
            _emit_d_attn_half(nc, em, li, xi, Wqkv, Wo, qs_rows,
                              aq_d, bq_d, ak_d, bk_d, masks, ident, xcol, qsr,
                              dbg_l0=(dqh if (dbg and li == 0) else None))
            if li == 0:
                dump(1)
            _emit_d_ffn_half(nc, em, li, xi, W1, W2, gb_cols, b1_cols, b2_cols,
                             ones_bf, xcol,
                             dbg_f0=(dff if (dbg and li == 0) else None))
            if li == 0:
                dump(2)

        p9s = contextlib.ExitStack()
        with p9s:
            hb = p9s.enter_context(nc.sbuf_tensor("d_f_hb", [128, CT * 512], F32))
            sqb = p9s.enter_context(nc.sbuf_tensor("d_f_sqb", [128, CT * 512], BF16))
            rstd = p9s.enter_context(nc.sbuf_tensor("d_f_rstd", [128, 512], F32))
            mrstd = p9s.enter_context(nc.sbuf_tensor("d_f_mrstd", [128, 512], F32))
            tmp = p9s.enter_context(nc.sbuf_tensor("d_f_tmp", [128, 512], F32))
            xt = p9s.enter_context(nc.sbuf_tensor("d_f_xt", [128, 512], F32))
            pa = p9s.enter_context(nc.psum_tensor("d_f_pa", [128, 512], F32))
            pb = p9s.enter_context(nc.psum_tensor("d_f_pb", [128, 512], F32))
            for s in range(8):
                col = slice(s * 512, (s + 1) * 512)
                em.wait("sync", "tensor")
                em.wait("sync", "vector")
                em.wait("sync", "scalar")
                for ct in range(CT):
                    em.dma("sync", hb[:, ct * 512:(ct + 1) * 512],
                           xi[ct * 128:(ct + 1) * 128, col])
                em.dwait("tensor", "sync")
                em.dwait("scalar", "sync")
                em.dwait("vector", "sync")
                emit_ln_stats_strip(em, lambda ct: hb[:, ct * 512:(ct + 1) * 512],
                                    512, ones_bf[:], ones_f[:], sqb, pa, pb,
                                    rstd, mrstd, tmp)
                for ct in range(CT):
                    em.wait("sync", "vector")
                    em.dwait("sync", "scalar")
                    em.dma("sync", xt[:], x_in[ct * 128:(ct + 1) * 128, col])
                    em.dwait("vector", "sync")
                    nc.vector.tensor_mul(tmp[:], hb[:, ct * 512:(ct + 1) * 512], rstd[:])
                    nc.vector.tensor_sub(tmp[:], tmp[:], mrstd[:])
                    vc = em.mark("vector", nc.vector.tensor_add(xt[:], xt[:], tmp[:]))
                    em.wait("scalar", "vector", vc)
                    em.dma("scalar", x_out[ct * 128:(ct + 1) * 128, col], xt[:])
                    em.dwait("vector", "scalar")
            em.dwait("scalar", "scalar")
            em.barrier()
    return


def _emit_d_attn_half(nc, em, li, xi, Wqkv, Wo, qs_rows, aq_d, bq_d, ak_d, bk_d,
                      masks, ident, xcol, qsr, dbg_l0=None):
    CH = 512
    em.barrier()
    ps = contextlib.ExitStack()
    P = lambda n: f"a{li}_{n}"
    with ps:
        xt2 = ps.enter_context(nc.sbuf_tensor(P("xt2"), [128, 2 * CH], F32))
        xs = ps.enter_context(nc.sbuf_tensor(P("xs"), [128, CT * CH], BF16))
        wsl = ps.enter_context(nc.sbuf_tensor(P("wsl"), [128, 2 * CT * 512], BF16))
        qc = ps.enter_context(nc.sbuf_tensor(P("qc"), [128, 4 * 1024], BF16))
        kc = ps.enter_context(nc.sbuf_tensor(P("kc"), [128, 4 * 1024], BF16))
        vc_sb = ps.enter_context(nc.sbuf_tensor(P("vc"), [128, 4 * 1024], BF16))
        sqs = ps.enter_context(nc.sbuf_tensor(P("sqs"), [128, 1024], BF16))
        nrm = ps.enter_context(nc.sbuf_tensor(P("nrm"), [128, 16], F32))
        rec = ps.enter_context(nc.sbuf_tensor(P("rec"), [128, 16], BF16))
        qn_scr = ps.enter_context(nc.sbuf_tensor(P("qns"), [128, 1024], BF16))
        qhat = ps.enter_context(nc.sbuf_tensor(P("qhat"), [128, 1024], BF16))
        khat = ps.enter_context(nc.sbuf_tensor(P("khat"), [128, 1024], BF16))
        rots = ps.enter_context(nc.sbuf_tensor(P("rots"), [128, 4 * 64], BF16))
        qhT = ps.enter_context(nc.sbuf_tensor(P("qhT"), [128, 8 * 512], BF16))
        khT = ps.enter_context(nc.sbuf_tensor(P("khT"), [128, 8 * 768], BF16))
        vbuf = ps.enter_context(nc.sbuf_tensor(P("vbuf"), [128, 16 * 6 * 64], BF16))
        simb = ps.enter_context(nc.sbuf_tensor(P("simb"), [128, 2 * 384], F32))
        Pb = ps.enter_context(nc.sbuf_tensor(P("Pb"), [128, 2 * 384], BF16))
        PTb = ps.enter_context(nc.sbuf_tensor(P("PTb"), [128, 2 * 384], BF16))
        mxd = ps.enter_context(nc.sbuf_tensor(P("mxd"), [128, 8], F32))
        atT = ps.enter_context(nc.sbuf_tensor(P("atT"), [128, CT * 512], BF16))
        u_sb = ps.enter_context(nc.sbuf_tensor(P("u"), [128, 2 * 512], F32))
        pA = ps.enter_context(nc.psum_tensor(P("pA"), [128, 512], F32))
        pB = ps.enter_context(nc.psum_tensor(P("pB"), [128, 512], F32))
        pC = ps.enter_context(nc.psum_tensor(P("pC"), [128, 512], F32))
        tb1 = ps.enter_context(nc.psum_tensor(P("tb1"), [128, 512], BF16))
        tb2 = ps.enter_context(nc.psum_tensor(P("tb2"), [128, 512], BF16))
        pv = ps.enter_context(nc.psum_tensor(P("pv"), [64, 512], F32))

        em.wait("gpsimd", "vector")
        em.dma("gpsimd", qsr[:, 0:D], qs_rows[li, 0:1, :].partition_broadcast(128))
        em.dma("gpsimd", qsr[:, D:2 * D], qs_rows[li, 1:2, :].partition_broadcast(128))
        gbank = [Bank(pA[:]), Bank(pB[:])]
        wslot = [wsl[:, 0:CT * 512], wsl[:, CT * 512:2 * CT * 512]]
        wqtok = [0, 0]
        tbank = [Bank(tb1[:]), Bank(tb2[:])]
        em.dwait("vector", "gpsimd")
        em.wait("vector", "tensor")
        nc.vector.memset(khT[:], 0.0)
        em.mark("vector", nc.vector.memset(vbuf[:], 0.0))

        for ch in range(8):
            t0 = ch * CH
            em.wait("sync", "tensor")
            em.wait("sync", "vector")
            em.wait("vector", "tensor")
            for ct in range(CT):
                if ct >= 2:
                    em.wait("sync", "vector")
                em.dma("sync", xt2[:, (ct % 2) * CH:(ct % 2 + 1) * CH],
                       xi[ct * 128:(ct + 1) * 128, t0:t0 + CH])
                em.dwait("vector", "sync")
                xtile = xt2[:, (ct % 2) * CH:(ct % 2 + 1) * CH]
                dst = xs[:, ct * CH:(ct + 1) * CH]
                if ct < 4:
                    vcp = nc.vector.tensor_copy(dst, xtile)
                else:
                    if ch == 0:
                        nc.vector.memset(dst[:, 0:1], 0.0)
                    else:
                        nc.vector.tensor_copy(dst[:, 0:1], xcol[:, ct:ct + 1])
                    vcp = nc.vector.tensor_copy(dst[:, 1:CH], xtile[:, 0:CH - 1])
                em.mark("vector", vcp)
                if ct >= 4:
                    em.mark("vector", nc.vector.tensor_copy(
                        xcol[:, ct:ct + 1], xtile[:, CH - 1:CH]))
            if dbg_l0 and ch == 0:
                em.barrier()
                em.dma("sync", dbg_l0[5], xs[:])
                em.dwait("sync", "sync")
                em.barrier()
            for ns in range(6):
                sl = ns % 2
                slot = wslot[sl]
                em.wait("gpsimd", "tensor", wqtok[sl])
                dc = em.dma("gpsimd", slot.rearrange("p (k c) -> p k c", c=512),
                       Wqkv[li].rearrange("(k p) m -> p k m", p=128)
                       [:, :, ns * 512:(ns + 1) * 512])
                em.dwait("tensor", "gpsimd", dc)
                em.wait("tensor", "vector")
                dstbuf = (qc, kc, vc_sb)[ns // 2]
                for tt in range(4):
                    bank = gbank[tt % 2]
                    pap = bank.acquire(em)
                    for k in range(CT):
                        mm = nc.tensor.matmul(
                            pap, xs[:, k * CH + tt * 128: k * CH + (tt + 1) * 128],
                            slot[:, k * 512:(k + 1) * 512],
                            start=(k == 0), stop=(k == CT - 1))
                    mmc = em.mark("tensor", mm)
                    wqtok[sl] = mmc
                    em.wait("vector", "tensor", mmc)
                    vcp = em.mark("vector", nc.vector.tensor_copy(
                        dstbuf[:, tt * 1024 + (ns % 2) * 512: tt * 1024 + (ns % 2 + 1) * 512],
                        pap))
                    bank.release("vector", vcp)
            for tt in range(4):
                for (buf, hat, a_d, b_d, qoff) in ((qc, qhat, aq_d, bq_d, 0),
                                                   (kc, khat, ak_d, bk_d, D)):
                    blk = buf[:, tt * 1024:(tt + 1) * 1024]
                    r0 = t0 + tt * 128
                    em.wait("sync", "vector")
                    em.dma("sync", rots[:, 0:64], a_d[r0:r0 + 128, :])
                    em.dma("sync", rots[:, 64:128], b_d[r0:r0 + 128, :])
                    em.wait("scalar", "vector")
                    sc = em.mark("scalar", nc.scalar.activation(sqs[:], blk, AF.Square))
                    em.wait("vector", "scalar", sc)
                    em.mark("vector", nc.vector.reduce_sum(
                        nrm[:], sqs[:].rearrange("p (h d) -> p h d", d=64), axis=AX.X))
                    em.wait("vector", "vector")
                    em.wait("scalar", "vector")
                    em.mark("scalar", nc.scalar.activation(nrm[:], nrm[:], AF.Ln,
                                                            bias=1e-24))
                    em.wait("scalar", "scalar")
                    sc = em.mark("scalar", nc.scalar.activation(
                        rec[:], nrm[:], AF.Exp, scale=-0.5))
                    em.wait("vector", "scalar", sc)
                    em.dwait("vector", "sync")
                    nc.vector.tensor_tensor(
                        qn_scr[:], blk,
                        rec[:].unsqueeze(2).broadcast_to([128, 16, 64]),
                        ALU.mult)
                    em.mark("vector", nc.vector.tensor_tensor(
                        qn_scr[:], qn_scr[:], qsr[:, qoff:qoff + D], ALU.mult))
                    em.wait("vector", "vector")
                    nc.vector.tensor_tensor(
                        sqs[:], qn_scr[:],
                        rots[:, 0:64].unsqueeze(1).broadcast_to([128, 16, 64]),
                        ALU.mult)
                    q3 = qn_scr[:].rearrange("p (h d) -> p h d", d=64)
                    h3o = hat[:].rearrange("p (h d) -> p h d", d=64)
                    nc.vector.tensor_tensor(
                        h3o[:, :, 0:32], q3[:, :, 32:64],
                        rots[:, 64:96].unsqueeze(1).broadcast_to([128, 16, 32]),
                        ALU.mult)
                    em.mark("vector", nc.vector.tensor_tensor(
                        h3o[:, :, 32:64], q3[:, :, 0:32],
                        rots[:, 96:128].unsqueeze(1).broadcast_to([128, 16, 32]),
                        ALU.mult))
                    em.wait("vector", "vector")
                    em.mark("vector", nc.vector.tensor_add(hat[:], hat[:], sqs[:]))
                    if dbg_l0 and ch == 0 and tt == 0 and buf is qc:
                        em.barrier()
                        em.dma("sync", dbg_l0[3][:, 0:1024], qc[:, 0:1024])
                        em.dma("sync", dbg_l0[3][:, 1024:2048], qn_scr[:])
                        em.dma("sync", dbg_l0[3][:, 2048:3072], qhat[:])
                        em.dma("sync", dbg_l0[3][:, 3072:3136], rots[:, 0:64])
                        em.dma("sync", dbg_l0[3][:, 3136:3200], rots[:, 64:128])
                        em.dma("sync", dbg_l0[4][:, 0:16], rec[:])
                        em.dwait("sync", "sync")
                        em.barrier()
                em.wait("tensor", "vector")
                for (hat2, dstT, wide, off) in ((qhat, qhT, 512, tt * 128),
                                                (khat, khT, 768, 256 + tt * 128)):
                    for p in range(8):
                        bank = tbank[p % 2]
                        pap = bank.acquire(em)
                        for hh in range(2):
                            h = p * 2 + hh
                            tr = nc.tensor.transpose(
                                pap[hh * 64:hh * 64 + 64, 0:128],
                                hat2[:, h * 64:(h + 1) * 64], ident[:])
                        tc = em.mark("tensor", tr)
                        em.wait("vector", "tensor", tc)
                        vcp = em.mark("vector", nc.vector.tensor_copy(
                            dstT[:, p * wide + off: p * wide + off + 128],
                            pap[:, 0:128]))
                        bank.release("vector", vcp)
                em.mark("vector", nc.vector.tensor_copy(
                    vbuf[:].rearrange("p (h s d) -> p h s d", s=6, d=64)[:, :, 2 + tt],
                    vc_sb[:, tt * 1024:(tt + 1) * 1024].rearrange("p (h d) -> p h d", d=64)))
            if dbg_l0 and ch == 0:
                em.barrier()
                em.dma("sync", dbg_l0[0], qhT[:])
                em.dma("sync", dbg_l0[1], khT[:])
                em.dma("sync", dbg_l0[2], vbuf[:])
                em.dwait("sync", "sync")
                em.barrier()
            em.wait("vector", "scalar")
            sim_tok = {}
            NIT = 64
            def a_sim(it):
                h, qt = it // 4, it % 4
                p_, hh = h // 2, h % 2
                bank = gbank[it % 2]
                pap = bank.acquire(em)
                mm = nc.tensor.matmul(
                    pap[:, 0:384],
                    qhT[hh * 64:hh * 64 + 64, p_ * 512 + qt * 128: p_ * 512 + (qt + 1) * 128],
                    khT[hh * 64:hh * 64 + 64, p_ * 768 + qt * 128: p_ * 768 + qt * 128 + 384],
                    start=True, stop=True)
                sim_tok[it] = (em.mark("tensor", mm), bank)
            def a_rest(it):
                h, qt = it // 4, it % 4
                p_, hh = h // 2, h % 2
                mmc, bank = sim_tok.pop(it)
                pap = bank.ap
                sl = it % 2
                if ch == 0 and qt == 0:
                    mk = masks[:, 384:768]
                elif ch == 0 and qt == 1:
                    mk = masks[:, 768:1152]
                else:
                    mk = masks[:, 0:384]
                em.wait("vector", "tensor", mmc)
                em.wait("vector", "scalar")
                vcp = em.mark("vector", nc.vector.tensor_tensor(
                    simb[:, sl * 384:(sl + 1) * 384], pap[:, 0:384], mk, ALU.add))
                bank.release("vector", vcp)
                em.wait("scalar", "vector", vcp)
                sc = em.mark("scalar", nc.scalar.activation(
                    Pb[:, sl * 384:(sl + 1) * 384], simb[:, sl * 384:(sl + 1) * 384],
                    AF.Exp, bias=-20.0))
                em.wait("vector", "scalar", sc)
                em.mark("vector", nc.vector.reduce_sum(
                    mxd[:, 2 + sl:3 + sl], Pb[:, sl * 384:(sl + 1) * 384], axis=AX.X))
                em.wait("vector", "vector")
                em.mark("vector", nc.vector.reciprocal(
                    mxd[:, 4 + sl:5 + sl], mxd[:, 2 + sl:3 + sl]))
                em.wait("vector", "vector")
                em.mark("vector", nc.vector.tensor_scalar(
                    Pb[:, sl * 384:(sl + 1) * 384], Pb[:, sl * 384:(sl + 1) * 384],
                    mxd[:, 4 + sl:5 + sl], None, ALU.mult))
                em.wait("tensor", "vector")
                bankt = tbank[sl]
                papt = bankt.acquire(em)
                for jj in range(3):
                    tr = nc.tensor.transpose(
                        papt[:, jj * 128:(jj + 1) * 128],
                        Pb[:, sl * 384 + jj * 128: sl * 384 + (jj + 1) * 128], ident[:])
                tc = em.mark("tensor", tr)
                em.wait("vector", "tensor", tc)
                vcp = em.mark("vector", nc.vector.tensor_copy(
                    PTb[:, sl * 384:(sl + 1) * 384], papt[:, 0:384]))
                bankt.release("vector", vcp)
                em.wait("tensor", "vector", vcp)
                for jj in range(3):
                    mm = nc.tensor.matmul(
                        pv[:, qt * 128:(qt + 1) * 128],
                        vbuf[:, h * 384 + (qt + jj) * 64: h * 384 + (qt + jj + 1) * 64],
                        PTb[:, sl * 384 + jj * 128: sl * 384 + (jj + 1) * 128],
                        start=(jj == 0), stop=(jj == 2))
                em.mark("tensor", mm)
                if qt == 3:
                    em.wait("vector", "tensor")
                    vcp = em.mark("vector", nc.vector.tensor_copy(
                        atT[hh * 64:hh * 64 + 64, p_ * 512:(p_ + 1) * 512], pv[:]))
                    em.wait("tensor", "vector", vcp)
            for it in range(NIT + 1):
                if it < NIT:
                    a_sim(it)
                if it > 0:
                    a_rest(it - 1)
            em.wait("vector", "tensor")
            em.mark("vector", nc.vector.tensor_copy(
                khT[:].rearrange("p (g c) -> p g c", c=768)[:, :, 0:256],
                khT[:].rearrange("p (g c) -> p g c", c=768)[:, :, 512:768]))
            em.mark("vector", nc.vector.tensor_copy(
                vbuf[:].rearrange("p (h s d) -> p h s d", s=6, d=64)[:, :, 0:2],
                vbuf[:].rearrange("p (h s d) -> p h s d", s=6, d=64)[:, :, 4:6]))
            for ct in range(CT):
                sl = ct % 2
                slot = wslot[sl]
                em.wait("gpsimd", "tensor", wqtok[sl])
                dc = em.dma("gpsimd", slot[:, 0:CT * 128].rearrange("p (k c) -> p k c", c=128),
                       Wo[li].rearrange("(k p) m -> p k m", p=128)
                       [:, :, ct * 128:(ct + 1) * 128])
                em.dwait("tensor", "gpsimd", dc)
                bank = gbank[ct % 2]
                pap = bank.acquire(em)
                for k in range(CT):
                    mm = nc.tensor.matmul(pap, slot[:, k * 128:(k + 1) * 128],
                                          atT[:, k * 512:(k + 1) * 512],
                                          start=(k == 0), stop=(k == CT - 1))
                mmc = em.mark("tensor", mm)
                wqtok[sl] = mmc
                em.wait("sync", "vector")
                em.dwait("sync", "scalar")
                em.dma("sync", xt2[:, 0:CH], xi[ct * 128:(ct + 1) * 128, t0:t0 + CH])
                em.wait("vector", "tensor", mmc)
                em.dwait("vector", "sync")
                em.dwait("vector", "scalar")
                uu = u_sb[:, (ct % 2) * 512:(ct % 2 + 1) * 512]
                vcp = em.mark("vector", nc.vector.tensor_add(uu, pap, xt2[:, 0:CH]))
                bank.release("vector", vcp)
                em.wait("scalar", "vector", vcp)
                em.dma("scalar", xi[ct * 128:(ct + 1) * 128, t0:t0 + CH], uu)
    return


def _emit_d_ffn_half(nc, em, li, xi, W1, W2, gb_cols, b1_cols, b2_cols,
                     ones_bf, xcol, dbg_f0=None):
    ST = 1024
    em.barrier()
    ps = contextlib.ExitStack()
    P = lambda n: f"f{li}_{n}"
    with ps:
        xt2 = ps.enter_context(nc.sbuf_tensor(P("xt2"), [128, 2 * ST], F32))
        xs2 = ps.enter_context(nc.sbuf_tensor(P("xs2"), [128, CT * ST], BF16))
        xn2 = ps.enter_context(nc.sbuf_tensor(P("xn2"), [128, CT * ST], BF16))
        sqb = ps.enter_context(nc.sbuf_tensor(P("sqb"), [128, CT * 512], BF16))
        rstd = ps.enter_context(nc.sbuf_tensor(P("rstd"), [128, ST], F32))
        mrstd = ps.enter_context(nc.sbuf_tensor(P("mrstd"), [128, ST], F32))
        tmp = ps.enter_context(nc.sbuf_tensor(P("tmp"), [128, ST], F32))
        h2 = ps.enter_context(nc.sbuf_tensor(P("h2"), [128, 22 * ST], BF16))
        w1a = ps.enter_context(nc.sbuf_tensor(P("w1a"), [128, 2 * CT * 128], BF16))
        w1g = ps.enter_context(nc.sbuf_tensor(P("w1g"), [128, 2 * CT * 128], BF16))
        w2s = ps.enter_context(nc.sbuf_tensor(P("w2s"), [128, 2 * 22 * 128], BF16))
        gact = ps.enter_context(nc.sbuf_tensor(P("gact"), [128, 2 * 512], BF16))
        u_sb = ps.enter_context(nc.sbuf_tensor(P("u"), [128, 2 * 512], F32))
        pa = ps.enter_context(nc.psum_tensor(P("pa"), [128, 512], F32))
        pb = ps.enter_context(nc.psum_tensor(P("pb"), [128, 512], F32))
        pg = ps.enter_context(nc.psum_tensor(P("pg"), [128, 512], F32))
        pa2 = ps.enter_context(nc.psum_tensor(P("pa2"), [128, 512], F32))
        pg2 = ps.enter_context(nc.psum_tensor(P("pg2"), [128, 512], F32))
        pout = ps.enter_context(nc.psum_tensor(P("po"), [128, 512], F32))
        pout2 = ps.enter_context(nc.psum_tensor(P("po2"), [128, 512], F32))
        gbc2 = ps.enter_context(nc.sbuf_tensor(P("gbc2"), [128, 2 * CT], F32))
        b1c2 = ps.enter_context(nc.sbuf_tensor(P("b1c2"), [128, 2 * 22], F32))
        b2c2 = ps.enter_context(nc.sbuf_tensor(P("b2c2"), [128, CT], F32))

        em.dma("sync", gbc2[:].rearrange("p (a c) -> p a c", c=CT), gb_cols[li].transpose([1, 0, 2]))
        em.dma("sync", b1c2[:].rearrange("p (a c) -> p a c", c=22), b1_cols[li].transpose([1, 0, 2]))
        em.dma("sync", b2c2[:], b2_cols[li])
        w1slot = [(w1a[:, 0:CT * 128], w1g[:, 0:CT * 128]),
                  (w1a[:, CT * 128:2 * CT * 128], w1g[:, CT * 128:2 * CT * 128])]
        w2slot = [w2s[:, 0:22 * 128], w2s[:, 22 * 128:2 * 22 * 128]]
        w1tok = [0, 0]
        w2tok = [0, 0]
        obank = [Bank(pout[:]), Bank(pout2[:])]
        em.dwait("vector", "sync")
        em.dwait("scalar", "sync")

        for s in range(4):
            t0 = s * ST
            em.wait("sync", "tensor")
            em.wait("sync", "vector")
            em.wait("sync", "scalar")
            em.dwait("sync", "scalar")
            em.wait("vector", "tensor")
            for ct in range(CT):
                if ct >= 2:
                    em.wait("sync", "vector")
                em.dma("sync", xt2[:, (ct % 2) * ST:(ct % 2 + 1) * ST],
                       xi[ct * 128:(ct + 1) * 128, t0:t0 + ST])
                em.dwait("vector", "sync")
                xtile = xt2[:, (ct % 2) * ST:(ct % 2 + 1) * ST]
                dst = xs2[:, ct * ST:(ct + 1) * ST]
                if ct < 4:
                    vcp = nc.vector.tensor_copy(dst, xtile)
                else:
                    if s == 0:
                        nc.vector.memset(dst[:, 0:1], 0.0)
                    else:
                        nc.vector.tensor_copy(dst[:, 0:1], xcol[:, ct:ct + 1])
                    vcp = nc.vector.tensor_copy(dst[:, 1:ST], xtile[:, 0:ST - 1])
                em.mark("vector", vcp)
            for half in range(2):
                hc = slice(half * 512, (half + 1) * 512)
                emit_ln_stats_strip(
                    em, lambda ct: xs2[:, ct * ST + half * 512: ct * ST + (half + 1) * 512],
                    512, ones_bf[:], None, sqb, pa, pb,
                    rstd[:, hc], mrstd[:, hc], tmp[:, hc],
                    x_dtype_bf=True)
            for ct in range(CT):
                nc.vector.tensor_mul(tmp[:], xs2[:, ct * ST:(ct + 1) * ST], rstd[:])
                nc.vector.tensor_sub(tmp[:], tmp[:], mrstd[:])
                em.mark("vector", nc.vector.tensor_scalar(
                    xn2[:, ct * ST:(ct + 1) * ST], tmp[:],
                    gbc2[:, ct:ct + 1], gbc2[:, CT + ct:CT + ct + 1],
                    ALU.mult, ALU.add))
            if dbg_f0 and s == 0:
                em.barrier()
                em.dma("sync", dbg_f0[0], xn2[:])
                em.dma("sync", dbg_f0[1][:, 0:ST], rstd[:])
                em.dma("sync", dbg_f0[1][:, ST:2 * ST], mrstd[:])
                em.dma("sync", dbg_f0[2], xs2[:])
                em.dwait("sync", "sync")
                em.barrier()
            w1banks = [(pa, pg), (pa2, pg2)]
            w1ev = [[em.cnt["vector"], em.cnt["scalar"]],
                    [em.cnt["vector"], em.cnt["scalar"]]]
            for kk in range(22):
                rows = min(128, 2730 - kk * 128)
                sl = kk % 2
                sa, sg = w1slot[sl]
                em.wait("gpsimd", "tensor", w1tok[sl])
                em.dma("gpsimd", sa.rearrange("p (k c) -> p k c", c=128)[:, :, 0:rows],
                       W1[li].rearrange("(k p) m -> p k m", p=128)
                       [:, :, kk * 128:kk * 128 + rows])
                dc = em.dma("gpsimd", sg.rearrange("p (k c) -> p k c", c=128)[:, :, 0:rows],
                       W1[li].rearrange("(k p) m -> p k m", p=128)
                       [:, :, 2730 + kk * 128:2730 + kk * 128 + rows])
                em.dwait("tensor", "gpsimd", dc)
                for sub in range(2):
                    bidx = (kk * 2 + sub) % 2
                    bpa, bpg = w1banks[bidx]
                    em.wait("tensor", "vector", w1ev[bidx][0])
                    em.wait("tensor", "scalar", w1ev[bidx][1])
                    for k in range(CT):
                        mma = nc.tensor.matmul(
                            bpa[0:rows, :], sa[:, k * 128:k * 128 + rows],
                            xn2[:, k * ST + sub * 512: k * ST + (sub + 1) * 512],
                            start=(k == 0), stop=(k == CT - 1))
                    for k in range(CT):
                        mmg = nc.tensor.matmul(
                            bpg[0:rows, :], sg[:, k * 128:k * 128 + rows],
                            xn2[:, k * ST + sub * 512: k * ST + (sub + 1) * 512],
                            start=(k == 0), stop=(k == CT - 1))
                    mmc = em.mark("tensor", mmg)
                    w1tok[sl] = mmc
                    em.wait("scalar", "tensor", mmc)
                    em.wait("scalar", "vector", w1ev[bidx][0])
                    sc = em.mark("scalar", nc.scalar.activation(
                        gact[0:rows, bidx * 512:bidx * 512 + 512],
                        bpg[0:rows, :], AF.Gelu,
                        bias=b1c2[0:rows, 22 + kk:22 + kk + 1]))
                    em.wait("vector", "scalar", sc)
                    vv = em.mark("vector", nc.vector.scalar_tensor_tensor(
                        h2[0:rows, kk * ST + sub * 512: kk * ST + (sub + 1) * 512],
                        bpa[0:rows, :], b1c2[0:rows, kk:kk + 1],
                        gact[0:rows, bidx * 512:bidx * 512 + 512],
                        ALU.add, ALU.mult))
                    w1ev[bidx] = [vv, sc]
            for ct in range(CT):
                sl2 = ct % 2
                slot = w2slot[sl2]
                em.wait("gpsimd", "tensor", w2tok[sl2])
                em.dma("gpsimd", slot.rearrange("p (k c) -> p k c", c=128)[:, 0:21, :],
                       W2[li][0:2688, :].rearrange("(k p) m -> p k m", p=128)
                       [:, :, ct * 128:(ct + 1) * 128])
                dc = em.dma("gpsimd", slot.rearrange("p (k c) -> p k c", c=128)[0:42, 21:22, :],
                       W2[li][2688:2730, ct * 128:(ct + 1) * 128].unsqueeze(1))
                em.dwait("tensor", "gpsimd", dc)
                for sub in range(2):
                    bank = obank[sub]
                    pap = bank.acquire(em)
                    em.wait("tensor", "vector")
                    for kk in range(22):
                        rows = min(128, 2730 - kk * 128)
                        mm = nc.tensor.matmul(
                            pap, slot[0:rows, kk * 128:(kk + 1) * 128],
                            h2[0:rows, kk * ST + sub * 512: kk * ST + (sub + 1) * 512],
                            start=(kk == 0), stop=(kk == 21))
                    mmc = em.mark("tensor", mm)
                    w2tok[sl2] = mmc
                    em.wait("sync", "vector")
                    em.dwait("sync", "scalar")
                    em.dma("sync", xt2[:, sub * 512:(sub + 1) * 512],
                           xi[ct * 128:(ct + 1) * 128, t0 + sub * 512:t0 + (sub + 1) * 512])
                    em.wait("vector", "tensor", mmc)
                    em.dwait("vector", "sync")
                    em.dwait("vector", "scalar")
                    xtile = xt2[:, sub * 512:(sub + 1) * 512]
                    if sub == 1:
                        em.mark("vector", nc.vector.tensor_copy(
                            xcol[:, ct:ct + 1], xtile[:, 511:512]))
                    uu = u_sb[:, sub * 512:(sub + 1) * 512]
                    vcp = em.mark("vector", nc.vector.scalar_tensor_tensor(
                        uu, pap, b2c2[:, ct:ct + 1], xtile, ALU.add, ALU.add))
                    bank.release("vector", vcp)
                    em.wait("scalar", "vector", vcp)
                    em.dma("scalar", xi[ct * 128:(ct + 1) * 128,
                                        t0 + sub * 512:t0 + (sub + 1) * 512], uu)
    return


def run_l1_ffn_unfused(xT_list, I):
    qs_rows = np.stack([np.stack([np.tile(I["L1_ffn_q_scale"][i], 16),
                                  np.tile(I["L1_ffn_k_scale"][i], 16)])
                        for i in range(6)]).astype(np.float32)
    gb = np.stack([np.stack([_cols(I["L1_ffn_ln_g"][i]), _cols(I["L1_ffn_ln_b"][i])])
                   for i in range(6)]).astype(np.float32)
    b1c = np.zeros((6, 2, 128, 22), np.float32)
    for i in range(6):
        a = I["L1_ffn_b1"][i][:2730]
        g = I["L1_ffn_b1"][i][2730:]
        ap = np.zeros(22 * 128, np.float32); ap[:2730] = a
        gp = np.zeros(22 * 128, np.float32); gp[:2730] = g
        b1c[i, 0] = ap.reshape(22, 128).T
        b1c[i, 1] = gp.reshape(22, 128).T
    b2c = np.stack([_cols(I["L1_ffn_b2"][i]) for i in range(6)]).astype(np.float32)
    nc = build_d()
    ins = [dict(x=xT, Wqkv=I["L1_ffn_Wqkv"], Wo_d=I["L1_ffn_Wo"], W1=I["L1_ffn_W1"],
                W2=I["L1_ffn_W2"], qs_rows=qs_rows, gb_cols=gb, b1_cols=b1c,
                b2_cols=b2c) for xT in xT_list]
    r = run_bass_kernel_spmd(nc, ins, core_ids=list(range(NCORES)))
    return [res["xo"] for res in r.results]



def _t(a):
    return np.ascontiguousarray(np.asarray(a).T)


def _cols(v):
    return np.ascontiguousarray(np.asarray(v).reshape(CT, 128).T)


def run_l0_ffn_unfused(xT_list, I):
    nc1 = build_b1()
    r1 = run_bass_kernel_spmd(nc1, [{"x": xT} for xT in xT_list],
                              core_ids=list(range(NCORES)))
    xn_list = [r["xn"] for r in r1.results]

    bcols = np.zeros((128, 4 * CT), np.float32)
    bcols[:, 0:CT] = _cols(I["L0_ffn_bf"])
    bcols[:, CT:2 * CT] = _cols(I["L0_ffn_bi"])
    bcols[:, 2 * CT:3 * CT] = _cols(I["L0_ffn_bc"])
    bcols[:, 3 * CT:4 * CT] = _cols(I["L0_ffn_in_c"])
    wmap = {
        "wfT": _t(I["L0_ffn_wf"]), "ufT": _t(I["L0_ffn_uf"]),
        "wiT": _t(I["L0_ffn_wi"]), "uiT": _t(I["L0_ffn_ui"]),
        "wcT": _t(I["L0_ffn_wc"]), "ucT": _t(I["L0_ffn_uc"]),
        "uoT": _t(I["L0_ffn_uo"]), "bcols": bcols,
    }
    nc2 = build_b2()
    in2 = []
    for s in range(NCORES):
        xg = np.stack([xn_list[b][:, s * TS:(s + 1) * TS] for b in range(NCORES)])
        in2.append(dict(xg=np.ascontiguousarray(xg), **wmap))
    r2 = run_bass_kernel_spmd(nc2, in2, core_ids=list(range(NCORES)))
    UT = np.concatenate([r["UT"] for r in r2.results], axis=1)

    nc3 = build_b3()
    in3 = [dict(x=xT_list[b], xn=xn_list[b], UT=UT, woT=_t(I["L0_ffn_wo"]),
                bo_col=_cols(I["L0_ffn_bo"])) for b in range(NCORES)]
    r3 = run_bass_kernel_spmd(nc3, in3, core_ids=list(range(NCORES)))
    return [r["xo"] for r in r3.results]


def build_fused():
    nc = bass.Bass("TRN2", target_bir_lowering=False, debug=False)
    em = EM(nc)
    f32in = lambda n, s: nc.dram_tensor(n, s, F32, kind="ExternalInput").ap()
    x_in = f32in("x", [D, T])
    A_IO = {n: f32in(n, s) for n, s in (
        ("W_disp", [D, 256]), ("W_val", [D, D]), ("W_cproj", [D, D]),
        ("relT", [16, 64]), ("W_strain", [16, 16]), ("W_pos", [16, 16]),
        ("W_bond", [16, 1]), ("W_dmg", [16, 16]), ("W_dmg_out", [16, 1]),
        ("bdmg_rep", [128, 1]), ("bdo_rep", [128, 1]))}
    w_names = ["wfT", "ufT", "wiT", "uiT", "wcT", "ucT", "uoT"]
    B_wd = {n: f32in(n, [D, D]) for n in w_names}
    b_bcols = f32in("b_bcols", [128, 4 * CT])
    woT = f32in("woT", [D, D])
    b_bo_col = f32in("b_bo_col", [128, CT])
    C_IO = {n: f32in(n, s) for n, s in (
        ("Wq", [D, D]), ("Wk", [D, D]), ("Wv", [D, D]), ("Wr", [D, 512]),
        ("Wo", [D, D]), ("bcols", [128, 36]))}
    D_IO = {n: f32in(n, s) for n, s in (
        ("Wqkv", [6, D, 3072]), ("Wo_d", [6, D, D]), ("W1", [6, D, 5460]),
        ("W2", [6, 2730, D]), ("qs_rows", [6, 2, D]), ("gb_cols", [6, 2, 128, CT]),
        ("b1_cols", [6, 2, 128, 22]), ("b2_cols", [6, 128, CT]))}
    x_out = nc.dram_tensor("xo", [D, T], F32, kind="ExternalOutput").ap()

    x1 = nc.dram_tensor("f_x1", [D, T], F32).ap()
    x2 = nc.dram_tensor("f_x2", [D, T], F32).ap()
    x3 = nc.dram_tensor("f_x3", [D, T], F32).ap()
    a2a_in = nc.dram_tensor("f_a2ai", [NCORES, D, TS], BF16)
    a2a_out = nc.dram_tensor("f_a2ao", [NCORES, D, TS], BF16)
    ag_in = nc.dram_tensor("f_agi", [D, TS], F32)
    ag_out = nc.dram_tensor("f_ago", [NCORES * D, TS], F32)
    csem = nc.alloc_semaphore("f_csem")

    emit_const_pool(nc, em, None, [EPS, 1e-24, -30.0, -20.0, 1e-30], alloc=nc)
    emit_a(nc, em, x_in, x1, A_IO)
    emit_b1(nc, em, x1, None,
            xn_fn=lambda ct, s: a2a_in.ap()[s, ct * 128:(ct + 1) * 128, :])
    em.dwait("gpsimd", "scalar")
    with allow_ici_all_to_all():
        nc.gpsimd.collective_compute(
            "AllToAll", ALU.bypass, replica_groups=[list(range(NCORES))],
            ins=[a2a_in.ap().opt()], outs=[a2a_out.ap().opt()]).then_inc(csem)
    for e in ("tensor", "vector", "scalar", "sync", "gpsimd"):
        em.eng(e).wait_ge(csem, 1)
    emit_b2(nc, em, a2a_out.ap(), B_wd, b_bcols, ag_in.ap())
    em.dwait("gpsimd", "scalar")
    nc.gpsimd.collective_compute(
        "AllGather", ALU.bypass, replica_groups=[list(range(NCORES))],
        ins=[ag_in.ap().opt()], outs=[ag_out.ap().opt()]).then_inc(csem)
    for e in ("tensor", "vector", "scalar", "sync", "gpsimd"):
        em.eng(e).wait_ge(csem, 2)
    emit_b3(nc, em, x1, None, None, woT, b_bo_col, x2,
            xn_fn=lambda ct, s: a2a_in.ap()[s, ct * 128:(ct + 1) * 128, :],
            ut_fn=lambda ct, s: ag_out.ap()[s * CT * 128 + ct * 128:
                                            s * CT * 128 + (ct + 1) * 128, :])
    emit_c(nc, em, x2, x3, C_IO)
    emit_d(nc, em, x3, x_out, D_IO)
    return nc


def _fused_inputs(I, b):
    x = I["L0_attn_x"]
    m = dict(x=np.ascontiguousarray(x[b].T))
    for n in ("W_disp", "W_val", "W_cproj", "W_strain", "W_pos", "W_bond",
              "W_dmg", "W_dmg_out"):
        m[n] = I[f"L0_attn_{n}"]
    m["relT"] = np.ascontiguousarray(I["L0_attn_rel_pos_emb"].T)
    m["bdmg_rep"] = np.ascontiguousarray(np.tile(I["L0_attn_b_dmg"], 8)[:, None]).astype(np.float32)
    m["bdo_rep"] = np.ascontiguousarray(np.tile(I["L0_attn_b_dmg_out"], 128)[:, None]).astype(np.float32)
    for n, s in (("wfT", "wf"), ("ufT", "uf"), ("wiT", "wi"), ("uiT", "ui"),
                 ("wcT", "wc"), ("ucT", "uc"), ("uoT", "uo"), ("woT", "wo")):
        m[n] = _t(I[f"L0_ffn_{s}"])
    bcols = np.zeros((128, 4 * CT), np.float32)
    bcols[:, 0:CT] = _cols(I["L0_ffn_bf"]); bcols[:, CT:2 * CT] = _cols(I["L0_ffn_bi"])
    bcols[:, 2 * CT:3 * CT] = _cols(I["L0_ffn_bc"]); bcols[:, 3 * CT:] = _cols(I["L0_ffn_in_c"])
    m["b_bcols"] = bcols
    m["b_bo_col"] = _cols(I["L0_ffn_bo"])
    for n in ("Wq", "Wk", "Wv", "Wr", "Wo"):
        m[n] = I[f"L1_attn_{n}"]
    cb = np.zeros((128, 36), np.float32)
    cb[:, 0:8] = _cols(I["L1_attn_bq"]); cb[:, 8:16] = _cols(I["L1_attn_bk"])
    cb[:, 16:24] = _cols(I["L1_attn_bv"]); cb[:, 24:32] = _cols(I["L1_attn_bo"])
    cb[:, 32:36] = np.ascontiguousarray(I["L1_attn_br"].reshape(4, 128).T)
    m["bcols"] = cb
    m["Wqkv"] = I["L1_ffn_Wqkv"]; m["Wo_d"] = I["L1_ffn_Wo"]
    m["W1"] = I["L1_ffn_W1"]; m["W2"] = I["L1_ffn_W2"]
    m["qs_rows"] = np.stack([np.stack([np.tile(I["L1_ffn_q_scale"][i], 16),
                                       np.tile(I["L1_ffn_k_scale"][i], 16)])
                             for i in range(6)]).astype(np.float32)
    m["gb_cols"] = np.stack([np.stack([_cols(I["L1_ffn_ln_g"][i]),
                                       _cols(I["L1_ffn_ln_b"][i])])
                             for i in range(6)]).astype(np.float32)
    b1c = np.zeros((6, 2, 128, 22), np.float32)
    for i in range(6):
        ap_ = np.zeros(22 * 128, np.float32); ap_[:2730] = I["L1_ffn_b1"][i][:2730]
        gp_ = np.zeros(22 * 128, np.float32); gp_[:2730] = I["L1_ffn_b1"][i][2730:]
        b1c[i, 0] = ap_.reshape(22, 128).T; b1c[i, 1] = gp_.reshape(22, 128).T
    m["b1_cols"] = b1c
    m["b2_cols"] = np.stack([_cols(I["L1_ffn_b2"][i]) for i in range(6)]).astype(np.float32)
    return m


def kernel(**inputs):
    nc = build_fused()
    in_maps = [_fused_inputs(inputs, b) for b in range(NCORES)]
    r = run_bass_kernel_spmd(nc, in_maps, core_ids=list(range(NCORES)))
    return np.stack([res["xo"].T for res in r.results]).astype(np.float32)


def kernel_unfused(**inputs):

    x = inputs["L0_attn_x"]
    xT_list = [np.ascontiguousarray(x[b].T) for b in range(NCORES)]
    xT_list = run_l0_attn_unfused(xT_list, inputs)
    xT_list = run_l0_ffn_unfused(xT_list, inputs)
    xT_list = run_l1_attn_unfused(xT_list, inputs)
    xT_list = run_l1_ffn_unfused(xT_list, inputs)
    out = np.stack([xT.T for xT in xT_list]).astype(np.float32)
    return out


# version 11 (final)
# speedup vs baseline: 1.0925x; 1.0925x over previous
import os
os.environ.setdefault("TRN_ALLOW_ICI_ALL_TO_ALL", "1")
import contextlib
import numpy as np
import ml_dtypes
import concourse.bass as bass
import concourse.mybir as mybir
from concourse.bass_utils import run_bass_kernel_spmd
from concourse.aot_env import allow_ici_all_to_all

F32 = mybir.dt.float32
BF16 = mybir.dt.bfloat16
AF = mybir.ActivationFunctionType
ALU = mybir.AluOpType
AX = mybir.AxisListType

NCORES = 8
T = 4096
D = 1024
CT = D // 128
TS = T // NCORES
EPS = 1e-5

BF = ml_dtypes.bfloat16



_UNIQ = [0]


def _uname(s):
    _UNIQ[0] += 1
    return f"{s}_{_UNIQ[0]}"


class EM:

    ENG = ("tensor", "vector", "scalar", "gpsimd", "sync")

    def __init__(self, nc):
        self.nc = nc
        self.sem = {}
        self.cnt = {}
        self.ctx = []
        for e in self.ENG:
            h = nc.alloc_semaphore(f"prog_{e}")
            self.sem[e] = h
            self.cnt[e] = 0
        self.dsem = {}
        self.dcnt = {}
        for q in ("sync", "gpsimd", "scalar", "vector"):
            self.dsem[q] = nc.alloc_semaphore(f"dma_{q}")
            self.dcnt[q] = 0

    def eng(self, name):
        return getattr(self.nc, name)

    def mark(self, engine, inst):
        inst.then_inc(self.sem[engine], 1)
        self.cnt[engine] += 1
        return self.cnt[engine]

    def wait(self, consumer, producer, cnt=None):
        c = self.cnt[producer] if cnt is None else cnt
        if c > 0:
            self.eng(consumer).wait_ge(self.sem[producer], c)

    def dma(self, q, dst, src, **kw):
        inst = self.eng(q).dma_start(dst, src, **kw)
        inst.then_inc(self.dsem[q], 16)
        self.dcnt[q] += 1
        return self.dcnt[q]

    def dwait(self, consumer, q, cnt=None):
        c = self.dcnt[q] if cnt is None else cnt
        if c > 0:
            self.eng(consumer).wait_ge(self.dsem[q], 16 * c)

    def barrier(self):
        self.nc.all_engine_barrier()

    def syncall(self, engines=("tensor", "vector", "scalar", "gpsimd", "sync")):
        for c in engines:
            for p in self.ENG:
                if p != c:
                    self.wait(c, p)
            for q in self.dsem:
                self.dwait(c, q)


def emit_const_pool(nc, em, cpool_sb, values, alloc=None):
    if cpool_sb is None:
        cpool_sb = nc.alloc_sbuf_tensor("g_cpool", [128, max(1, len(values))], F32)
    arr = np.zeros((128, max(1, len(values))), np.float32)
    for i, v in enumerate(values):
        arr[:, i] = v
    d = nc.inline_tensor(arr, _uname("const_pool"))
    em.dma("sync", cpool_sb[:, :arr.shape[1]], d.ap())
    for i, v in enumerate(values):
        nc.const_aps.register(F32, float(v), cpool_sb[:, i:i + 1])


class Bank:
    def __init__(self, ap):
        self.ap = ap
        self.tok = None

    def acquire(self, em, user="tensor"):
        if self.tok is not None:
            em.wait(user, self.tok[0], self.tok[1])
            self.tok = None
        return self.ap

    def release(self, engine, cnt):
        self.tok = (engine, cnt)



def emit_ln_stats_strip(em, x_tile_fn, ncols, ones_bf, ones_f32, sq_scr,
                        pa, pb, rstd, mrstd, tmp, x_dtype_bf=False,
                        extra_producers=()):
    nc = em.nc
    em.wait("tensor", "vector")
    em.wait("tensor", "scalar")
    em.wait("scalar", "tensor")
    em.wait("scalar", "vector")
    for p in extra_producers:
        em.wait("tensor", p)
        em.wait("scalar", p)
    s_base = em.cnt["scalar"]
    ones = ones_bf if x_dtype_bf else ones_f32
    for ct in range(CT):
        nc.tensor.matmul(pa[:, :ncols], ones, x_tile_fn(ct),
                         start=(ct == 0), stop=(ct == CT - 1))
    for ct in range(CT):
        em.mark("scalar", nc.scalar.activation(
            sq_scr[:, ct * ncols:(ct + 1) * ncols], x_tile_fn(ct), AF.Square))
    for ct in range(CT):
        em.wait("tensor", "scalar", s_base + ct + 1)
        mm = nc.tensor.matmul(pb[:, :ncols], ones_bf,
                              sq_scr[:, ct * ncols:(ct + 1) * ncols],
                              start=(ct == 0), stop=(ct == CT - 1))
    em.mark("tensor", mm)
    mm_c = em.cnt["tensor"]
    em.wait("vector", "tensor", mm_c)
    nc.vector.tensor_scalar(mrstd[:, :ncols], pa[:, :ncols], 1.0 / D, None, ALU.mult)
    em.mark("vector", nc.vector.tensor_scalar(rstd[:, :ncols], pb[:, :ncols], 1.0 / D, None, ALU.mult))
    em.wait("vector", "vector")
    nc.vector.scalar_tensor_tensor(
        tmp[:, :ncols], mrstd[:, :ncols], 1.0, mrstd[:, :ncols], ALU.mult, ALU.mult)
    v1 = em.mark("vector", nc.vector.tensor_sub(
        rstd[:, :ncols], tmp[:, :ncols], rstd[:, :ncols]))
    em.wait("scalar", "vector", v1)
    em.mark("scalar", nc.scalar.activation(rstd[:, :ncols], rstd[:, :ncols],
                                           AF.Ln, bias=EPS, scale=-1.0))
    em.wait("scalar", "scalar")
    s1 = em.mark("scalar", nc.scalar.activation(
        rstd[:, :ncols], rstd[:, :ncols], AF.Exp, bias=0.0, scale=-0.5))
    em.wait("vector", "scalar", s1)
    v2 = em.mark("vector", nc.vector.tensor_mul(mrstd[:, :ncols], mrstd[:, :ncols],
                                                rstd[:, :ncols]))
    return v2


def emit_ln_apply_tile(em, out_ap, x_ap, rstd_ap, mrstd_ap, tmp_ap):
    nc = em.nc
    nc.vector.tensor_mul(tmp_ap, x_ap, rstd_ap)
    return em.mark("vector", nc.vector.tensor_sub(out_ap, tmp_ap, mrstd_ap))



def emit_gemm_acc(em, psum_ap, lhsT_list, rhs_list, start=True, stop=True):
    nc = em.nc
    n = len(lhsT_list)
    mm = None
    for k in range(n):
        mm = nc.tensor.matmul(psum_ap, lhsT_list[k], rhs_list[k],
                              start=(start and k == 0), stop=(stop and k == n - 1))
    return em.mark("tensor", mm)


def load_weight_cast(em, sb, dram_ap):
    return em.dma("gpsimd", sb, dram_ap)



def build_b1():
    nc = bass.Bass("TRN2", target_bir_lowering=False, debug=False)
    x_in = nc.dram_tensor("x", [D, T], F32, kind="ExternalInput").ap()
    xn_out = nc.dram_tensor("xn", [D, T], BF16, kind="ExternalOutput").ap()
    em = EM(nc)
    emit_const_pool(nc, em, None, [EPS], alloc=nc)
    emit_b1(nc, em, x_in, xn_out)
    return nc


def emit_b1(nc, em, x_in, xn_out, xn_fn=None):
    NS = 8
    ncols = T // NS
    with (
        nc.sbuf_tensor("b1_X", [128, CT * T], F32) as X,
        nc.sbuf_tensor("b1_xnb", [128, CT * ncols], BF16) as xnb,
        nc.sbuf_tensor("b1_sqb", [128, CT * ncols], BF16) as sqb,
        nc.sbuf_tensor("b1_rstd", [128, ncols], F32) as rstd,
        nc.sbuf_tensor("b1_mrstd", [128, ncols], F32) as mrstd,
        nc.sbuf_tensor("b1_tmp", [128, ncols], F32) as tmp,
        nc.sbuf_tensor("b1_ones_bf", [128, 128], BF16) as ones_bf,
        nc.sbuf_tensor("b1_ones_f", [128, 128], F32) as ones_f,
        nc.psum_tensor("b1_pa", [128, ncols], F32) as pa,
        nc.psum_tensor("b1_pb", [128, ncols], F32) as pb,
    ):
        ones_d = nc.inline_tensor(np.ones((128, 128), BF), _uname("ones_bf_c"))
        ones_fd = nc.inline_tensor(np.ones((128, 128), np.float32), _uname("ones_f_c"))
        em.dma("sync", ones_bf[:], ones_d.ap())
        em.dma("sync", ones_f[:], ones_fd.ap())
        for ct in range(CT):
            em.dma("sync", X[:, ct * T:(ct + 1) * T], x_in[ct * 128:(ct + 1) * 128, :])
        em.dwait("tensor", "sync")
        em.dwait("scalar", "sync")
        em.dwait("vector", "sync")
        for s in range(NS):
            col = slice(s * ncols, (s + 1) * ncols)
            vr = emit_ln_stats_strip(
                em, lambda ct: X[:, ct * T + s * ncols: ct * T + (s + 1) * ncols],
                ncols, ones_bf[:], ones_f[:], sqb, pa, pb, rstd, mrstd, tmp)
            em.dwait("vector", "scalar")
            for ct in range(CT):
                xt = X[:, ct * T + s * ncols: ct * T + (s + 1) * ncols]
                emit_ln_apply_tile(em, xnb[:, ct * ncols:(ct + 1) * ncols], xt,
                                   rstd[:, :ncols], mrstd[:, :ncols], tmp[:])
            vv = em.cnt["vector"]
            em.wait("scalar", "vector", vv)
            for ct in range(CT):
                em.dma("scalar",
                       xn_fn(ct, s) if xn_fn else xn_out[ct * 128:(ct + 1) * 128, col],
                       xnb[:, ct * ncols:(ct + 1) * ncols])
        em.dwait("scalar", "scalar")
        em.barrier()
    return


def build_b2():
    nc = bass.Bass("TRN2", target_bir_lowering=False, debug=False)
    xg = nc.dram_tensor("xg", [NCORES, D, TS], BF16, kind="ExternalInput").ap()
    w_names = ["wfT", "ufT", "wiT", "uiT", "wcT", "ucT", "uoT"]
    wd = {n: nc.dram_tensor(n, [D, D], F32, kind="ExternalInput").ap() for n in w_names}
    bcols = nc.dram_tensor("bcols", [128, 4 * CT], F32, kind="ExternalInput").ap()
    ut_out = nc.dram_tensor("UT", [D, TS], F32, kind="ExternalOutput").ap()
    em = EM(nc)
    emit_b2(nc, em, xg, wd, bcols, ut_out)
    return nc


def emit_b2(nc, em, xg, wd, bcols, ut_out):
    w_names = ["wfT", "ufT", "wiT", "uiT", "wcT", "ucT", "uoT"]
    with (
        nc.sbuf_tensor("b2_W", [128, 7 * CT * D // 128 * 128], BF16) as Wbuf,
        nc.sbuf_tensor("b2_xgs", [128, CT * TS], BF16) as xgs,
        nc.sbuf_tensor("b2_c", [128, CT * TS], F32) as c_sb,
        nc.sbuf_tensor("b2_cbf", [128, CT * TS], BF16) as cbf,
        nc.sbuf_tensor("b2_bc", [128, 4 * CT], F32) as bc_sb,
        nc.sbuf_tensor("b2_f", [128, TS], F32) as f_sb,
        nc.sbuf_tensor("b2_i", [128, TS], F32) as i_sb,
        nc.sbuf_tensor("b2_g", [128, TS], F32) as g_sb,
        nc.sbuf_tensor("b2_u", [128, TS], F32) as u_sb,
        nc.psum_tensor("b2_pf", [128, TS], F32) as pf,
        nc.psum_tensor("b2_pi", [128, TS], F32) as pi,
        nc.psum_tensor("b2_pg", [128, TS], F32) as pg,
    ):
        def wtile(w, ki, mi):
            idx = (w * CT + ki) * CT + mi
            return Wbuf[:, idx * 128:(idx + 1) * 128]
        for w, nme in enumerate(w_names):
            for ki in range(CT):
                em.dma("gpsimd", Wbuf[:, (w * CT + ki) * CT * 128:(w * CT + ki + 1) * CT * 128],
                       wd[nme][ki * 128:(ki + 1) * 128, :])
        em.dma("sync", bc_sb[:], bcols)
        em.dwait("vector", "sync")
        ini = None
        for ct in range(CT):
            ini = nc.vector.tensor_scalar(c_sb[:, ct * TS:(ct + 1) * TS],
                                          bc_sb[:, 3 * CT + ct:3 * CT + ct + 1].broadcast_to([128, TS]),
                                          1.0, None, ALU.mult)
        em.mark("vector", ini)
        em.dwait("tensor", "gpsimd")
        em.dwait("scalar", "sync")
        for step in range(NCORES):
            em.wait("vector", "tensor")
            cp = None
            for ct in range(CT):
                cp = nc.vector.tensor_copy(cbf[:, ct * TS:(ct + 1) * TS], c_sb[:, ct * TS:(ct + 1) * TS])
            vcast = em.mark("vector", cp)
            em.wait("sync", "tensor")
            for ct in range(CT):
                em.dma("sync", xgs[:, ct * TS:(ct + 1) * TS],
                       xg[step, ct * 128:(ct + 1) * 128, :])
            em.dwait("tensor", "sync")
            em.wait("tensor", "vector", vcast)
            for ct in range(CT):
                for (wi_, ui_, bank) in ((0, 1, pf), (2, 3, pi), (4, 5, pg)):
                    for k in range(CT):
                        nc.tensor.matmul(bank[:], wtile(wi_, k, ct),
                                         xgs[:, k * TS:(k + 1) * TS], start=(k == 0), stop=False)
                    for k in range(CT):
                        mm = nc.tensor.matmul(bank[:], wtile(ui_, k, ct),
                                              cbf[:, k * TS:(k + 1) * TS], start=False, stop=(k == CT - 1))
                mmc = em.mark("tensor", mm)
                em.wait("scalar", "tensor", mmc)
                nc.scalar.activation(f_sb[:], pf[:], AF.Sigmoid, bias=bc_sb[:, ct:ct + 1])
                nc.scalar.activation(i_sb[:], pi[:], AF.Sigmoid, bias=bc_sb[:, CT + ct:CT + ct + 1])
                sc = em.mark("scalar", nc.scalar.activation(
                    g_sb[:], pg[:], AF.Tanh, bias=bc_sb[:, 2 * CT + ct:2 * CT + ct + 1]))
                em.wait("vector", "scalar", sc)
                cs = c_sb[:, ct * TS:(ct + 1) * TS]
                nc.vector.tensor_mul(u_sb[:], f_sb[:], cs)
                nc.vector.tensor_mul(g_sb[:], i_sb[:], g_sb[:])
                vc = em.mark("vector", nc.vector.tensor_add(cs, u_sb[:], g_sb[:]))
                em.wait("tensor", "scalar", sc)
        em.wait("vector", "tensor")
        cp = None
        for ct in range(CT):
            cp = nc.vector.tensor_copy(cbf[:, ct * TS:(ct + 1) * TS], c_sb[:, ct * TS:(ct + 1) * TS])
        vcast = em.mark("vector", cp)
        em.wait("tensor", "vector", vcast)
        for ct in range(CT):
            for k in range(CT):
                mm = nc.tensor.matmul(pf[:], wtile(6, k, ct), cbf[:, k * TS:(k + 1) * TS],
                                      start=(k == 0), stop=(k == CT - 1))
            mmc = em.mark("tensor", mm)
            em.wait("vector", "tensor", mmc)
            vc = em.mark("vector", nc.vector.tensor_copy(u_sb[:], pf[:]))
            em.wait("scalar", "vector", vc)
            em.dma("scalar", ut_out[ct * 128:(ct + 1) * 128, :], u_sb[:])
            em.dwait("tensor", "scalar")
            em.dwait("vector", "scalar")
        em.dwait("scalar", "scalar")
        em.barrier()
    return


def build_b3():
    nc = bass.Bass("TRN2", target_bir_lowering=False, debug=False)
    x_in = nc.dram_tensor("x", [D, T], F32, kind="ExternalInput").ap()
    xn_in = nc.dram_tensor("xn", [D, T], BF16, kind="ExternalInput").ap()
    ut_in = nc.dram_tensor("UT", [D, T], F32, kind="ExternalInput").ap()
    woT = nc.dram_tensor("woT", [D, D], F32, kind="ExternalInput").ap()
    bo_col = nc.dram_tensor("bo_col", [128, CT], F32, kind="ExternalInput").ap()
    x_out = nc.dram_tensor("xo", [D, T], F32, kind="ExternalOutput").ap()
    em = EM(nc)
    emit_const_pool(nc, em, None, [EPS], alloc=nc)
    emit_b3(nc, em, x_in, xn_in, ut_in, woT, bo_col, x_out)
    return nc


def emit_b3(nc, em, x_in, xn_in, ut_in, woT, bo_col, x_out,
            xn_fn=None, ut_fn=None):
    NS = 8
    ncols = T // NS
    with (
        nc.sbuf_tensor("b3_W", [128, CT * CT * 128], BF16) as Wbuf,
        nc.sbuf_tensor("b3_xnb", [128, 2 * CT * ncols], BF16) as xnb,
        nc.sbuf_tensor("b3_ut", [128, ncols], F32) as ut_sb,
        nc.sbuf_tensor("b3_h", [128, CT * ncols], BF16) as h_sb,
        nc.sbuf_tensor("b3_bo", [128, CT], F32) as bo_sb,
        nc.sbuf_tensor("b3_u", [128, ncols], F32) as u_sb,
        nc.sbuf_tensor("b3_sqb", [128, CT * ncols], BF16) as sqb,
        nc.sbuf_tensor("b3_rstd", [128, ncols], F32) as rstd,
        nc.sbuf_tensor("b3_mrstd", [128, ncols], F32) as mrstd,
        nc.sbuf_tensor("b3_tmp", [128, ncols], F32) as tmp,
        nc.sbuf_tensor("b3_xt", [128, ncols], F32) as xt_sb,
        nc.sbuf_tensor("b3_ones_bf", [128, 128], BF16) as ones_bf,
        nc.psum_tensor("b3_ph", [128, ncols], F32) as ph,
        nc.psum_tensor("b3_pa", [128, ncols], F32) as pa,
        nc.psum_tensor("b3_pb", [128, ncols], F32) as pb,
    ):
        ones_d = nc.inline_tensor(np.ones((128, 128), BF), _uname("ones_bf_c"))
        em.dma("sync", ones_bf[:], ones_d.ap())
        em.dma("sync", bo_sb[:], bo_col)
        for ki in range(CT):
            em.dma("gpsimd", Wbuf[:, ki * CT * 128:(ki + 1) * CT * 128],
                   woT[ki * 128:(ki + 1) * 128, :])
        em.dwait("tensor", "gpsimd")
        em.dwait("scalar", "sync")
        em.dwait("vector", "sync")

        def wtile(ki, mi):
            return Wbuf[:, (ki * CT + mi) * 128:(ki * CT + mi + 1) * 128]

        for s in range(NS):
            col = slice(s * ncols, (s + 1) * ncols)
            xb = xnb[:, (s % 2) * CT * ncols:((s % 2) + 1) * CT * ncols]
            em.wait("sync", "tensor")
            for ct in range(CT):
                em.dma("sync", xb[:, ct * ncols:(ct + 1) * ncols],
                       xn_fn(ct, s) if xn_fn else xn_in[ct * 128:(ct + 1) * 128, col])
            em.dwait("tensor", "sync")
            for ct in range(CT):
                em.dma("sync", ut_sb[:],
                       ut_fn(ct, s) if ut_fn else ut_in[ct * 128:(ct + 1) * 128, col])
                for k in range(CT):
                    mm = nc.tensor.matmul(ph[:], wtile(k, ct), xb[:, k * ncols:(k + 1) * ncols],
                                          start=(k == 0), stop=(k == CT - 1))
                mmc = em.mark("tensor", mm)
                em.wait("vector", "tensor", mmc)
                em.dwait("vector", "sync")
                nc.vector.scalar_tensor_tensor(u_sb[:], ut_sb[:], 1.0, ph[:], ALU.mult, ALU.add)
                vc = em.mark("vector", nc.vector.tensor_scalar(
                    u_sb[:], u_sb[:], bo_sb[:, ct:ct + 1], None, ALU.add))
                em.wait("scalar", "vector", vc)
                sc = em.mark("scalar", nc.scalar.activation(
                    h_sb[:, ct * ncols:(ct + 1) * ncols], u_sb[:], AF.Tanh))
                em.wait("tensor", "scalar", sc)
                em.wait("sync", "vector", vc)
            vr = emit_ln_stats_strip(
                em, lambda ct: h_sb[:, ct * ncols:(ct + 1) * ncols],
                ncols, ones_bf[:], None, sqb, pa, pb, rstd, mrstd, tmp,
                x_dtype_bf=True, extra_producers=())
            for ct in range(CT):
                em.dma("sync", xt_sb[:], x_in[ct * 128:(ct + 1) * 128, col])
                nc.vector.tensor_mul(tmp[:], h_sb[:, ct * ncols:(ct + 1) * ncols], rstd[:, :ncols])
                em.dwait("vector", "sync")
                nc.vector.tensor_sub(tmp[:], tmp[:], mrstd[:, :ncols])
                vc = em.mark("vector", nc.vector.tensor_add(xt_sb[:], xt_sb[:], tmp[:]))
                em.wait("scalar", "vector", vc)
                em.dma("scalar", x_out[ct * 128:(ct + 1) * 128, col], xt_sb[:])
                em.dwait("vector", "scalar")
                em.dwait("sync", "scalar")
        em.dwait("scalar", "scalar")
        em.barrier()
    return



def _l0_mask():
    m = np.zeros((128, 1024), np.float32)
    t = np.arange(128)
    for jb in range(8):
        for ji in range(8):
            j = jb * 8 + ji
            bad = j < 63 - t
            for h in range(16):
                m[bad, jb * 128 + ji * 16 + h] = -1e30
    return m.astype(BF)


def build_a(dbg=False):
    nc = bass.Bass("TRN2", target_bir_lowering=False, debug=False)
    x_in = nc.dram_tensor("x", [D, T], F32, kind="ExternalInput").ap()
    names = ("W_disp", "W_val", "W_cproj", "relT", "W_strain", "W_pos", "W_bond",
             "W_dmg", "W_dmg_out", "bdmg_rep", "bdo_rep")
    shapes = ([D, 256], [D, D], [D, D], [16, 64], [16, 16], [16, 16], [16, 1],
              [16, 16], [16, 1], [128, 1], [128, 1])
    IO = {n: nc.dram_tensor(n, s, F32, kind="ExternalInput").ap()
          for n, s in zip(names, shapes)}
    x_out = nc.dram_tensor("xo", [D, T], F32, kind="ExternalOutput").ap()
    em = EM(nc)
    emit_const_pool(nc, em, None, [EPS], alloc=nc)
    emit_a(nc, em, x_in, x_out, IO, dbg=dbg)
    return nc


def emit_a(nc, em, x_in, x_out, IO, dbg=False):
    dd = {}
    W_disp = IO["W_disp"]; W_val = IO["W_val"]; W_cproj = IO["W_cproj"]
    relT = IO["relT"]; W_strain = IO["W_strain"]; W_pos = IO["W_pos"]
    W_bond = IO["W_bond"]; W_dmg = IO["W_dmg"]; W_dmg_out = IO["W_dmg_out"]
    bdmg_rep = IO["bdmg_rep"]; bdo_rep = IO["bdo_rep"]

    if dbg:
        for nme, shp, dt_ in (("d_disp", [128, 2 * T], BF16), ("d_A", [128, 2 * 4160], BF16),
                              ("d_wtm", [128, 4096], BF16), ("d_whd", [128, 4096], BF16),
                              ("d_Pp", [128, 1024], BF16), ("d_atT", [128, CT * 512], BF16),
                              ("d_prel", [128, 64], F32)):
            dd[nme] = nc.dram_tensor(nme, shp, dt_, kind="ExternalOutput").ap()
    vdr = nc.dram_tensor("a_vdr", [D, T], BF16, kind="ExternalOutput" if dbg else "Internal").ap()
    npan = 8 if dbg else 1
    pan_full = nc.dram_tensor("a_pan", [npan * 16 * 4 * 128 * 256], BF16,
                              kind="ExternalOutput" if dbg else "Internal").ap()
    if dbg:
        whd_all = nc.dram_tensor("whd_all", [8, 128, 4096], BF16,
                                 kind="ExternalOutput").ap()
        atT_all = nc.dram_tensor("atT_all", [8, 128, 4096], BF16,
                                 kind="ExternalOutput").ap()
        bos_all = nc.dram_tensor("bos_all", [8, 128, 4096], BF16,
                                 kind="ExternalOutput").ap()

    CH = 512
    es = contextlib.ExitStack()
    with es:
        ones_bf = es.enter_context(nc.sbuf_tensor("a_ones_bf", [128, 128], BF16))
        ones_f = es.enter_context(nc.sbuf_tensor("a_ones_f", [128, 128], F32))
        ident = es.enter_context(nc.sbuf_tensor("a_ident", [128, 128], BF16))
        xn1 = es.enter_context(nc.sbuf_tensor("a_xn1", [128, CT * T], BF16))
        bdiag = es.enter_context(nc.sbuf_tensor("a_bdiag", [128, 2 * 128 + 2 * 16], BF16))
        prel = es.enter_context(nc.sbuf_tensor("a_prel", [128, 64], F32))
        bcols = es.enter_context(nc.sbuf_tensor("a_bcols", [128, 2], F32))
        m0 = es.enter_context(nc.sbuf_tensor("a_m0", [128, 1024], BF16))
        zz = es.enter_context(nc.sbuf_tensor("a_zz", [128, 2048], BF16))
        em.dma("sync", ones_bf[:], nc.inline_tensor(np.ones((128, 128), BF), _uname("ones_c")).ap())
        em.dma("sync", ones_f[:], nc.inline_tensor(np.ones((128, 128), np.float32), _uname("onesf_c")).ap())
        em.dma("sync", ident[:], nc.inline_tensor(np.eye(128, dtype=BF), _uname("ident_c")).ap())
        em.dma("sync", m0[:], nc.inline_tensor(_l0_mask(), _uname("l0mask")).ap())
        em.dma("sync", bcols[:, 0:1], bdmg_rep)
        em.dma("sync", bcols[:, 1:2], bdo_rep)
        em.dwait("vector", "sync")
        em.mark("vector", nc.vector.memset(bdiag[:], 0.0))
        em.mark("vector", nc.vector.memset(zz[:], 0.0))
        em.dwait("gpsimd", "sync")
        em.wait("gpsimd", "vector")
        for hh in range(8):
            em.dma("gpsimd", bdiag[hh * 16:(hh + 1) * 16, hh * 16:(hh + 1) * 16], W_strain)
            em.dma("gpsimd", bdiag[hh * 16:(hh + 1) * 16, 128 + hh * 16:128 + (hh + 1) * 16], W_dmg)
            em.dma("gpsimd", bdiag[hh * 16:(hh + 1) * 16, 256 + hh:256 + hh + 1], W_bond)
            em.dma("gpsimd", bdiag[hh * 16:(hh + 1) * 16, 272 + hh:272 + hh + 1], W_dmg_out)
        em.wait("scalar", "vector")
        for i in range(16 * npan):
            em.dma("scalar",
                   pan_full.rearrange("(a b) -> a b", b=2048)[i * 64:(i + 1) * 64, :],
                   zz[0:64, :])

        p1 = contextlib.ExitStack()
        with p1:
            xb = p1.enter_context(nc.sbuf_tensor("a_p1_xb", [128, CT * 512], F32))
            sqb = p1.enter_context(nc.sbuf_tensor("a_p1_sqb", [128, CT * 512], BF16))
            rstd = p1.enter_context(nc.sbuf_tensor("a_p1_rstd", [128, 512], F32))
            mrstd = p1.enter_context(nc.sbuf_tensor("a_p1_mrstd", [128, 512], F32))
            tmp = p1.enter_context(nc.sbuf_tensor("a_p1_tmp", [128, 512], F32))
            pa = p1.enter_context(nc.psum_tensor("a_p1_pa", [128, 512], F32))
            pb = p1.enter_context(nc.psum_tensor("a_p1_pb", [128, 512], F32))
            for s in range(8):
                col = slice(s * 512, (s + 1) * 512)
                em.wait("sync", "tensor")
                em.wait("sync", "vector")
                em.wait("sync", "scalar")
                for ct in range(CT):
                    em.dma("sync", xb[:, ct * 512:(ct + 1) * 512],
                           x_in[ct * 128:(ct + 1) * 128, col])
                em.dwait("tensor", "sync")
                em.dwait("scalar", "sync")
                em.dwait("vector", "sync")
                emit_ln_stats_strip(em, lambda ct: xb[:, ct * 512:(ct + 1) * 512],
                                    512, ones_bf[:], ones_f[:], sqb, pa, pb,
                                    rstd, mrstd, tmp)
                for ct in range(CT):
                    emit_ln_apply_tile(
                        em, xn1[:, ct * T + s * 512: ct * T + (s + 1) * 512],
                        xb[:, ct * 512:(ct + 1) * 512], rstd[:], mrstd[:], tmp[:])
            em.barrier()

        p2 = contextlib.ExitStack()
        with p2:
            wsl = p2.enter_context(nc.sbuf_tensor("a_p2_wsl", [128, 2 * CT * 128], BF16))
            A_buf = p2.enter_context(nc.sbuf_tensor("a_p2_A", [128, 2 * 4160], BF16))
            D_buf = p2.enter_context(nc.sbuf_tensor("a_p2_D", [128, 2 * 4160], BF16))
            pbuf = p2.enter_context(nc.sbuf_tensor("a_p2_pb", [128, 2 * 512], BF16))
            pp0 = p2.enter_context(nc.psum_tensor("a_p2_p0", [128, 512], F32))
            pp1 = p2.enter_context(nc.psum_tensor("a_p2_p1", [128, 512], F32))
            p2d = contextlib.ExitStack()
            dispT = p2d.enter_context(nc.sbuf_tensor("a_p2_disp", [128, 2 * T], BF16))
            em.wait("vector", "tensor")
            nc.vector.memset(A_buf[:, 0:64], 0.0)
            nc.vector.memset(A_buf[:, 4160:4160 + 64], 0.0)
            nc.vector.memset(D_buf[:, 0:64], 0.0)
            em.mark("vector", nc.vector.memset(D_buf[:, 4160:4160 + 64], 0.0))
            pbanks = [Bank(pp0[:]), Bank(pp1[:])]
            wslot = [wsl[:, 0:CT * 128], wsl[:, CT * 128:2 * CT * 128]]
            watok = [0, 0]
            wi = 0
            for s in range(8):
                col = slice(s * 512, (s + 1) * 512)
                for mi in range(2):
                    sl = wi % 2
                    slot = wslot[sl]
                    em.wait("gpsimd", "tensor", watok[sl])
                    dc = em.dma("gpsimd", slot.rearrange("p (k c) -> p k c", c=128),
                           W_disp.rearrange("(k p) m -> p k m", p=128)
                           [:, :, mi * 128:(mi + 1) * 128])
                    em.dwait("tensor", "gpsimd", dc)
                    bank = pbanks[wi % 2]
                    pap = bank.acquire(em)
                    for k in range(CT):
                        mm = nc.tensor.matmul(
                            pap, slot[:, k * 128:(k + 1) * 128],
                            xn1[:, k * T + s * 512: k * T + (s + 1) * 512],
                            start=(k == 0), stop=(k == CT - 1))
                    mmc = em.mark("tensor", mm)
                    watok[sl] = mmc
                    em.wait("vector", "tensor", mmc)
                    vcp = em.mark("vector", nc.vector.tensor_copy(
                        dispT[:, mi * T + s * 512: mi * T + (s + 1) * 512], pap))
                    bank.release("vector", vcp)
                    wi += 1
            for s in range(8):
                col = slice(s * 512, (s + 1) * 512)
                for mi in range(CT):
                    sl = wi % 2
                    slot = wslot[sl]
                    em.wait("gpsimd", "tensor", watok[sl])
                    dc = em.dma("gpsimd", slot.rearrange("p (k c) -> p k c", c=128),
                           W_val.rearrange("(k p) m -> p k m", p=128)
                           [:, :, mi * 128:(mi + 1) * 128])
                    em.dwait("tensor", "gpsimd", dc)
                    bank = pbanks[wi % 2]
                    pap = bank.acquire(em)
                    for k in range(CT):
                        mm = nc.tensor.matmul(
                            pap, slot[:, k * 128:(k + 1) * 128],
                            xn1[:, k * T + s * 512: k * T + (s + 1) * 512],
                            start=(k == 0), stop=(k == CT - 1))
                    mmc = em.mark("tensor", mm)
                    watok[sl] = mmc
                    em.wait("vector", "tensor", mmc)
                    em.dwait("vector", "scalar", max(0, em.dcnt["scalar"] - 1))
                    pbv = pbuf[:, (wi % 2) * 512:(wi % 2 + 1) * 512]
                    vcp = em.mark("vector", nc.vector.tensor_copy(pbv, pap))
                    bank.release("vector", vcp)
                    em.wait("scalar", "vector", vcp)
                    em.dma("scalar", vdr[mi * 128:(mi + 1) * 128, col], pbv)
                    wi += 1
            em.dwait("tensor", "gpsimd")
            for hcT in range(2):
                for s in range(8):
                    for (wofs, dst) in ((0, A_buf), (128, D_buf)):
                        bank = pbanks[wi % 2]
                        pap = bank.acquire(em)
                        mm = nc.tensor.matmul(
                            pap, bdiag[:, wofs:wofs + 128],
                            dispT[:, hcT * T + s * 512: hcT * T + (s + 1) * 512],
                            start=True, stop=True)
                        mmc = em.mark("tensor", mm)
                        em.wait("vector", "tensor", mmc)
                        vcp = em.mark("vector", nc.vector.tensor_copy(
                            dst[:, hcT * 4160 + 64 + s * 512: hcT * 4160 + 64 + (s + 1) * 512],
                            pap))
                        bank.release("vector", vcp)
                        wi += 1
            em.wait("vector", "vector")
            p2d.close()
            wps = wsl[:, 0:16]
            rel_s = wsl[:, 16:16 + 64]
            em.wait("gpsimd", "tensor")
            em.dma("gpsimd", wps[0:16, :], W_pos)
            em.dma("gpsimd", rel_s[0:16, 0:64], relT)
            em.dwait("tensor", "gpsimd")
            mm = nc.tensor.matmul(pp0[0:16, 0:64], wps[0:16, :], rel_s[0:16, 0:64],
                                  start=True, stop=True)
            mmc = em.mark("tensor", mm)
            em.wait("vector", "tensor", mmc)
            vcp = em.mark("vector", nc.vector.tensor_copy(prel[0:16, :], pp0[0:16, 0:64]))
            em.wait("sync", "vector", vcp)
            for hh in range(1, 8):
                em.dma("sync", prel[hh * 16:(hh + 1) * 16, :], prel[0:16, :])
            em.dwait("scalar", "scalar")
            em.dwait("scalar", "sync")
            em.barrier()
            if dbg:
                em.dma("sync", dd["d_disp"], dispT[:])
                em.dma("sync", dd["d_A"], A_buf[:])
                em.dma("sync", dd["d_prel"], prel[:])
                em.dwait("sync", "sync")
                em.barrier()

            p3 = contextlib.ExitStack()
            with p3:
                st0 = p3.enter_context(nc.sbuf_tensor("a_p3_st0", [128, 512], BF16))
                st1 = p3.enter_context(nc.sbuf_tensor("a_p3_st1", [128, 512], BF16))
                st2 = p3.enter_context(nc.sbuf_tensor("a_p3_st2", [128, 512], BF16))
                st3 = p3.enter_context(nc.sbuf_tensor("a_p3_st3", [128, 512], BF16))
                ba0 = p3.enter_context(nc.sbuf_tensor("a_p3_ba0", [128, 512], BF16))
                ba1 = p3.enter_context(nc.sbuf_tensor("a_p3_ba1", [128, 512], BF16))
                ba2 = p3.enter_context(nc.sbuf_tensor("a_p3_ba2", [128, 512], BF16))
                ba3 = p3.enter_context(nc.sbuf_tensor("a_p3_ba3", [128, 512], BF16))
                Lb = p3.enter_context(nc.sbuf_tensor("a_p3_LB", [16, 8 * 512], BF16))
                Ld = p3.enter_context(nc.sbuf_tensor("a_p3_LD", [16, 8 * 512], BF16))
                sg = p3.enter_context(nc.sbuf_tensor("a_p3_sg", [128, 512], F32))
                Lsb = p3.enter_context(nc.sbuf_tensor("a_p3_Lsb", [128, 512], BF16))
                w_tm = p3.enter_context(nc.sbuf_tensor("a_p3_wtm", [128, 4096], BF16))
                P_w = p3.enter_context(nc.sbuf_tensor("a_p3_Pw", [128, 4096], BF16))
                w_hd = p3.enter_context(nc.sbuf_tensor("a_p3_whd", [128, 4096], BF16))
                mxs = p3.enter_context(nc.sbuf_tensor("a_p3_mxs", [128, 8], F32))
                vTh = p3.enter_context(nc.sbuf_tensor("a_p3_vTh", [64, 640], BF16))
                vtk = p3.enter_context(nc.sbuf_tensor("a_p3_vtk", [128, 5 * 64], BF16))
                Pp = p3.enter_context(nc.sbuf_tensor("a_p3_Pp", [128, 4 * 256], BF16))
                PTs = p3.enter_context(nc.sbuf_tensor("a_p3_PTs", [128, 256], BF16))
                atT = p3.enter_context(nc.sbuf_tensor("a_p3_atT", [128, CT * 512], BF16))
                bo_s = p3.enter_context(nc.sbuf_tensor("a_p3_bo", [128, CT * 512], BF16))
                sqb = p3.enter_context(nc.sbuf_tensor("a_p3_sqb", [128, CT * 512], BF16))
                rstd = p3.enter_context(nc.sbuf_tensor("a_p3_rstd", [128, 512], F32))
                mrstd = p3.enter_context(nc.sbuf_tensor("a_p3_mrstd", [128, 512], F32))
                tmp = p3.enter_context(nc.sbuf_tensor("a_p3_tmp", [128, 512], F32))
                xt = p3.enter_context(nc.sbuf_tensor("a_p3_xt", [128, 512], F32))
                wctok = [0, 0]
                p16a = p3.enter_context(nc.psum_tensor("a_p3_ba", [16, 512], F32))
                p16b = p3.enter_context(nc.psum_tensor("a_p3_bb", [16, 512], F32))
                pstk = p3.enter_context(nc.psum_tensor("a_p3_stk", [128, 512], BF16))
                ptr2 = p3.enter_context(nc.psum_tensor("a_p3_tr", [128, 512], BF16))
                ppv = p3.enter_context(nc.psum_tensor("a_p3_pv", [64, 128], F32))
                for ch in range(8):
                    t0 = ch * CH
                    for jb8 in range(8):
                        for jj in range(8):
                            j = jb8 * 8 + jj
                            em.wait("vector", "scalar")
                            nc.vector.tensor_sub(
                                st0[:], A_buf[:, 64 + t0 + j - 63:64 + t0 + j - 63 + 512],
                                A_buf[:, 64 + t0:64 + t0 + 512])
                            nc.vector.tensor_sub(
                                st1[:], A_buf[:, 4160 + 64 + t0 + j - 63:4160 + 64 + t0 + j - 63 + 512],
                                A_buf[:, 4160 + 64 + t0:4160 + 64 + t0 + 512])
                            nc.vector.tensor_sub(
                                st2[:], D_buf[:, 64 + t0 + j - 63:64 + t0 + j - 63 + 512],
                                D_buf[:, 64 + t0:64 + t0 + 512])
                            em.mark("vector", nc.vector.tensor_sub(
                                st3[:], D_buf[:, 4160 + 64 + t0 + j - 63:4160 + 64 + t0 + j - 63 + 512],
                                D_buf[:, 4160 + 64 + t0:4160 + 64 + t0 + 512]))
                            em.wait("scalar", "vector")
                            em.wait("scalar", "tensor")
                            nc.scalar.activation(ba0[:], st0[:], AF.Gelu, bias=prel[:, j:j + 1])
                            nc.scalar.activation(ba1[:], st1[:], AF.Gelu, bias=prel[:, j:j + 1])
                            nc.scalar.activation(ba2[:], st2[:], AF.Gelu, bias=bcols[:, 0:1])
                            sc = em.mark("scalar", nc.scalar.activation(
                                ba3[:], st3[:], AF.Gelu, bias=bcols[:, 0:1]))
                            em.wait("tensor", "scalar", sc)
                            em.wait("tensor", "vector")
                            nc.tensor.matmul(p16a[:], bdiag[:, 256:272], ba0[:],
                                             start=True, stop=False)
                            nc.tensor.matmul(p16a[:], bdiag[:, 256:272], ba1[:],
                                             start=False, stop=True)
                            nc.tensor.matmul(p16b[:], bdiag[:, 272:288], ba2[:],
                                             start=True, stop=False)
                            md = nc.tensor.matmul(p16b[:], bdiag[:, 272:288], ba3[:],
                                                  start=False, stop=True)
                            mmc = em.mark("tensor", md)
                            em.wait("vector", "tensor", mmc)
                            nc.vector.tensor_copy(Lb[:, jj * 512:(jj + 1) * 512], p16a[0:16, :])
                            em.mark("vector", nc.vector.tensor_copy(
                                Ld[:, jj * 512:(jj + 1) * 512], p16b[0:16, :]))
                        for jj in range(8):
                            j = jb8 * 8 + jj
                            em.wait("scalar", "vector")
                            sc = em.mark("scalar", nc.scalar.activation(
                                sg[0:16, :], Ld[:, jj * 512:(jj + 1) * 512], AF.Sigmoid,
                                bias=bcols[0:16, 1:2]))
                            em.wait("vector", "scalar", sc)
                            em.wait("vector", "tensor")
                            vcp = em.mark("vector", nc.vector.scalar_tensor_tensor(
                                Lsb[0:16, :], sg[0:16, :], -10.0, Lb[:, jj * 512:(jj + 1) * 512],
                                ALU.mult, ALU.add))
                            em.wait("tensor", "vector", vcp)
                            for tsub in range(4):
                                tr = nc.tensor.transpose(
                                    pstk[:, tsub * 128 + jj * 16: tsub * 128 + jj * 16 + 16],
                                    Lsb[0:16, tsub * 128:(tsub + 1) * 128], ident[0:16, 0:16])
                            tc = em.mark("tensor", tr)
                        em.wait("vector", "tensor")
                        for tsub in range(4):
                            vcp = nc.vector.tensor_copy(
                                w_tm[:, tsub * 1024 + jb8 * 128: tsub * 1024 + (jb8 + 1) * 128],
                                pstk[:, tsub * 128:(tsub + 1) * 128])
                        em.mark("vector", vcp)
                        em.wait("tensor", "vector")
                    if ch == 0:
                        em.mark("vector", nc.vector.tensor_add(
                            w_tm[:, 0:1024], w_tm[:, 0:1024], m0[:]))
                    if dbg and ch == 0:
                        em.barrier()
                        em.dma("sync", dd["d_wtm"], w_tm[:])
                        em.dwait("sync", "sync")
                        em.barrier()
                    em.wait("vector", "vector")
                    for tsub in range(4):
                        for h in range(16):
                            src = w_tm[:].rearrange("p (ts j h) -> p ts j h",
                                                    ts=4, j=64, h=16)[:, tsub, :, h]
                            em.wait("vector", "scalar")
                            em.mark("vector", nc.vector.reduce_max(
                                mxs[:, 0:1], src, axis=AX.X, negate=True))
                            em.wait("vector", "vector")
                            em.wait("scalar", "vector")
                            pdst = P_w[:].rearrange("p (ts j h) -> p ts j h",
                                                    ts=4, j=64, h=16)[:, tsub, :, h]
                            sc = em.mark("scalar", nc.scalar.activation(
                                pdst, src, AF.Exp, bias=mxs[:, 0:1]))
                            em.wait("vector", "scalar", sc)
                            em.mark("vector", nc.vector.reduce_sum(
                                mxs[:, 1:2], pdst, axis=AX.X))
                            em.wait("vector", "vector")
                            em.mark("vector", nc.vector.reciprocal(mxs[:, 2:3], mxs[:, 1:2]))
                            em.wait("vector", "vector")
                            em.dwait("vector", "gpsimd")
                            em.mark("vector", nc.vector.tensor_scalar(
                                w_hd[:].rearrange("p (a b) -> p a b", b=64)[:, h * 4 + tsub],
                                pdst, mxs[:, 2:3], None, ALU.mult))
                    if dbg and ch == 0:
                        em.barrier()
                        em.dma("sync", dd["d_whd"], w_hd[:])
                        em.dwait("sync", "sync")
                        em.barrier()
                    pan = pan_full[(ch % npan) * 16 * 131072:
                                    ((ch % npan) + 1) * 16 * 131072]
                    vv = em.cnt["vector"]
                    em.wait("scalar", "vector", vv)
                    em.dwait("scalar", "sync")
                    em.dwait("scalar", "scalar")
                    if dbg:
                        em.dma("scalar", whd_all[ch], w_hd[:])
                    for hq in range(16):
                        em.dma("scalar", _l0_band_dst(pan, hq),
                               w_hd[:, hq * 256:(hq + 1) * 256]
                               .rearrange("p (ts j) -> p ts j", ts=4, j=64))
                    for h in range(16):
                        em.wait("sync", "tensor")
                        lo = max(0, t0 - 128)
                        ln = min(t0 + 512, T) - lo
                        seg0 = (lo - (t0 - 128)) // 128
                        em.dma("sync", vTh[:, seg0 * 128:seg0 * 128 + ln],
                               vdr[h * 64:(h + 1) * 64, lo:lo + ln])
                        em.dwait("tensor", "sync")
                        em.wait("tensor", "vector")
                        if ch == 0:
                            em.wait("vector", "tensor")
                            em.mark("vector", nc.vector.memset(vtk[:, 0:64], 0.0))
                            em.wait("tensor", "vector")
                        for sgi in range(seg0, 5):
                            tr = nc.tensor.transpose(
                                ptr2[:, sgi * 64:(sgi + 1) * 64],
                                vTh[:, sgi * 128:(sgi + 1) * 128], ident[0:64, 0:64])
                        tc = em.mark("tensor", tr)
                        em.wait("vector", "tensor", tc)
                        vcp = em.mark("vector", nc.vector.tensor_copy(
                            vtk[:, seg0 * 64:5 * 64], ptr2[:, seg0 * 64:5 * 64]))
                        em.wait("tensor", "vector", vcp)
                        em.dwait("sync", "scalar")
                        em.dma("sync", Pp[:],
                               pan.rearrange("(h ti c) -> h ti c", h=16, ti=128)[h])
                        em.dwait("tensor", "sync")
                        for tsub in range(4):
                            for half in range(2):
                                tr = nc.tensor.transpose(
                                    ptr2[:, half * 128:(half + 1) * 128],
                                    Pp[:, tsub * 256 + half * 128: tsub * 256 + (half + 1) * 128],
                                    ident[:])
                            tc = em.mark("tensor", tr)
                            em.wait("vector", "tensor", tc)
                            vcp = em.mark("vector", nc.vector.tensor_copy(
                                PTs[:], ptr2[:, 0:256]))
                            em.wait("tensor", "vector", vcp)
                            for half in range(2):
                                mm = nc.tensor.matmul(
                                    ppv[:], vtk[:, (tsub + half) * 64:(tsub + half + 1) * 64],
                                    PTs[:, half * 128:(half + 1) * 128],
                                    start=(half == 0), stop=(half == 1))
                            mmc = em.mark("tensor", mm)
                            em.wait("vector", "tensor", mmc)
                            vcp = em.mark("vector", nc.vector.tensor_copy(
                                atT[(h % 2) * 64:(h % 2) * 64 + 64,
                                    (h // 2) * 512 + tsub * 128:(h // 2) * 512 + (tsub + 1) * 128],
                                ppv[:]))
                            em.wait("tensor", "vector", vcp)
                    if dbg and ch == 0:
                        em.barrier()
                        em.dma("sync", dd["d_Pp"], Pp[:])
                        em.dma("sync", dd["d_atT"], atT[:])
                        em.dwait("sync", "sync")
                        em.barrier()
                    em.wait("tensor", "vector")
                    if dbg:
                        em.wait("scalar", "vector")
                        em.dma("scalar", atT_all[ch], atT[:])
                    wslot2 = [wsl[:, 0:CT * 128], wsl[:, CT * 128:2 * CT * 128]]
                    for ct in range(CT):
                        sl = ct % 2
                        slot = wslot2[sl]
                        em.wait("gpsimd", "tensor", wctok[sl])
                        dc = em.dma("gpsimd", slot.rearrange("p (k c) -> p k c", c=128),
                               W_cproj.rearrange("(k p) m -> p k m", p=128)
                               [:, :, ct * 128:(ct + 1) * 128])
                        em.dwait("tensor", "gpsimd", dc)
                        bank = pbanks[ct % 2]
                        pap = bank.acquire(em)
                        for k in range(CT):
                            mm = nc.tensor.matmul(pap, slot[:, k * 128:(k + 1) * 128],
                                                  atT[:, k * 512:(k + 1) * 512],
                                                  start=(k == 0), stop=(k == CT - 1))
                        mmc = em.mark("tensor", mm)
                        wctok[sl] = mmc
                        em.wait("vector", "tensor", mmc)
                        em.wait("vector", "scalar")
                        vcp = em.mark("vector", nc.vector.tensor_copy(
                            bo_s[:, ct * 512:(ct + 1) * 512], pap))
                        bank.release("vector", vcp)
                    if dbg:
                        em.wait("scalar", "vector")
                        em.dma("scalar", bos_all[ch], bo_s[:])
                    emit_ln_stats_strip(em, lambda ct: bo_s[:, ct * 512:(ct + 1) * 512],
                                        512, ones_bf[:], None, sqb,
                                        pp0[:], pp1[:], rstd, mrstd, tmp,
                                        x_dtype_bf=True)
                    pbanks[0].release("vector", em.cnt["vector"])
                    pbanks[1].release("vector", em.cnt["vector"])
                    for ct in range(CT):
                        em.wait("sync", "vector")
                        em.dwait("sync", "scalar")
                        em.dma("sync", xt[:], x_in[ct * 128:(ct + 1) * 128, t0:t0 + 512])
                        em.dwait("vector", "sync")
                        nc.vector.tensor_mul(tmp[:], bo_s[:, ct * 512:(ct + 1) * 512], rstd[:])
                        nc.vector.tensor_sub(tmp[:], tmp[:], mrstd[:])
                        vcp = em.mark("vector", nc.vector.tensor_add(xt[:], xt[:], tmp[:]))
                        em.wait("scalar", "vector", vcp)
                        em.dma("scalar", x_out[ct * 128:(ct + 1) * 128, t0:t0 + 512], xt[:])
                        em.dwait("vector", "scalar")
                em.dwait("scalar", "scalar")
                em.barrier()
    return


def _l0_band_dst(pan, h):
    from concourse.ap import AP as _AP
    return _AP(pan.tensor, pan.offset + h * 131072 + 65,
               [[1025, 128], [256, 4], [1, 64]])


def run_l0_attn_unfused(xT_list, I):
    bdmg_rep = np.ascontiguousarray(np.tile(I["L0_attn_b_dmg"], 8)[:, None]).astype(np.float32)
    bdo_rep = np.ascontiguousarray(np.tile(I["L0_attn_b_dmg_out"], 128)[:, None]).astype(np.float32)
    nc = build_a()
    ins = [dict(x=xT, W_disp=I["L0_attn_W_disp"], W_val=I["L0_attn_W_val"],
                W_cproj=I["L0_attn_W_cproj"],
                relT=np.ascontiguousarray(I["L0_attn_rel_pos_emb"].T),
                W_strain=I["L0_attn_W_strain"], W_pos=I["L0_attn_W_pos"],
                W_bond=I["L0_attn_W_bond"], W_dmg=I["L0_attn_W_dmg"],
                W_dmg_out=I["L0_attn_W_dmg_out"], bdmg_rep=bdmg_rep,
                bdo_rep=bdo_rep) for xT in xT_list]
    r = run_bass_kernel_spmd(nc, ins, core_ids=list(range(len(ins))))
    return [res["xo"] for res in r.results]



def build_c(dbg=False):
    nc = bass.Bass("TRN2", target_bir_lowering=False, debug=False)
    x_in = nc.dram_tensor("x", [D, T], F32, kind="ExternalInput").ap()
    IO = {n: nc.dram_tensor(n, s, F32, kind="ExternalInput").ap()
          for n, s in (("Wq", [D, D]), ("Wk", [D, D]), ("Wv", [D, D]),
                       ("Wr", [D, 512]), ("Wo", [D, D]), ("bcols", [128, 36]))}
    x_out = nc.dram_tensor("xo", [D, T], F32, kind="ExternalOutput").ap()
    em = EM(nc)
    emit_const_pool(nc, em, None, [EPS, -30.0, 1e-30], alloc=nc)
    emit_c(nc, em, x_in, x_out, IO, dbg=dbg)
    return nc


def emit_c(nc, em, x_in, x_out, IO, dbg=False):
    Wq = IO["Wq"]; Wk = IO["Wk"]; Wv = IO["Wv"]; Wr = IO["Wr"]; Wo = IO["Wo"]
    bcols = IO["bcols"]

    kind = "ExternalOutput" if dbg else "Internal"
    if dbg:
        pdbg = nc.dram_tensor("pdbg", [128, 4 * T], BF16, kind="ExternalOutput").ap()
        vdbg = nc.dram_tensor("vdbg", [128, 2048], BF16, kind="ExternalOutput").ap()
        ndbg = nc.dram_tensor("ndbg", [128, 8], F32, kind="ExternalOutput").ap()
        mdbg = nc.dram_tensor("mdbg", [128, 32], F32, kind="ExternalOutput").ap()
    qdr = nc.dram_tensor("c_qdr", [D, T], BF16, kind=kind).ap()
    kdr = nc.dram_tensor("c_kdr", [D, T], BF16, kind=kind).ap()
    vdr = nc.dram_tensor("c_vdr", [D, T], BF16, kind=kind).ap()
    rdr = nc.dram_tensor("c_rdr", [512, T], BF16, kind=kind).ap()
    adr = nc.dram_tensor("c_adr", [D, T], BF16, kind=kind).ap()

    SCALE = 1.0 / np.sqrt(96.0)
    SCALE_R = float(np.sqrt(SCALE))
    NS = 8
    ncols = 512

    es = contextlib.ExitStack()
    with es:
        xn = es.enter_context(nc.sbuf_tensor("c_xn", [128, CT * T], BF16))
        bc_sb = es.enter_context(nc.sbuf_tensor("c_bc", [128, 36], F32))
        ones_bf = es.enter_context(nc.sbuf_tensor("c_ones_bf", [128, 128], BF16))
        ident = es.enter_context(nc.sbuf_tensor("c_ident", [128, 128], BF16))
        em.dma("sync", ones_bf[:], nc.inline_tensor(np.ones((128, 128), BF), _uname("ones_c")).ap())
        em.dma("sync", ident[:], nc.inline_tensor(np.eye(128, dtype=BF), _uname("ident_c")).ap())
        em.dma("sync", bc_sb[:], bcols)

        p1 = contextlib.ExitStack()
        with p1:
            xb = p1.enter_context(nc.sbuf_tensor("c_xb", [128, CT * ncols], F32))
            sqb = p1.enter_context(nc.sbuf_tensor("c_sqb", [128, CT * ncols], BF16))
            rstd = p1.enter_context(nc.sbuf_tensor("c_rstd", [128, ncols], F32))
            mrstd = p1.enter_context(nc.sbuf_tensor("c_mrstd", [128, ncols], F32))
            tmp = p1.enter_context(nc.sbuf_tensor("c_tmp", [128, ncols], F32))
            ones_f = p1.enter_context(nc.sbuf_tensor("c_ones_f", [128, 128], F32))
            pa = p1.enter_context(nc.psum_tensor("c_pa", [128, ncols], F32))
            pb = p1.enter_context(nc.psum_tensor("c_pb", [128, ncols], F32))
            em.dma("sync", ones_f[:], nc.inline_tensor(np.ones((128, 128), np.float32), _uname("onesf_c")).ap())
            for s in range(NS):
                col = slice(s * ncols, (s + 1) * ncols)
                em.wait("sync", "tensor")
                em.wait("sync", "vector")
                em.wait("sync", "scalar")
                for ct in range(CT):
                    em.dma("sync", xb[:, ct * ncols:(ct + 1) * ncols],
                           x_in[ct * 128:(ct + 1) * 128, col])
                em.dwait("tensor", "sync")
                em.dwait("scalar", "sync")
                em.dwait("vector", "sync")
                emit_ln_stats_strip(em, lambda ct: xb[:, ct * ncols:(ct + 1) * ncols],
                                    ncols, ones_bf[:], ones_f[:], sqb, pa, pb,
                                    rstd, mrstd, tmp)
                for ct in range(CT):
                    emit_ln_apply_tile(
                        em, xn[:, ct * T + s * ncols: ct * T + (s + 1) * ncols],
                        xb[:, ct * ncols:(ct + 1) * ncols],
                        rstd[:, :ncols], mrstd[:, :ncols], tmp[:])
            em.barrier()

        projs = [
            (Wq, qdr, CT, 0, SCALE),
            (Wk, kdr, CT, 8, None),
            (Wv, vdr, CT, 16, None),
            (Wr, rdr, 4, 32, SCALE_R),
        ]
        p2 = contextlib.ExitStack()
        with p2:
            wsl = p2.enter_context(nc.sbuf_tensor("c_wsl", [128, 4 * CT * 128], BF16))
            pbuf = p2.enter_context(nc.sbuf_tensor("c_pbuf", [128, 2 * ncols], BF16))
            pp0 = p2.enter_context(nc.psum_tensor("c_pp0", [128, ncols], F32))
            pp1 = p2.enter_context(nc.psum_tensor("c_pp1", [128, ncols], F32))
            pbanks = [Bank(pp0[:]), Bank(pp1[:])]
            bi = 0
            wslot = [wsl[:, i * CT * 128:(i + 1) * CT * 128] for i in range(4)]
            wtok = [0] * 4
            wi = 0
            for s in range(NS):
                col = slice(s * ncols, (s + 1) * ncols)
                for (Wd, odr, nmi, boff, scl) in projs:
                    for mi in range(nmi):
                        sl = wi % 4
                        slot = wslot[sl]
                        em.wait("gpsimd", "tensor", wtok[sl])
                        dc = em.dma("gpsimd", slot.rearrange("p (k c) -> p k c", c=128),
                               Wd.rearrange("(k p) m -> p k m", p=128)[:, :, mi * 128:(mi + 1) * 128])
                        em.dwait("tensor", "gpsimd", dc)
                        wi += 1
                        bank = pbanks[bi % 2]
                        pap = bank.acquire(em)
                        for k in range(CT):
                            mm = nc.tensor.matmul(
                                pap, slot[:, k * 128:(k + 1) * 128],
                                xn[:, k * T + s * ncols: k * T + (s + 1) * ncols],
                                start=(k == 0), stop=(k == CT - 1))
                        mmc = em.mark("tensor", mm)
                        wtok[sl] = mmc
                        em.wait("vector", "tensor", mmc)
                        em.dwait("vector", "scalar", max(0, em.dcnt["scalar"] - 1))
                        pb_ = pbuf[:, (bi % 2) * ncols:((bi % 2) + 1) * ncols]
                        if scl is None:
                            vc = em.mark("vector", nc.vector.tensor_scalar(
                                pb_, pap, bc_sb[:, boff + mi:boff + mi + 1], None, ALU.add))
                        else:
                            vc = em.mark("vector", nc.vector.tensor_scalar(
                                pb_, pap, bc_sb[:, boff + mi:boff + mi + 1], scl,
                                ALU.add, ALU.mult))
                        bank.release("vector", vc)
                        em.wait("scalar", "vector", vc)
                        em.dma("scalar", odr[mi * 128:(mi + 1) * 128, col], pb_)
                        bi += 1
            em.dwait("scalar", "scalar")
            em.barrier()

        NG = 4
        GQ = NG * 128
        p3 = contextlib.ExitStack()
        with p3:
            qr_sb = p3.enter_context(nc.sbuf_tensor("c_qr", [128, T], BF16))
            kr_sb = p3.enter_context(nc.sbuf_tensor("c_kr", [128, T], BF16))
            vT_sb = p3.enter_context(nc.sbuf_tensor("c_vT", [64, T], BF16))
            vtok = p3.enter_context(nc.sbuf_tensor("c_vtok", [128, 32 * 64], BF16))
            P_sb = p3.enter_context(nc.sbuf_tensor("c_Pb", [128, NG * T], BF16))
            PT_sb = p3.enter_context(nc.sbuf_tensor("c_PT", [128, 32 * GQ], BF16))
            mx = p3.enter_context(nc.sbuf_tensor("c_mx", [128, 8 * NG], F32))
            nmx = p3.enter_context(nc.sbuf_tensor("c_nmx", [128, NG], F32))
            dn = p3.enter_context(nc.sbuf_tensor("c_dn", [128, 2 * NG], F32))
            csc = p3.enter_context(nc.sbuf_tensor("c_csc", [128, 2 * 512], F32))
            ag = p3.enter_context(nc.sbuf_tensor("c_ag", [128, CT * ncols], BF16))
            sA = p3.enter_context(nc.psum_tensor("c_sA", [128, 512], F32))
            sB = p3.enter_context(nc.psum_tensor("c_sB", [128, 512], F32))
            ptrb = p3.enter_context(nc.psum_tensor("c_ptr", [128, 512], BF16))
            pvb = p3.enter_context(nc.psum_tensor("c_pv", [64, 512], F32))
            simb = [Bank(sA[:]), Bank(sB[:])]
            cexp = [0, 0]
            for h in range(16):
                em.wait("sync", "tensor")
                em.wait("sync", "scalar")
                em.dma("sync", qr_sb[0:64, :], qdr[h * 64:(h + 1) * 64, :])
                em.dma("sync", qr_sb[64:96, :], rdr[h * 32:(h + 1) * 32, :])
                em.dma("sync", kr_sb[0:64, :], kdr[h * 64:(h + 1) * 64, :])
                em.dma("sync", kr_sb[64:96, :], rdr[h * 32:(h + 1) * 32, :])
                em.dma("sync", vT_sb[:, :], vdr[h * 64:(h + 1) * 64, :])
                em.dwait("tensor", "sync")
                em.wait("tensor", "vector")
                for j8 in range(4):
                    for jj in range(8):
                        j = j8 * 8 + jj
                        tr = nc.tensor.transpose(
                            ptrb[:, jj * 64:(jj + 1) * 64],
                            vT_sb[:, j * 128:(j + 1) * 128], ident[0:64, 0:64])
                    tc = em.mark("tensor", tr)
                    em.wait("vector", "tensor", tc)
                    vc = em.mark("vector", nc.vector.tensor_copy(
                        vtok[:, j8 * 512:(j8 + 1) * 512], ptrb[:, 0:512]))
                    em.wait("tensor", "vector", vc)
                for g in range(T // GQ):
                    em.wait("scalar", "vector")
                    for qt in range(NG):
                        q0 = g * GQ + qt * 128
                        lhs = qr_sb[0:96, q0:q0 + 128]
                        for si in range(NS):
                            bank = simb[si % 2]
                            pap = bank.acquire(em)
                            mm = nc.tensor.matmul(pap, lhs,
                                                  kr_sb[0:96, si * 512:(si + 1) * 512],
                                                  start=True, stop=True)
                            mmc = em.mark("tensor", mm)
                            em.wait("vector", "tensor", mmc)
                            em.wait("vector", "scalar", cexp[si % 2])
                            vc = em.mark("vector", nc.vector.tensor_scalar(
                                csc[:, (si % 2) * 512:(si % 2 + 1) * 512],
                                pap, 30.0, None, ALU.min))
                            bank.release("vector", vc)
                            em.wait("scalar", "vector", vc)
                            sc = em.mark("scalar", nc.scalar.activation(
                                P_sb[:, qt * T + si * 512: qt * T + (si + 1) * 512],
                                csc[:, (si % 2) * 512:(si % 2 + 1) * 512],
                                AF.Exp, bias=-30.0))
                            cexp[si % 2] = sc
                    scc = em.cnt["scalar"]
                    em.wait("vector", "scalar", scc)
                    for qt in range(NG):
                        em.mark("vector", nc.vector.reduce_sum(
                            dn[:, qt:qt + 1], P_sb[:, qt * T:(qt + 1) * T], axis=AX.X))
                    em.wait("vector", "vector")
                    for qt in range(NG):
                        em.mark("vector", nc.vector.reciprocal(
                            dn[:, NG + qt:NG + qt + 1], dn[:, qt:qt + 1]))
                    vcc = em.cnt["vector"]
                    em.wait("scalar", "vector", vcc)
                    for qt in range(NG):
                        for hf in range(2):
                            cs = slice(qt * T + hf * 2048, qt * T + (hf + 1) * 2048)
                            em.mark("scalar", nc.scalar.activation(
                                P_sb[:, cs], P_sb[:, cs], AF.Copy,
                                scale=dn[:, NG + qt:NG + qt + 1]))
                    if dbg and h == 0 and g == 0:
                        em.barrier()
                        em.dma("scalar", pdbg, P_sb[:])
                        em.dma("scalar", vdbg, vtok[:])
                        em.dma("scalar", ndbg[:, 4:4 + NG], dn[:, NG:2 * NG])
                        em.dwait("scalar", "scalar")
                        em.barrier()
                    em.wait("tensor", "scalar")
                    em.wait("tensor", "vector")
                    for j in range(32):
                        for qt in range(NG):
                            tr = nc.tensor.transpose(
                                ptrb[:, qt * 128:(qt + 1) * 128],
                                P_sb[:, qt * T + j * 128: qt * T + (j + 1) * 128],
                                ident[:])
                        tc = em.mark("tensor", tr)
                        em.wait("scalar", "tensor", tc)
                        vc = em.mark("scalar", nc.scalar.copy(
                            PT_sb[:, j * GQ:(j + 1) * GQ], ptrb[:, 0:GQ]))
                        em.wait("tensor", "scalar", vc)
                    for j in range(32):
                        mm = nc.tensor.matmul(pvb[:], vtok[:, j * 64:(j + 1) * 64],
                                              PT_sb[:, j * GQ:(j + 1) * GQ],
                                              start=(j == 0), stop=(j == 31))
                    mmc = em.mark("tensor", mm)
                    em.wait("vector", "tensor", mmc)
                    em.dwait("vector", "scalar")
                    vc = em.mark("vector", nc.vector.tensor_copy(
                        ag[(h % 2) * 64:(h % 2) * 64 + 64,
                           (h // 2) * ncols:(h // 2 + 1) * ncols], pvb[:]))
                    em.wait("scalar", "vector", vc)
                    em.wait("tensor", "vector", vc)
                    em.dma("scalar",
                           adr.rearrange("(ct p) t -> ct p t", p=128)
                           [h // 2, (h % 2) * 64:(h % 2) * 64 + 64, g * GQ:(g + 1) * GQ],
                           ag[(h % 2) * 64:(h % 2) * 64 + 64,
                              (h // 2) * ncols:(h // 2 + 1) * ncols])
            em.dwait("scalar", "scalar")
            em.barrier()

        p4 = contextlib.ExitStack()
        with p4:
            wsl = p4.enter_context(nc.sbuf_tensor("c_wsl4", [128, 2 * CT * 128], BF16))
            ab = p4.enter_context(nc.sbuf_tensor("c_ab", [128, CT * ncols], BF16))
            bo_strip = p4.enter_context(nc.sbuf_tensor("c_bo_strip", [128, CT * ncols], BF16))
            sqb = p4.enter_context(nc.sbuf_tensor("c_sqb4", [128, CT * ncols], BF16))
            rstd = p4.enter_context(nc.sbuf_tensor("c_rstd4", [128, ncols], F32))
            mrstd = p4.enter_context(nc.sbuf_tensor("c_mrstd4", [128, ncols], F32))
            tmp = p4.enter_context(nc.sbuf_tensor("c_tmp4", [128, ncols], F32))
            xt_sb = p4.enter_context(nc.sbuf_tensor("c_xt", [128, ncols], F32))
            pp0 = p4.enter_context(nc.psum_tensor("c_pp04", [128, ncols], F32))
            pp1 = p4.enter_context(nc.psum_tensor("c_pp14", [128, ncols], F32))
            pa = p4.enter_context(nc.psum_tensor("c_pa4", [128, ncols], F32))
            pb = p4.enter_context(nc.psum_tensor("c_pb4", [128, ncols], F32))
            wslot = [wsl[:, 0:CT * 128], wsl[:, CT * 128:2 * CT * 128]]
            w4tok = [0, 0]
            pbanks = [Bank(pp0[:]), Bank(pp1[:])]
            wi = 0
            for s in range(NS):
                col = slice(s * ncols, (s + 1) * ncols)
                em.wait("sync", "tensor")
                for ct in range(CT):
                    em.dma("sync", ab[:, ct * ncols:(ct + 1) * ncols],
                           adr[ct * 128:(ct + 1) * 128, col])
                em.dwait("tensor", "sync")
                for mi in range(CT):
                    sl = wi % 2
                    slot = wslot[sl]
                    em.wait("gpsimd", "tensor", w4tok[sl])
                    dc = em.dma("gpsimd", slot.rearrange("p (k c) -> p k c", c=128),
                           Wo.rearrange("(k p) m -> p k m", p=128)[:, :, mi * 128:(mi + 1) * 128])
                    em.dwait("tensor", "gpsimd", dc)
                    bank = pbanks[wi % 2]
                    pap = bank.acquire(em)
                    for k in range(CT):
                        mm = nc.tensor.matmul(pap, slot[:, k * 128:(k + 1) * 128],
                                              ab[:, k * ncols:(k + 1) * ncols],
                                              start=(k == 0), stop=(k == CT - 1))
                    mmc = em.mark("tensor", mm)
                    w4tok[sl] = mmc
                    em.wait("vector", "tensor", mmc)
                    em.wait("vector", "scalar")
                    vc = em.mark("vector", nc.vector.tensor_scalar(
                        bo_strip[:, mi * ncols:(mi + 1) * ncols], pap,
                        bc_sb[:, 24 + mi:24 + mi + 1], None, ALU.add))
                    bank.release("vector", vc)
                    wi += 1
                emit_ln_stats_strip(em, lambda ct: bo_strip[:, ct * ncols:(ct + 1) * ncols],
                                    ncols, ones_bf[:], None, sqb, pa, pb,
                                    rstd, mrstd, tmp, x_dtype_bf=True)
                for ct in range(CT):
                    em.wait("sync", "vector")
                    em.dwait("sync", "scalar")
                    em.dma("sync", xt_sb[:], x_in[ct * 128:(ct + 1) * 128, col])
                    em.dwait("vector", "sync")
                    nc.vector.tensor_mul(tmp[:], bo_strip[:, ct * ncols:(ct + 1) * ncols],
                                         rstd[:, :ncols])
                    nc.vector.tensor_sub(tmp[:], tmp[:], mrstd[:, :ncols])
                    vc = em.mark("vector", nc.vector.tensor_add(xt_sb[:], xt_sb[:], tmp[:]))
                    em.wait("scalar", "vector", vc)
                    em.dma("scalar", x_out[ct * 128:(ct + 1) * 128, col], xt_sb[:])
                    em.dwait("vector", "scalar")
            em.dwait("scalar", "scalar")
            em.barrier()
    return


def run_l1_attn_unfused(xT_list, I):
    bcols = np.zeros((128, 36), np.float32)
    bcols[:, 0:8] = _cols(I["L1_attn_bq"])
    bcols[:, 8:16] = _cols(I["L1_attn_bk"])
    bcols[:, 16:24] = _cols(I["L1_attn_bv"])
    bcols[:, 24:32] = _cols(I["L1_attn_bo"])
    bcols[:, 32:36] = np.ascontiguousarray(I["L1_attn_br"].reshape(4, 128).T)
    nc = build_c()
    ins = [dict(x=xT, Wq=I["L1_attn_Wq"], Wk=I["L1_attn_Wk"], Wv=I["L1_attn_Wv"],
                Wr=I["L1_attn_Wr"], Wo=I["L1_attn_Wo"], bcols=bcols)
           for xT in xT_list]
    r = run_bass_kernel_spmd(nc, ins, core_ids=list(range(NCORES)))
    return [res["xo"] for res in r.results]



def _l1ffn_tables():
    DIM_HEAD, WINDOW, n = 64, 256, T
    half = np.arange(0, DIM_HEAD, 2, dtype=np.float64)
    inv_freq = 1.0 / 10000.0 ** (half / DIM_HEAD)
    t = np.arange(n, dtype=np.float64)
    freqs = t[:, None] * inv_freq[None, :]
    freqs = np.concatenate([freqs, freqs], -1)
    base_scale = (half + 0.4 * DIM_HEAD) / (1.4 * DIM_HEAD)
    power = (t - n // 2) / (WINDOW // 2)
    scale = base_scale[None, :] ** power[:, None]
    scale = np.concatenate([scale, scale], -1)
    cos, sin = np.cos(freqs), np.sin(freqs)
    sgn = np.concatenate([-np.ones(32), np.ones(32)])[None, :]
    QK_SCALE = 8.0
    Aq = (QK_SCALE * cos * scale).astype(BF)
    Bq = (QK_SCALE * sin * scale * sgn).astype(BF)
    Ak = (cos / scale).astype(BF)
    Bk = (sin / scale * sgn).astype(BF)
    return Aq, Bq, Ak, Bk


def _l1ffn_masks():
    i = np.arange(128)[:, None]
    jl = np.arange(384)[None, :]
    std = np.where((jl >= i) & (jl <= i + 256), 0.0, -1e30).astype(np.float32)
    f0 = np.where((jl >= i) & (jl <= i + 256) & (jl >= 256), 0.0, -1e30).astype(np.float32)
    f1 = np.where((jl >= i) & (jl <= i + 256) & (jl >= 128), 0.0, -1e30).astype(np.float32)
    return std, f0, f1


def build_d(dbg=False):
    nc = bass.Bass("TRN2", target_bir_lowering=False, debug=False)
    x_in = nc.dram_tensor("x", [D, T], F32, kind="ExternalInput").ap()
    IO = {n: nc.dram_tensor(n, s, F32, kind="ExternalInput").ap()
          for n, s in (("Wqkv", [6, D, 3072]), ("Wo_d", [6, D, D]),
                       ("W1", [6, D, 5460]), ("W2", [6, 2730, D]),
                       ("qs_rows", [6, 2, D]), ("gb_cols", [6, 2, 128, CT]),
                       ("b1_cols", [6, 2, 128, 22]), ("b2_cols", [6, 128, CT]))}
    x_out = nc.dram_tensor("xo", [D, T], F32, kind="ExternalOutput").ap()
    em = EM(nc)
    emit_const_pool(nc, em, None, [EPS, 1e-24, -20.0, 1e-30], alloc=nc)
    emit_d(nc, em, x_in, x_out, IO, dbg=dbg)
    return nc


def emit_d(nc, em, x_in, x_out, IO, dbg=False):
    Wqkv = IO["Wqkv"]; Wo = IO["Wo_d"]; W1 = IO["W1"]; W2 = IO["W2"]
    qs_rows = IO["qs_rows"]; gb_cols = IO["gb_cols"]; b1_cols = IO["b1_cols"]
    b2_cols = IO["b2_cols"]

    xi = nc.dram_tensor("d_xi", [D, T], F32).ap()
    if dbg:
        xd = [nc.dram_tensor(f"xd{i}", [D, T], F32, kind="ExternalOutput").ap()
              for i in range(3)]
        dqh = [nc.dram_tensor("dqh", [128, 8 * 512], BF16, kind="ExternalOutput").ap(),
               nc.dram_tensor("dkh", [128, 8 * 768], BF16, kind="ExternalOutput").ap(),
               nc.dram_tensor("dvb", [128, 16 * 6 * 64], BF16, kind="ExternalOutput").ap(),
               nc.dram_tensor("drot", [128, 3200], BF16, kind="ExternalOutput").ap(),
               nc.dram_tensor("drec", [128, 16], BF16, kind="ExternalOutput").ap(),
               nc.dram_tensor("dxs", [128, 8 * 512], BF16, kind="ExternalOutput").ap(),
               nc.dram_tensor("dslot", [128, 8 * 512], BF16, kind="ExternalOutput").ap()]
        dff = [nc.dram_tensor("dxn2", [128, 8 * 1024], BF16, kind="ExternalOutput").ap(),
               nc.dram_tensor("drst", [128, 2 * 1024], F32, kind="ExternalOutput").ap(),
               nc.dram_tensor("dxs2", [128, 8 * 1024], BF16, kind="ExternalOutput").ap()]

    Aq, Bq, Ak, Bk = _l1ffn_tables()
    mstd, mf0, mf1 = _l1ffn_masks()

    NCH = 8
    CH = 512
    es = contextlib.ExitStack()
    with es:
        ones_bf = es.enter_context(nc.sbuf_tensor("d_ones_bf", [128, 128], BF16))
        ones_f = es.enter_context(nc.sbuf_tensor("d_ones_f", [128, 128], F32))
        ident = es.enter_context(nc.sbuf_tensor("d_ident", [128, 128], BF16))
        masks = es.enter_context(nc.sbuf_tensor("d_masks", [128, 3 * 384], F32))
        xcol = es.enter_context(nc.sbuf_tensor("d_xcol", [128, CT], F32))
        qsr = es.enter_context(nc.sbuf_tensor("d_qsr", [128, 2 * D], BF16))
        gbc = es.enter_context(nc.sbuf_tensor("d_gbc", [128, 2 * CT], F32))
        b1c = es.enter_context(nc.sbuf_tensor("d_b1c", [128, 2 * 22], F32))
        b2c = es.enter_context(nc.sbuf_tensor("d_b2c", [128, CT], F32))
        em.dma("sync", ones_bf[:], nc.inline_tensor(np.ones((128, 128), BF), _uname("ones_c")).ap())
        em.dma("sync", ones_f[:], nc.inline_tensor(np.ones((128, 128), np.float32), _uname("onesf_c")).ap())
        em.dma("sync", ident[:], nc.inline_tensor(np.eye(128, dtype=BF), _uname("ident_c")).ap())
        em.dma("sync", masks[:, 0:384], nc.inline_tensor(mstd, _uname("mask_std")).ap())
        em.dma("sync", masks[:, 384:768], nc.inline_tensor(mf0, _uname("mask_f0")).ap())
        em.dma("sync", masks[:, 768:1152], nc.inline_tensor(mf1, _uname("mask_f1")).ap())
        aq_d = nc.inline_tensor(Aq, _uname("rot_aq")).ap()
        bq_d = nc.inline_tensor(Bq, _uname("rot_bq")).ap()
        ak_d = nc.inline_tensor(Ak, _uname("rot_ak")).ap()
        bk_d = nc.inline_tensor(Bk, _uname("rot_bk")).ap()

        p0s = contextlib.ExitStack()
        with p0s:
            xb = p0s.enter_context(nc.sbuf_tensor("d_i_xb", [128, CT * 512], F32))
            sqb = p0s.enter_context(nc.sbuf_tensor("d_i_sqb", [128, CT * 512], BF16))
            rstd = p0s.enter_context(nc.sbuf_tensor("d_i_rstd", [128, 512], F32))
            mrstd = p0s.enter_context(nc.sbuf_tensor("d_i_mrstd", [128, 512], F32))
            tmp = p0s.enter_context(nc.sbuf_tensor("d_i_tmp", [128, 512], F32))
            ob = p0s.enter_context(nc.sbuf_tensor("d_i_ob", [128, CT * 512], F32))
            pa = p0s.enter_context(nc.psum_tensor("d_i_pa", [128, 512], F32))
            pb = p0s.enter_context(nc.psum_tensor("d_i_pb", [128, 512], F32))
            for s in range(8):
                col = slice(s * 512, (s + 1) * 512)
                em.wait("sync", "tensor")
                em.wait("sync", "vector")
                em.wait("sync", "scalar")
                em.dwait("sync", "scalar")
                for ct in range(CT):
                    em.dma("sync", xb[:, ct * 512:(ct + 1) * 512],
                           x_in[ct * 128:(ct + 1) * 128, col])
                em.dwait("tensor", "sync")
                em.dwait("scalar", "sync")
                em.dwait("vector", "sync")
                emit_ln_stats_strip(em, lambda ct: xb[:, ct * 512:(ct + 1) * 512],
                                    512, ones_bf[:], ones_f[:], sqb, pa, pb,
                                    rstd, mrstd, tmp)
                for ct in range(CT):
                    emit_ln_apply_tile(em, ob[:, ct * 512:(ct + 1) * 512],
                                       xb[:, ct * 512:(ct + 1) * 512],
                                       rstd[:], mrstd[:], tmp[:])
                vv = em.cnt["vector"]
                em.wait("scalar", "vector", vv)
                for ct in range(CT):
                    em.dma("scalar", xi[ct * 128:(ct + 1) * 128, col],
                           ob[:, ct * 512:(ct + 1) * 512])
            em.dwait("scalar", "scalar")
            em.barrier()

        def dump(i):
            if dbg:
                em.barrier()
                em.dma("sync", xd[i], xi)
                em.dwait("sync", "sync")
                em.barrier()
        dump(0)
        for li in range(6):
            _emit_d_attn_half(nc, em, li, xi, Wqkv, Wo, qs_rows,
                              aq_d, bq_d, ak_d, bk_d, masks, ident, xcol, qsr,
                              dbg_l0=(dqh if (dbg and li == 0) else None))
            if li == 0:
                dump(1)
            _emit_d_ffn_half(nc, em, li, xi, W1, W2, gb_cols, b1_cols, b2_cols,
                             ones_bf, xcol,
                             dbg_f0=(dff if (dbg and li == 0) else None))
            if li == 0:
                dump(2)

        p9s = contextlib.ExitStack()
        with p9s:
            hb = p9s.enter_context(nc.sbuf_tensor("d_f_hb", [128, CT * 512], F32))
            sqb = p9s.enter_context(nc.sbuf_tensor("d_f_sqb", [128, CT * 512], BF16))
            rstd = p9s.enter_context(nc.sbuf_tensor("d_f_rstd", [128, 512], F32))
            mrstd = p9s.enter_context(nc.sbuf_tensor("d_f_mrstd", [128, 512], F32))
            tmp = p9s.enter_context(nc.sbuf_tensor("d_f_tmp", [128, 512], F32))
            xt = p9s.enter_context(nc.sbuf_tensor("d_f_xt", [128, 512], F32))
            pa = p9s.enter_context(nc.psum_tensor("d_f_pa", [128, 512], F32))
            pb = p9s.enter_context(nc.psum_tensor("d_f_pb", [128, 512], F32))
            for s in range(8):
                col = slice(s * 512, (s + 1) * 512)
                em.wait("sync", "tensor")
                em.wait("sync", "vector")
                em.wait("sync", "scalar")
                for ct in range(CT):
                    em.dma("sync", hb[:, ct * 512:(ct + 1) * 512],
                           xi[ct * 128:(ct + 1) * 128, col])
                em.dwait("tensor", "sync")
                em.dwait("scalar", "sync")
                em.dwait("vector", "sync")
                emit_ln_stats_strip(em, lambda ct: hb[:, ct * 512:(ct + 1) * 512],
                                    512, ones_bf[:], ones_f[:], sqb, pa, pb,
                                    rstd, mrstd, tmp)
                for ct in range(CT):
                    em.wait("sync", "vector")
                    em.dwait("sync", "scalar")
                    em.dma("sync", xt[:], x_in[ct * 128:(ct + 1) * 128, col])
                    em.dwait("vector", "sync")
                    nc.vector.tensor_mul(tmp[:], hb[:, ct * 512:(ct + 1) * 512], rstd[:])
                    nc.vector.tensor_sub(tmp[:], tmp[:], mrstd[:])
                    vc = em.mark("vector", nc.vector.tensor_add(xt[:], xt[:], tmp[:]))
                    em.wait("scalar", "vector", vc)
                    em.dma("scalar", x_out[ct * 128:(ct + 1) * 128, col], xt[:])
                    em.dwait("vector", "scalar")
            em.dwait("scalar", "scalar")
            em.barrier()
    return


def _emit_d_attn_half(nc, em, li, xi, Wqkv, Wo, qs_rows, aq_d, bq_d, ak_d, bk_d,
                      masks, ident, xcol, qsr, dbg_l0=None):
    CH = 512
    em.barrier()
    ps = contextlib.ExitStack()
    P = lambda n: f"a{li}_{n}"
    with ps:
        xt2 = ps.enter_context(nc.sbuf_tensor(P("xt2"), [128, 2 * CH], F32))
        xs = ps.enter_context(nc.sbuf_tensor(P("xs"), [128, CT * CH], BF16))
        wsl = ps.enter_context(nc.sbuf_tensor(P("wsl"), [128, 2 * CT * 512], BF16))
        qc = ps.enter_context(nc.sbuf_tensor(P("qc"), [128, 4 * 1024], BF16))
        kc = ps.enter_context(nc.sbuf_tensor(P("kc"), [128, 4 * 1024], BF16))
        vc_sb = ps.enter_context(nc.sbuf_tensor(P("vc"), [128, 4 * 1024], BF16))
        sqs = ps.enter_context(nc.sbuf_tensor(P("sqs"), [128, 1024], BF16))
        nrm = ps.enter_context(nc.sbuf_tensor(P("nrm"), [128, 16], F32))
        rec = ps.enter_context(nc.sbuf_tensor(P("rec"), [128, 16], BF16))
        qn_scr = ps.enter_context(nc.sbuf_tensor(P("qns"), [128, 1024], BF16))
        qhat = ps.enter_context(nc.sbuf_tensor(P("qhat"), [128, 1024], BF16))
        khat = ps.enter_context(nc.sbuf_tensor(P("khat"), [128, 1024], BF16))
        rots = ps.enter_context(nc.sbuf_tensor(P("rots"), [128, 4 * 64], BF16))
        qhT = ps.enter_context(nc.sbuf_tensor(P("qhT"), [128, 8 * 512], BF16))
        khT = ps.enter_context(nc.sbuf_tensor(P("khT"), [128, 8 * 768], BF16))
        vbuf = ps.enter_context(nc.sbuf_tensor(P("vbuf"), [128, 16 * 6 * 64], BF16))
        simb = ps.enter_context(nc.sbuf_tensor(P("simb"), [128, 2 * 384], F32))
        Pb = ps.enter_context(nc.sbuf_tensor(P("Pb"), [128, 2 * 384], BF16))
        PTb = ps.enter_context(nc.sbuf_tensor(P("PTb"), [128, 2 * 384], BF16))
        mxd = ps.enter_context(nc.sbuf_tensor(P("mxd"), [128, 8], F32))
        atT = ps.enter_context(nc.sbuf_tensor(P("atT"), [128, CT * 512], BF16))
        u_sb = ps.enter_context(nc.sbuf_tensor(P("u"), [128, 2 * 512], F32))
        pA = ps.enter_context(nc.psum_tensor(P("pA"), [128, 512], F32))
        pB = ps.enter_context(nc.psum_tensor(P("pB"), [128, 512], F32))
        pC = ps.enter_context(nc.psum_tensor(P("pC"), [128, 512], F32))
        tb1 = ps.enter_context(nc.psum_tensor(P("tb1"), [128, 512], BF16))
        tb2 = ps.enter_context(nc.psum_tensor(P("tb2"), [128, 512], BF16))
        pv = ps.enter_context(nc.psum_tensor(P("pv"), [64, 512], F32))

        em.wait("gpsimd", "vector")
        em.dma("gpsimd", qsr[:, 0:D], qs_rows[li, 0:1, :].partition_broadcast(128))
        em.dma("gpsimd", qsr[:, D:2 * D], qs_rows[li, 1:2, :].partition_broadcast(128))
        gbank = [Bank(pA[:]), Bank(pB[:])]
        wslot = [wsl[:, 0:CT * 512], wsl[:, CT * 512:2 * CT * 512]]
        wqtok = [0, 0]
        tbank = [Bank(tb1[:]), Bank(tb2[:])]
        em.dwait("vector", "gpsimd")
        em.wait("vector", "tensor")
        nc.vector.memset(khT[:], 0.0)
        em.mark("vector", nc.vector.memset(vbuf[:], 0.0))

        for ch in range(8):
            t0 = ch * CH
            em.wait("sync", "tensor")
            em.wait("sync", "vector")
            em.wait("vector", "tensor")
            for ct in range(CT):
                if ct >= 2:
                    em.wait("sync", "vector")
                em.dma("sync", xt2[:, (ct % 2) * CH:(ct % 2 + 1) * CH],
                       xi[ct * 128:(ct + 1) * 128, t0:t0 + CH])
                em.dwait("vector", "sync")
                xtile = xt2[:, (ct % 2) * CH:(ct % 2 + 1) * CH]
                dst = xs[:, ct * CH:(ct + 1) * CH]
                if ct < 4:
                    vcp = nc.vector.tensor_copy(dst, xtile)
                else:
                    if ch == 0:
                        nc.vector.memset(dst[:, 0:1], 0.0)
                    else:
                        nc.vector.tensor_copy(dst[:, 0:1], xcol[:, ct:ct + 1])
                    vcp = nc.vector.tensor_copy(dst[:, 1:CH], xtile[:, 0:CH - 1])
                em.mark("vector", vcp)
                if ct >= 4:
                    em.mark("vector", nc.vector.tensor_copy(
                        xcol[:, ct:ct + 1], xtile[:, CH - 1:CH]))
            if dbg_l0 and ch == 0:
                em.barrier()
                em.dma("sync", dbg_l0[5], xs[:])
                em.dwait("sync", "sync")
                em.barrier()
            for ns in range(6):
                sl = ns % 2
                slot = wslot[sl]
                em.wait("gpsimd", "tensor", wqtok[sl])
                dc = em.dma("gpsimd", slot.rearrange("p (k c) -> p k c", c=512),
                       Wqkv[li].rearrange("(k p) m -> p k m", p=128)
                       [:, :, ns * 512:(ns + 1) * 512])
                em.dwait("tensor", "gpsimd", dc)
                em.wait("tensor", "vector")
                dstbuf = (qc, kc, vc_sb)[ns // 2]
                for tt in range(4):
                    bank = gbank[tt % 2]
                    pap = bank.acquire(em)
                    for k in range(CT):
                        mm = nc.tensor.matmul(
                            pap, xs[:, k * CH + tt * 128: k * CH + (tt + 1) * 128],
                            slot[:, k * 512:(k + 1) * 512],
                            start=(k == 0), stop=(k == CT - 1))
                    mmc = em.mark("tensor", mm)
                    wqtok[sl] = mmc
                    em.wait("vector", "tensor", mmc)
                    vcp = em.mark("vector", nc.vector.tensor_copy(
                        dstbuf[:, tt * 1024 + (ns % 2) * 512: tt * 1024 + (ns % 2 + 1) * 512],
                        pap))
                    bank.release("vector", vcp)
            for tt in range(4):
                for (buf, hat, a_d, b_d, qoff) in ((qc, qhat, aq_d, bq_d, 0),
                                                   (kc, khat, ak_d, bk_d, D)):
                    blk = buf[:, tt * 1024:(tt + 1) * 1024]
                    r0 = t0 + tt * 128
                    em.wait("sync", "vector")
                    em.dma("sync", rots[:, 0:64], a_d[r0:r0 + 128, :])
                    em.dma("sync", rots[:, 64:128], b_d[r0:r0 + 128, :])
                    em.wait("scalar", "vector")
                    sc = em.mark("scalar", nc.scalar.activation(sqs[:], blk, AF.Square))
                    em.wait("vector", "scalar", sc)
                    em.mark("vector", nc.vector.reduce_sum(
                        nrm[:], sqs[:].rearrange("p (h d) -> p h d", d=64), axis=AX.X))
                    em.wait("vector", "vector")
                    em.wait("scalar", "vector")
                    em.mark("scalar", nc.scalar.activation(nrm[:], nrm[:], AF.Ln,
                                                            bias=1e-24))
                    em.wait("scalar", "scalar")
                    sc = em.mark("scalar", nc.scalar.activation(
                        rec[:], nrm[:], AF.Exp, scale=-0.5))
                    em.wait("vector", "scalar", sc)
                    em.dwait("vector", "sync")
                    nc.vector.tensor_tensor(
                        qn_scr[:], blk,
                        rec[:].unsqueeze(2).broadcast_to([128, 16, 64]),
                        ALU.mult)
                    em.mark("vector", nc.vector.tensor_tensor(
                        qn_scr[:], qn_scr[:], qsr[:, qoff:qoff + D], ALU.mult))
                    em.wait("vector", "vector")
                    nc.vector.tensor_tensor(
                        sqs[:], qn_scr[:],
                        rots[:, 0:64].unsqueeze(1).broadcast_to([128, 16, 64]),
                        ALU.mult)
                    q3 = qn_scr[:].rearrange("p (h d) -> p h d", d=64)
                    h3o = hat[:].rearrange("p (h d) -> p h d", d=64)
                    nc.vector.tensor_tensor(
                        h3o[:, :, 0:32], q3[:, :, 32:64],
                        rots[:, 64:96].unsqueeze(1).broadcast_to([128, 16, 32]),
                        ALU.mult)
                    em.mark("vector", nc.vector.tensor_tensor(
                        h3o[:, :, 32:64], q3[:, :, 0:32],
                        rots[:, 96:128].unsqueeze(1).broadcast_to([128, 16, 32]),
                        ALU.mult))
                    em.wait("vector", "vector")
                    em.mark("vector", nc.vector.tensor_add(hat[:], hat[:], sqs[:]))
                    if dbg_l0 and ch == 0 and tt == 0 and buf is qc:
                        em.barrier()
                        em.dma("sync", dbg_l0[3][:, 0:1024], qc[:, 0:1024])
                        em.dma("sync", dbg_l0[3][:, 1024:2048], qn_scr[:])
                        em.dma("sync", dbg_l0[3][:, 2048:3072], qhat[:])
                        em.dma("sync", dbg_l0[3][:, 3072:3136], rots[:, 0:64])
                        em.dma("sync", dbg_l0[3][:, 3136:3200], rots[:, 64:128])
                        em.dma("sync", dbg_l0[4][:, 0:16], rec[:])
                        em.dwait("sync", "sync")
                        em.barrier()
                em.wait("tensor", "vector")
                for (hat2, dstT, wide, off) in ((qhat, qhT, 512, tt * 128),
                                                (khat, khT, 768, 256 + tt * 128)):
                    for p in range(8):
                        bank = tbank[p % 2]
                        pap = bank.acquire(em)
                        for hh in range(2):
                            h = p * 2 + hh
                            tr = nc.tensor.transpose(
                                pap[hh * 64:hh * 64 + 64, 0:128],
                                hat2[:, h * 64:(h + 1) * 64], ident[:])
                        tc = em.mark("tensor", tr)
                        em.wait("vector", "tensor", tc)
                        vcp = em.mark("vector", nc.vector.tensor_copy(
                            dstT[:, p * wide + off: p * wide + off + 128],
                            pap[:, 0:128]))
                        bank.release("vector", vcp)
                em.mark("vector", nc.vector.tensor_copy(
                    vbuf[:].rearrange("p (h s d) -> p h s d", s=6, d=64)[:, :, 2 + tt],
                    vc_sb[:, tt * 1024:(tt + 1) * 1024].rearrange("p (h d) -> p h d", d=64)))
            if dbg_l0 and ch == 0:
                em.barrier()
                em.dma("sync", dbg_l0[0], qhT[:])
                em.dma("sync", dbg_l0[1], khT[:])
                em.dma("sync", dbg_l0[2], vbuf[:])
                em.dwait("sync", "sync")
                em.barrier()
            em.wait("vector", "scalar")
            sim_tok = {}
            NIT = 64
            def a_sim(it):
                h, qt = it // 4, it % 4
                p_, hh = h // 2, h % 2
                bank = gbank[it % 2]
                pap = bank.acquire(em)
                mm = nc.tensor.matmul(
                    pap[:, 0:384],
                    qhT[hh * 64:hh * 64 + 64, p_ * 512 + qt * 128: p_ * 512 + (qt + 1) * 128],
                    khT[hh * 64:hh * 64 + 64, p_ * 768 + qt * 128: p_ * 768 + qt * 128 + 384],
                    start=True, stop=True)
                sim_tok[it] = (em.mark("tensor", mm), bank)
            def a_rest(it):
                h, qt = it // 4, it % 4
                p_, hh = h // 2, h % 2
                mmc, bank = sim_tok.pop(it)
                pap = bank.ap
                sl = it % 2
                if ch == 0 and qt == 0:
                    mk = masks[:, 384:768]
                elif ch == 0 and qt == 1:
                    mk = masks[:, 768:1152]
                else:
                    mk = masks[:, 0:384]
                em.wait("vector", "tensor", mmc)
                em.wait("vector", "scalar")
                vcp = em.mark("vector", nc.vector.scalar_tensor_tensor(
                    simb[:, sl * 384:(sl + 1) * 384], pap[:, 0:384], 60.0, mk,
                    ALU.min, ALU.add))
                bank.release("vector", vcp)
                em.wait("scalar", "vector", vcp)
                sc = em.mark("scalar", nc.scalar.activation(
                    Pb[:, sl * 384:(sl + 1) * 384], simb[:, sl * 384:(sl + 1) * 384],
                    AF.Exp, bias=-20.0))
                em.wait("vector", "scalar", sc)
                em.mark("vector", nc.vector.reduce_sum(
                    mxd[:, 2 + sl:3 + sl], Pb[:, sl * 384:(sl + 1) * 384], axis=AX.X))
                em.wait("vector", "vector")
                em.mark("vector", nc.vector.reciprocal(
                    mxd[:, 4 + sl:5 + sl], mxd[:, 2 + sl:3 + sl]))
                em.wait("vector", "vector")
                em.mark("vector", nc.vector.tensor_scalar(
                    Pb[:, sl * 384:(sl + 1) * 384], Pb[:, sl * 384:(sl + 1) * 384],
                    mxd[:, 4 + sl:5 + sl], None, ALU.mult))
                em.wait("tensor", "vector")
                bankt = tbank[sl]
                papt = bankt.acquire(em)
                for jj in range(3):
                    tr = nc.tensor.transpose(
                        papt[:, jj * 128:(jj + 1) * 128],
                        Pb[:, sl * 384 + jj * 128: sl * 384 + (jj + 1) * 128], ident[:])
                tc = em.mark("tensor", tr)
                em.wait("vector", "tensor", tc)
                vcp = em.mark("vector", nc.vector.tensor_copy(
                    PTb[:, sl * 384:(sl + 1) * 384], papt[:, 0:384]))
                bankt.release("vector", vcp)
                em.wait("tensor", "vector", vcp)
                for jj in range(3):
                    mm = nc.tensor.matmul(
                        pv[:, qt * 128:(qt + 1) * 128],
                        vbuf[:, h * 384 + (qt + jj) * 64: h * 384 + (qt + jj + 1) * 64],
                        PTb[:, sl * 384 + jj * 128: sl * 384 + (jj + 1) * 128],
                        start=(jj == 0), stop=(jj == 2))
                em.mark("tensor", mm)
                if qt == 3:
                    em.wait("vector", "tensor")
                    vcp = em.mark("vector", nc.vector.tensor_copy(
                        atT[hh * 64:hh * 64 + 64, p_ * 512:(p_ + 1) * 512], pv[:]))
                    em.wait("tensor", "vector", vcp)
            for it in range(NIT + 1):
                if it < NIT:
                    a_sim(it)
                if it > 0:
                    a_rest(it - 1)
            em.wait("vector", "tensor")
            em.mark("vector", nc.vector.tensor_copy(
                khT[:].rearrange("p (g c) -> p g c", c=768)[:, :, 0:256],
                khT[:].rearrange("p (g c) -> p g c", c=768)[:, :, 512:768]))
            em.mark("vector", nc.vector.tensor_copy(
                vbuf[:].rearrange("p (h s d) -> p h s d", s=6, d=64)[:, :, 0:2],
                vbuf[:].rearrange("p (h s d) -> p h s d", s=6, d=64)[:, :, 4:6]))
            for ct in range(CT):
                sl = ct % 2
                slot = wslot[sl]
                em.wait("gpsimd", "tensor", wqtok[sl])
                dc = em.dma("gpsimd", slot[:, 0:CT * 128].rearrange("p (k c) -> p k c", c=128),
                       Wo[li].rearrange("(k p) m -> p k m", p=128)
                       [:, :, ct * 128:(ct + 1) * 128])
                em.dwait("tensor", "gpsimd", dc)
                bank = gbank[ct % 2]
                pap = bank.acquire(em)
                for k in range(CT):
                    mm = nc.tensor.matmul(pap, slot[:, k * 128:(k + 1) * 128],
                                          atT[:, k * 512:(k + 1) * 512],
                                          start=(k == 0), stop=(k == CT - 1))
                mmc = em.mark("tensor", mm)
                wqtok[sl] = mmc
                em.wait("sync", "vector")
                em.dwait("sync", "scalar")
                em.dma("sync", xt2[:, 0:CH], xi[ct * 128:(ct + 1) * 128, t0:t0 + CH])
                em.wait("vector", "tensor", mmc)
                em.dwait("vector", "sync")
                em.dwait("vector", "scalar")
                uu = u_sb[:, (ct % 2) * 512:(ct % 2 + 1) * 512]
                vcp = em.mark("vector", nc.vector.tensor_add(uu, pap, xt2[:, 0:CH]))
                bank.release("vector", vcp)
                em.wait("scalar", "vector", vcp)
                em.dma("scalar", xi[ct * 128:(ct + 1) * 128, t0:t0 + CH], uu)
    return


def _emit_d_ffn_half(nc, em, li, xi, W1, W2, gb_cols, b1_cols, b2_cols,
                     ones_bf, xcol, dbg_f0=None):
    ST = 1024
    em.barrier()
    ps = contextlib.ExitStack()
    P = lambda n: f"f{li}_{n}"
    with ps:
        xt2 = ps.enter_context(nc.sbuf_tensor(P("xt2"), [128, 2 * ST], F32))
        xs2 = ps.enter_context(nc.sbuf_tensor(P("xs2"), [128, CT * ST], BF16))
        xn2 = ps.enter_context(nc.sbuf_tensor(P("xn2"), [128, CT * ST], BF16))
        sqb = ps.enter_context(nc.sbuf_tensor(P("sqb"), [128, CT * 512], BF16))
        rstd = ps.enter_context(nc.sbuf_tensor(P("rstd"), [128, ST], F32))
        mrstd = ps.enter_context(nc.sbuf_tensor(P("mrstd"), [128, ST], F32))
        tmp = ps.enter_context(nc.sbuf_tensor(P("tmp"), [128, ST], F32))
        h2 = ps.enter_context(nc.sbuf_tensor(P("h2"), [128, 22 * ST], BF16))
        w1a = ps.enter_context(nc.sbuf_tensor(P("w1a"), [128, 2 * CT * 128], BF16))
        w1g = ps.enter_context(nc.sbuf_tensor(P("w1g"), [128, 2 * CT * 128], BF16))
        w2s = ps.enter_context(nc.sbuf_tensor(P("w2s"), [128, 2 * 22 * 128], BF16))
        gact = ps.enter_context(nc.sbuf_tensor(P("gact"), [128, 2 * 512], BF16))
        u_sb = ps.enter_context(nc.sbuf_tensor(P("u"), [128, 2 * 512], F32))
        pa = ps.enter_context(nc.psum_tensor(P("pa"), [128, 512], F32))
        pb = ps.enter_context(nc.psum_tensor(P("pb"), [128, 512], F32))
        pg = ps.enter_context(nc.psum_tensor(P("pg"), [128, 512], F32))
        pa2 = ps.enter_context(nc.psum_tensor(P("pa2"), [128, 512], F32))
        pg2 = ps.enter_context(nc.psum_tensor(P("pg2"), [128, 512], F32))
        pout = ps.enter_context(nc.psum_tensor(P("po"), [128, 512], F32))
        pout2 = ps.enter_context(nc.psum_tensor(P("po2"), [128, 512], F32))
        gbc2 = ps.enter_context(nc.sbuf_tensor(P("gbc2"), [128, 2 * CT], F32))
        b1c2 = ps.enter_context(nc.sbuf_tensor(P("b1c2"), [128, 2 * 22], F32))
        b2c2 = ps.enter_context(nc.sbuf_tensor(P("b2c2"), [128, CT], F32))

        em.dma("sync", gbc2[:].rearrange("p (a c) -> p a c", c=CT), gb_cols[li].transpose([1, 0, 2]))
        em.dma("sync", b1c2[:].rearrange("p (a c) -> p a c", c=22), b1_cols[li].transpose([1, 0, 2]))
        em.dma("sync", b2c2[:], b2_cols[li])
        w1slot = [(w1a[:, 0:CT * 128], w1g[:, 0:CT * 128]),
                  (w1a[:, CT * 128:2 * CT * 128], w1g[:, CT * 128:2 * CT * 128])]
        w2slot = [w2s[:, 0:22 * 128], w2s[:, 22 * 128:2 * 22 * 128]]
        w1tok = [0, 0]
        w2tok = [0, 0]
        obank = [Bank(pout[:]), Bank(pout2[:])]
        em.dwait("vector", "sync")
        em.dwait("scalar", "sync")

        for s in range(4):
            t0 = s * ST
            em.wait("sync", "tensor")
            em.wait("sync", "vector")
            em.wait("sync", "scalar")
            em.dwait("sync", "scalar")
            em.wait("vector", "tensor")
            for ct in range(CT):
                if ct >= 2:
                    em.wait("sync", "vector")
                em.dma("sync", xt2[:, (ct % 2) * ST:(ct % 2 + 1) * ST],
                       xi[ct * 128:(ct + 1) * 128, t0:t0 + ST])
                em.dwait("vector", "sync")
                xtile = xt2[:, (ct % 2) * ST:(ct % 2 + 1) * ST]
                dst = xs2[:, ct * ST:(ct + 1) * ST]
                if ct < 4:
                    vcp = nc.vector.tensor_copy(dst, xtile)
                else:
                    if s == 0:
                        nc.vector.memset(dst[:, 0:1], 0.0)
                    else:
                        nc.vector.tensor_copy(dst[:, 0:1], xcol[:, ct:ct + 1])
                    vcp = nc.vector.tensor_copy(dst[:, 1:ST], xtile[:, 0:ST - 1])
                em.mark("vector", vcp)
            for half in range(2):
                hc = slice(half * 512, (half + 1) * 512)
                emit_ln_stats_strip(
                    em, lambda ct: xs2[:, ct * ST + half * 512: ct * ST + (half + 1) * 512],
                    512, ones_bf[:], None, sqb, pa, pb,
                    rstd[:, hc], mrstd[:, hc], tmp[:, hc],
                    x_dtype_bf=True)
            for ct in range(CT):
                nc.vector.tensor_mul(tmp[:], xs2[:, ct * ST:(ct + 1) * ST], rstd[:])
                nc.vector.tensor_sub(tmp[:], tmp[:], mrstd[:])
                em.mark("vector", nc.vector.tensor_scalar(
                    xn2[:, ct * ST:(ct + 1) * ST], tmp[:],
                    gbc2[:, ct:ct + 1], gbc2[:, CT + ct:CT + ct + 1],
                    ALU.mult, ALU.add))
            if dbg_f0 and s == 0:
                em.barrier()
                em.dma("sync", dbg_f0[0], xn2[:])
                em.dma("sync", dbg_f0[1][:, 0:ST], rstd[:])
                em.dma("sync", dbg_f0[1][:, ST:2 * ST], mrstd[:])
                em.dma("sync", dbg_f0[2], xs2[:])
                em.dwait("sync", "sync")
                em.barrier()
            w1banks = [(pa, pg), (pa2, pg2)]
            w1ev = [[em.cnt["vector"], em.cnt["scalar"]],
                    [em.cnt["vector"], em.cnt["scalar"]]]
            for kk in range(22):
                rows = min(128, 2730 - kk * 128)
                sl = kk % 2
                sa, sg = w1slot[sl]
                em.wait("gpsimd", "tensor", w1tok[sl])
                em.dma("gpsimd", sa.rearrange("p (k c) -> p k c", c=128)[:, :, 0:rows],
                       W1[li].rearrange("(k p) m -> p k m", p=128)
                       [:, :, kk * 128:kk * 128 + rows])
                dc = em.dma("gpsimd", sg.rearrange("p (k c) -> p k c", c=128)[:, :, 0:rows],
                       W1[li].rearrange("(k p) m -> p k m", p=128)
                       [:, :, 2730 + kk * 128:2730 + kk * 128 + rows])
                em.dwait("tensor", "gpsimd", dc)
                for sub in range(2):
                    em.wait("tensor", "vector")
                    em.wait("tensor", "scalar")
                    for k in range(CT):
                        mma = nc.tensor.matmul(
                            pa[0:rows, :], sa[:, k * 128:k * 128 + rows],
                            xn2[:, k * ST + sub * 512: k * ST + (sub + 1) * 512],
                            start=(k == 0), stop=(k == CT - 1))
                    for k in range(CT):
                        mmg = nc.tensor.matmul(
                            pg[0:rows, :], sg[:, k * 128:k * 128 + rows],
                            xn2[:, k * ST + sub * 512: k * ST + (sub + 1) * 512],
                            start=(k == 0), stop=(k == CT - 1))
                    mmc = em.mark("tensor", mmg)
                    w1tok[sl] = mmc
                    em.wait("scalar", "tensor", mmc)
                    sc = em.mark("scalar", nc.scalar.activation(
                        gact[0:rows, 0:512], pg[0:rows, :], AF.Gelu,
                        bias=b1c2[0:rows, 22 + kk:22 + kk + 1]))
                    em.wait("vector", "scalar", sc)
                    em.mark("vector", nc.vector.scalar_tensor_tensor(
                        h2[0:rows, kk * ST + sub * 512: kk * ST + (sub + 1) * 512],
                        pa[0:rows, :], b1c2[0:rows, kk:kk + 1],
                        gact[0:rows, 0:512],
                        ALU.add, ALU.mult))
            for ct in range(CT):
                sl2 = ct % 2
                slot = w2slot[sl2]
                em.wait("gpsimd", "tensor", w2tok[sl2])
                em.dma("gpsimd", slot.rearrange("p (k c) -> p k c", c=128)[:, 0:21, :],
                       W2[li][0:2688, :].rearrange("(k p) m -> p k m", p=128)
                       [:, :, ct * 128:(ct + 1) * 128])
                dc = em.dma("gpsimd", slot.rearrange("p (k c) -> p k c", c=128)[0:42, 21:22, :],
                       W2[li][2688:2730, ct * 128:(ct + 1) * 128].unsqueeze(1))
                em.dwait("tensor", "gpsimd", dc)
                for sub in range(2):
                    bank = obank[sub]
                    pap = bank.acquire(em)
                    em.wait("tensor", "vector")
                    for kk in range(22):
                        rows = min(128, 2730 - kk * 128)
                        mm = nc.tensor.matmul(
                            pap, slot[0:rows, kk * 128:(kk + 1) * 128],
                            h2[0:rows, kk * ST + sub * 512: kk * ST + (sub + 1) * 512],
                            start=(kk == 0), stop=(kk == 21))
                    mmc = em.mark("tensor", mm)
                    w2tok[sl2] = mmc
                    em.wait("sync", "vector")
                    em.dwait("sync", "scalar")
                    em.dma("sync", xt2[:, sub * 512:(sub + 1) * 512],
                           xi[ct * 128:(ct + 1) * 128, t0 + sub * 512:t0 + (sub + 1) * 512])
                    em.wait("vector", "tensor", mmc)
                    em.dwait("vector", "sync")
                    em.dwait("vector", "scalar")
                    xtile = xt2[:, sub * 512:(sub + 1) * 512]
                    if sub == 1:
                        em.mark("vector", nc.vector.tensor_copy(
                            xcol[:, ct:ct + 1], xtile[:, 511:512]))
                    uu = u_sb[:, sub * 512:(sub + 1) * 512]
                    vcp = em.mark("vector", nc.vector.scalar_tensor_tensor(
                        uu, pap, b2c2[:, ct:ct + 1], xtile, ALU.add, ALU.add))
                    bank.release("vector", vcp)
                    em.wait("scalar", "vector", vcp)
                    em.dma("scalar", xi[ct * 128:(ct + 1) * 128,
                                        t0 + sub * 512:t0 + (sub + 1) * 512], uu)
    return


def run_l1_ffn_unfused(xT_list, I):
    qs_rows = np.stack([np.stack([np.tile(I["L1_ffn_q_scale"][i], 16),
                                  np.tile(I["L1_ffn_k_scale"][i], 16)])
                        for i in range(6)]).astype(np.float32)
    gb = np.stack([np.stack([_cols(I["L1_ffn_ln_g"][i]), _cols(I["L1_ffn_ln_b"][i])])
                   for i in range(6)]).astype(np.float32)
    b1c = np.zeros((6, 2, 128, 22), np.float32)
    for i in range(6):
        a = I["L1_ffn_b1"][i][:2730]
        g = I["L1_ffn_b1"][i][2730:]
        ap = np.zeros(22 * 128, np.float32); ap[:2730] = a
        gp = np.zeros(22 * 128, np.float32); gp[:2730] = g
        b1c[i, 0] = ap.reshape(22, 128).T
        b1c[i, 1] = gp.reshape(22, 128).T
    b2c = np.stack([_cols(I["L1_ffn_b2"][i]) for i in range(6)]).astype(np.float32)
    nc = build_d()
    ins = [dict(x=xT, Wqkv=I["L1_ffn_Wqkv"], Wo_d=I["L1_ffn_Wo"], W1=I["L1_ffn_W1"],
                W2=I["L1_ffn_W2"], qs_rows=qs_rows, gb_cols=gb, b1_cols=b1c,
                b2_cols=b2c) for xT in xT_list]
    r = run_bass_kernel_spmd(nc, ins, core_ids=list(range(NCORES)))
    return [res["xo"] for res in r.results]



def _t(a):
    return np.ascontiguousarray(np.asarray(a).T)


def _cols(v):
    return np.ascontiguousarray(np.asarray(v).reshape(CT, 128).T)


def run_l0_ffn_unfused(xT_list, I):
    nc1 = build_b1()
    r1 = run_bass_kernel_spmd(nc1, [{"x": xT} for xT in xT_list],
                              core_ids=list(range(NCORES)))
    xn_list = [r["xn"] for r in r1.results]

    bcols = np.zeros((128, 4 * CT), np.float32)
    bcols[:, 0:CT] = _cols(I["L0_ffn_bf"])
    bcols[:, CT:2 * CT] = _cols(I["L0_ffn_bi"])
    bcols[:, 2 * CT:3 * CT] = _cols(I["L0_ffn_bc"])
    bcols[:, 3 * CT:4 * CT] = _cols(I["L0_ffn_in_c"])
    wmap = {
        "wfT": _t(I["L0_ffn_wf"]), "ufT": _t(I["L0_ffn_uf"]),
        "wiT": _t(I["L0_ffn_wi"]), "uiT": _t(I["L0_ffn_ui"]),
        "wcT": _t(I["L0_ffn_wc"]), "ucT": _t(I["L0_ffn_uc"]),
        "uoT": _t(I["L0_ffn_uo"]), "bcols": bcols,
    }
    nc2 = build_b2()
    in2 = []
    for s in range(NCORES):
        xg = np.stack([xn_list[b][:, s * TS:(s + 1) * TS] for b in range(NCORES)])
        in2.append(dict(xg=np.ascontiguousarray(xg), **wmap))
    r2 = run_bass_kernel_spmd(nc2, in2, core_ids=list(range(NCORES)))
    UT = np.concatenate([r["UT"] for r in r2.results], axis=1)

    nc3 = build_b3()
    in3 = [dict(x=xT_list[b], xn=xn_list[b], UT=UT, woT=_t(I["L0_ffn_wo"]),
                bo_col=_cols(I["L0_ffn_bo"])) for b in range(NCORES)]
    r3 = run_bass_kernel_spmd(nc3, in3, core_ids=list(range(NCORES)))
    return [r["xo"] for r in r3.results]


def build_fused():
    nc = bass.Bass("TRN2", target_bir_lowering=False, debug=False)
    em = EM(nc)
    f32in = lambda n, s: nc.dram_tensor(n, s, F32, kind="ExternalInput").ap()
    x_in = f32in("x", [D, T])
    A_IO = {n: f32in(n, s) for n, s in (
        ("W_disp", [D, 256]), ("W_val", [D, D]), ("W_cproj", [D, D]),
        ("relT", [16, 64]), ("W_strain", [16, 16]), ("W_pos", [16, 16]),
        ("W_bond", [16, 1]), ("W_dmg", [16, 16]), ("W_dmg_out", [16, 1]),
        ("bdmg_rep", [128, 1]), ("bdo_rep", [128, 1]))}
    w_names = ["wfT", "ufT", "wiT", "uiT", "wcT", "ucT", "uoT"]
    B_wd = {n: f32in(n, [D, D]) for n in w_names}
    b_bcols = f32in("b_bcols", [128, 4 * CT])
    woT = f32in("woT", [D, D])
    b_bo_col = f32in("b_bo_col", [128, CT])
    C_IO = {n: f32in(n, s) for n, s in (
        ("Wq", [D, D]), ("Wk", [D, D]), ("Wv", [D, D]), ("Wr", [D, 512]),
        ("Wo", [D, D]), ("bcols", [128, 36]))}
    D_IO = {n: f32in(n, s) for n, s in (
        ("Wqkv", [6, D, 3072]), ("Wo_d", [6, D, D]), ("W1", [6, D, 5460]),
        ("W2", [6, 2730, D]), ("qs_rows", [6, 2, D]), ("gb_cols", [6, 2, 128, CT]),
        ("b1_cols", [6, 2, 128, 22]), ("b2_cols", [6, 128, CT]))}
    x_out = nc.dram_tensor("xo", [D, T], F32, kind="ExternalOutput").ap()

    x1 = nc.dram_tensor("f_x1", [D, T], F32).ap()
    x2 = nc.dram_tensor("f_x2", [D, T], F32).ap()
    x3 = nc.dram_tensor("f_x3", [D, T], F32).ap()
    a2a_in = nc.dram_tensor("f_a2ai", [NCORES, D, TS], BF16)
    a2a_out = nc.dram_tensor("f_a2ao", [NCORES, D, TS], BF16)
    ag_in = nc.dram_tensor("f_agi", [D, TS], F32)
    ag_out = nc.dram_tensor("f_ago", [NCORES * D, TS], F32)
    csem = nc.alloc_semaphore("f_csem")

    emit_const_pool(nc, em, None, [EPS, 1e-24, -30.0, -20.0, 1e-30], alloc=nc)
    emit_a(nc, em, x_in, x1, A_IO)
    emit_b1(nc, em, x1, None,
            xn_fn=lambda ct, s: a2a_in.ap()[s, ct * 128:(ct + 1) * 128, :])
    em.dwait("gpsimd", "scalar")
    with allow_ici_all_to_all():
        nc.gpsimd.collective_compute(
            "AllToAll", ALU.bypass, replica_groups=[list(range(NCORES))],
            ins=[a2a_in.ap().opt()], outs=[a2a_out.ap().opt()]).then_inc(csem)
    for e in ("tensor", "vector", "scalar", "sync", "gpsimd"):
        em.eng(e).wait_ge(csem, 1)
    emit_b2(nc, em, a2a_out.ap(), B_wd, b_bcols, ag_in.ap())
    em.dwait("gpsimd", "scalar")
    nc.gpsimd.collective_compute(
        "AllGather", ALU.bypass, replica_groups=[list(range(NCORES))],
        ins=[ag_in.ap().opt()], outs=[ag_out.ap().opt()]).then_inc(csem)
    for e in ("tensor", "vector", "scalar", "sync", "gpsimd"):
        em.eng(e).wait_ge(csem, 2)
    emit_b3(nc, em, x1, None, None, woT, b_bo_col, x2,
            xn_fn=lambda ct, s: a2a_in.ap()[s, ct * 128:(ct + 1) * 128, :],
            ut_fn=lambda ct, s: ag_out.ap()[s * CT * 128 + ct * 128:
                                            s * CT * 128 + (ct + 1) * 128, :])
    emit_c(nc, em, x2, x3, C_IO)
    emit_d(nc, em, x3, x_out, D_IO)
    return nc


def _fused_inputs(I, b):
    x = I["L0_attn_x"]
    m = dict(x=np.ascontiguousarray(x[b].T))
    for n in ("W_disp", "W_val", "W_cproj", "W_strain", "W_pos", "W_bond",
              "W_dmg", "W_dmg_out"):
        m[n] = I[f"L0_attn_{n}"]
    m["relT"] = np.ascontiguousarray(I["L0_attn_rel_pos_emb"].T)
    m["bdmg_rep"] = np.ascontiguousarray(np.tile(I["L0_attn_b_dmg"], 8)[:, None]).astype(np.float32)
    m["bdo_rep"] = np.ascontiguousarray(np.tile(I["L0_attn_b_dmg_out"], 128)[:, None]).astype(np.float32)
    for n, s in (("wfT", "wf"), ("ufT", "uf"), ("wiT", "wi"), ("uiT", "ui"),
                 ("wcT", "wc"), ("ucT", "uc"), ("uoT", "uo"), ("woT", "wo")):
        m[n] = _t(I[f"L0_ffn_{s}"])
    bcols = np.zeros((128, 4 * CT), np.float32)
    bcols[:, 0:CT] = _cols(I["L0_ffn_bf"]); bcols[:, CT:2 * CT] = _cols(I["L0_ffn_bi"])
    bcols[:, 2 * CT:3 * CT] = _cols(I["L0_ffn_bc"]); bcols[:, 3 * CT:] = _cols(I["L0_ffn_in_c"])
    m["b_bcols"] = bcols
    m["b_bo_col"] = _cols(I["L0_ffn_bo"])
    for n in ("Wq", "Wk", "Wv", "Wr", "Wo"):
        m[n] = I[f"L1_attn_{n}"]
    cb = np.zeros((128, 36), np.float32)
    cb[:, 0:8] = _cols(I["L1_attn_bq"]); cb[:, 8:16] = _cols(I["L1_attn_bk"])
    cb[:, 16:24] = _cols(I["L1_attn_bv"]); cb[:, 24:32] = _cols(I["L1_attn_bo"])
    cb[:, 32:36] = np.ascontiguousarray(I["L1_attn_br"].reshape(4, 128).T)
    m["bcols"] = cb
    m["Wqkv"] = I["L1_ffn_Wqkv"]; m["Wo_d"] = I["L1_ffn_Wo"]
    m["W1"] = I["L1_ffn_W1"]; m["W2"] = I["L1_ffn_W2"]
    m["qs_rows"] = np.stack([np.stack([np.tile(I["L1_ffn_q_scale"][i], 16),
                                       np.tile(I["L1_ffn_k_scale"][i], 16)])
                             for i in range(6)]).astype(np.float32)
    m["gb_cols"] = np.stack([np.stack([_cols(I["L1_ffn_ln_g"][i]),
                                       _cols(I["L1_ffn_ln_b"][i])])
                             for i in range(6)]).astype(np.float32)
    b1c = np.zeros((6, 2, 128, 22), np.float32)
    for i in range(6):
        ap_ = np.zeros(22 * 128, np.float32); ap_[:2730] = I["L1_ffn_b1"][i][:2730]
        gp_ = np.zeros(22 * 128, np.float32); gp_[:2730] = I["L1_ffn_b1"][i][2730:]
        b1c[i, 0] = ap_.reshape(22, 128).T; b1c[i, 1] = gp_.reshape(22, 128).T
    m["b1_cols"] = b1c
    m["b2_cols"] = np.stack([_cols(I["L1_ffn_b2"][i]) for i in range(6)]).astype(np.float32)
    return m


def kernel(**inputs):
    nc = build_fused()
    in_maps = [_fused_inputs(inputs, b) for b in range(NCORES)]
    r = run_bass_kernel_spmd(nc, in_maps, core_ids=list(range(NCORES)))
    return np.stack([res["xo"].T for res in r.results]).astype(np.float32)


def kernel_unfused(**inputs):

    x = inputs["L0_attn_x"]
    xT_list = [np.ascontiguousarray(x[b].T) for b in range(NCORES)]
    xT_list = run_l0_attn_unfused(xT_list, inputs)
    xT_list = run_l0_ffn_unfused(xT_list, inputs)
    xT_list = run_l1_attn_unfused(xT_list, inputs)
    xT_list = run_l1_ffn_unfused(xT_list, inputs)
    out = np.stack([xT.T for xT in xT_list]).astype(np.float32)
    return out
